# version 03
# baseline (speedup 1.0000x reference)
import numpy as np
import contextlib
import ml_dtypes
import concourse.bass as bass
import concourse.mybir as mybir
from concourse.bass_utils import run_bass_kernel_spmd

F32 = mybir.dt.float32
BF16 = mybir.dt.bfloat16
AF = mybir.ActivationFunctionType
ALU = mybir.AluOpType
AX = mybir.AxisListType

P = 128
NCORES = 8
D = 2048
NB = 4
SEQ = 2048
SLOC = SEQ // NCORES
T = NB * SLOC
TH = T // 2
NCH = D // P
IFF = 8192
ICH = IFF // P
EPS = 1e-5
RSCALE = float(1.0 / np.sqrt(128.0))
BF = ml_dtypes.bfloat16

MODE = "fused"
DEBUG_FUSED = False



def _perm_rot():
    return np.concatenate([np.arange(0, 1024, 2), np.arange(1, 1024, 2),
                           np.arange(1024, 2048)])


def _perm_24():
    q = np.arange(IFF)
    return (q % 2048) * 4 + q // 2048


def _tiles(w, row_idx, col_idx):
    wp = np.asarray(w)[np.ix_(row_idx, col_idx)]
    nk = len(row_idx) // P
    no = len(col_idx) // P
    return np.ascontiguousarray(
        wp.reshape(nk, P, no, P).transpose(2, 1, 0, 3)).astype(np.float32)


def prep_weights(ins):
    pr_ = _perm_rot()
    p24 = _perm_24()
    rows = np.arange(D)
    c = {}
    W_in = np.asarray(ins["L0_attn_W_in"])
    b_in = np.asarray(ins["L0_attn_b_in"])
    colperm = np.concatenate([2048 + pr_, 4096 + pr_, 0 + pr_])
    c["w_qkv0"] = _tiles(W_in, rows, colperm)
    c["b_qkv0"] = np.ascontiguousarray(b_in[colperm].reshape(48, P).T).astype(np.float32)
    c["gb0"] = np.ascontiguousarray(np.concatenate(
        [np.asarray(ins["L0_attn_gamma"]).reshape(NCH, P).T,
         np.asarray(ins["L0_attn_beta"]).reshape(NCH, P).T], axis=1)).astype(np.float32)
    c["w_out0"] = _tiles(np.asarray(ins["L0_attn_W_out"]), pr_, rows)
    c["b_out0"] = np.ascontiguousarray(
        np.asarray(ins["L0_attn_b_out"]).reshape(NCH, P).T).astype(np.float32)
    wg = np.asarray(ins["L0_ffn_w_gate"])
    wu = np.asarray(ins["L0_ffn_w_up"])
    tg = _tiles(wg, rows, p24)
    tu = _tiles(wu, rows, p24)
    gu = np.empty((ICH, 2, P, NCH, P), np.float32)
    slot = 0
    for c0 in range(16):
        for j in range(4):
            ic = j * 16 + c0
            gu[slot, 0] = tg[ic]
            gu[slot, 1] = tu[ic]
            slot += 1
    c["w_gu0"] = np.ascontiguousarray(gu)
    c["w_down0"] = _tiles(np.asarray(ins["L0_ffn_w_down"]), p24, rows)
    wa = np.asarray(ins["L1_attn_w_attn"])
    ba = np.asarray(ins["L1_attn_b_attn"])
    ccols = np.concatenate([2048 + rows, 0 + rows])
    c["w_qk1"] = _tiles(wa, rows, ccols)
    c["b_qk1"] = np.ascontiguousarray(
        np.concatenate([ba[2048:4096].reshape(NCH, P).T,
                        ba[0:2048].reshape(NCH, P).T], axis=1)).astype(np.float32)
    c["w_v1"] = np.ascontiguousarray(
        wa[:, 4096:6144].reshape(NCH, P, D).transpose(1, 0, 2)).astype(np.float32)
    c["b_v1"] = np.ascontiguousarray(
        ba[4096:6144].reshape(NCH, P).T).astype(np.float32)
    c["w_proj1"] = _tiles(np.asarray(ins["L1_attn_w_proj"]), rows, rows)
    c["b_proj1"] = np.ascontiguousarray(
        np.asarray(ins["L1_attn_b_proj"]).reshape(NCH, P).T).astype(np.float32)
    wgu = np.asarray(ins["L1_ffn_w_gate_up"])
    gu1 = np.empty((ICH, 2, P, NCH, P), np.float32)
    for g in range(2):
        tt = _tiles(wgu[g * IFF:(g + 1) * IFF, :].T, rows, np.arange(IFF))
        for ic in range(ICH):
            gu1[ic, g] = tt[ic]
    c["w_gu1"] = np.ascontiguousarray(gu1)
    c["w_down1"] = _tiles(np.asarray(ins["L1_ffn_w_down"]).T, np.arange(IFF), rows)
    return c


def prep_core(cid, X):
    d = {}
    Xc = np.asarray(X)[:, SLOC * cid:SLOC * (cid + 1), :]
    d["x_in"] = np.ascontiguousarray(
        Xc.transpose(2, 0, 1).reshape(NCH, P, T)).astype(np.float32)
    pos = (SLOC * cid + np.arange(SLOC)).astype(np.float64)
    inv = 1.0 / (10000.0 ** (np.arange(0, 1024, 2, dtype=np.float64) / 1024.0))
    fr = pos[:, None] * inv[None, :]
    cos = np.cos(fr).T.reshape(4, P, SLOC).transpose(1, 0, 2)
    sin = np.sin(fr).T.reshape(4, P, SLOC).transpose(1, 0, 2)
    d["cos0"] = np.ascontiguousarray(np.tile(cos, (1, 1, 2))).astype(np.float32)
    d["sin0"] = np.ascontiguousarray(np.tile(sin, (1, 1, 2))).astype(np.float32)
    kg = np.arange(SEQ).reshape(NCH, P)
    qg = SLOC * cid + np.arange(SLOC)
    m = (kg[:, :, None] <= qg[None, None, :]).astype(np.float32)
    d["masks"] = np.ascontiguousarray(m.transpose(1, 0, 2))
    return d



class Prog:
    def __init__(self, nc, stack):
        self.nc = nc
        self.g, self.t, self.v, self.s, self.sy = (
            nc.gpsimd, nc.tensor, nc.vector, nc.scalar, nc.sync)
        self.h = {}
        self.n = {}
        for name in ("gw", "sq", "pe", "ac", "ve", "cc"):
            self.h[name] = stack.enter_context(nc.semaphore(name))
            self.n[name] = 0

    def inc(self, name, inst, k=1):
        inst.then_inc(self.h[name], k)
        self.n[name] += k
        return self.n[name]

    def wait(self, eng, name, val=None):
        eng.wait_ge(self.h[name], self.n[name] if val is None else val)

    def dma(self, q, out, in_, **kw):
        eng = {"gw": self.g, "sq": self.sy}[q]
        inst = eng.dma_start(out=out, in_=in_, **kw)
        return self.inc(q, inst, 16)

    def barrier(self):
        self.nc.all_engine_barrier()


class WStream:
    def __init__(self, pr, slot_aps):
        self.pr = pr
        self.aps = slot_aps
        self.freed = [0] * len(slot_aps)
        self.ready = [0] * len(slot_aps)
        self.i = 0

    def load(self, src):
        k = self.i % len(self.aps)
        if self.freed[k]:
            self.pr.wait(self.pr.g, "pe", self.freed[k])
        self.pr.dma("gw", self.aps[k], src)
        self.ready[k] = self.pr.n["gw"]
        self.i += 1
        return k

    def use(self, k):
        self.pr.wait(self.pr.t, "gw", self.ready[k])

    def done(self, k):
        self.freed[k] = self.pr.n["pe"]


class Banks:
    def __init__(self, pr):
        self.pr = pr
        self.mark = [None] * 8
        self.fill = [None] * 8

    def pe_acquire(self, idx):
        if self.mark[idx] is not None:
            self.pr.wait(self.pr.t, *self.mark[idx])

    def pe_filled(self, idx):
        self.fill[idx] = ("pe", self.pr.n["pe"])

    def consumer_wait(self, eng, idx):
        self.pr.wait(eng, *self.fill[idx])

    def consumed(self, idx, sem):
        self.mark[idx] = (sem, self.pr.n[sem])


def build(mode):
    nc = bass.Bass("TRN2", target_bir_lowering=False, debug=False)
    stack = contextlib.ExitStack()
    pr = Prog(nc, stack)
    g, t, v, s, sy = pr.g, pr.t, pr.v, pr.s, pr.sy
    fused = mode == "fused"
    do = lambda *phases: fused or mode in phases

    def dram_in(name, shape, dt=F32):
        return nc.dram_tensor(name, shape, dt, kind="ExternalInput").ap()

    def dram_out(name, shape, dt=F32):
        return nc.dram_tensor(name, shape, dt, kind="ExternalOutput").ap()

    def dram_int(name, shape, dt=F32):
        return nc.dram_tensor(name, shape, dt).ap()

    io = {}
    io["x_in"] = dram_in("x_in", [NCH, P, T])
    if do("a"):
        io["w_qkv0"] = dram_in("w_qkv0", [48, P, NCH, P])
        io["b_qkv0"] = dram_in("b_qkv0", [P, 48])
        io["gb0"] = dram_in("gb0", [P, 32])
        io["cos0"] = dram_in("cos0", [P, 4, TH])
        io["sin0"] = dram_in("sin0", [P, 4, TH])
    if do("b"):
        io["w_out0"] = dram_in("w_out0", [NCH, P, NCH, P])
        io["b_out0"] = dram_in("b_out0", [P, NCH])
        io["w_gu0"] = dram_in("w_gu0", [ICH, 2, P, NCH, P])
        io["w_down0"] = dram_in("w_down0", [NCH, P, ICH, P])
    if do("c"):
        io["w_qk1"] = dram_in("w_qk1", [32, P, NCH, P])
        io["b_qk1"] = dram_in("b_qk1", [P, 32])
        io["w_v1"] = dram_in("w_v1", [P, NCH, D])
        io["w_gu1"] = dram_in("w_gu1", [ICH, 2, P, NCH, P])
    if do("d"):
        io["masks"] = dram_in("masks", [P, NCH, SLOC])
        io["b_v1"] = dram_in("b_v1", [P, NCH])
        io["w_proj1"] = dram_in("w_proj1", [NCH, P, NCH, P])
        io["b_proj1"] = dram_in("b_proj1", [P, NCH])
        io["w_down1"] = dram_in("w_down1", [NCH, P, ICH, P])

    if fused:
        io["a_cc_in"] = dram_int("a_cc_in", [P, 64])
        io["a_cc_out"] = dram_int("a_cc_out", [P, 64])
        io["mk2_cc_in"] = dram_int("mk2_cc_in", [1, 64])
        io["mk2_cc_out"] = dram_int("mk2_cc_out", [1, 64])
        io["k_contrib"] = dram_int("k_contrib", [D, T], BF16)
        io["v_contrib"] = dram_int("v_contrib", [T, D], BF16)
        io["k_all"] = dram_int("k_all", [NCORES * D, T], BF16)
        io["v_all"] = dram_int("v_all", [NCORES * T, D], BF16)
        io["h_dram"] = dram_int("h_dram", [ICH, P, T], BF16)
        io["o_x"] = dram_out("o_x", [NCH, P, T])
        if DEBUG_FUSED:
            io["dbg"] = dram_out("dbg", [P, 512])
            io["o_x2"] = dram_out("o_x2", [NCH, P, T])
            io["o_y1"] = dram_out("o_y1", [NCH, P, T])
            io["o_xn1"] = dram_out("o_xn1", [NCH, P, T])
            io["o_kall"] = dram_out("o_kall", [NCORES * D, T], BF16)
            io["o_vall"] = dram_out("o_vall", [NCORES * T, D], BF16)
            io["o_h1"] = dram_out("o_h1", [ICH, P, T], BF16)
    else:
        if mode == "a":
            io["o_x"] = dram_out("o_x", [NCH, P, T])
            io["o_xn"] = dram_out("o_xn", [NCH, P, T], BF16)
            io["o_q"] = dram_out("o_q", [NCH, P, T], BF16)
            io["o_a"] = dram_out("o_a", [P, 64])
        if mode == "b":
            io["xn_in"] = dram_in("xn_in", [NCH, P, T], BF16)
            io["q_in"] = dram_in("q_in", [NCH, P, T], BF16)
            io["a_all"] = dram_in("a_all", [NCORES, P, 64])
            io["h_dram"] = dram_int("h_dram", [ICH, P, T], BF16)
            io["o_x"] = dram_out("o_x", [NCH, P, T])
        if mode == "c":
            io["o_q"] = dram_out("o_q", [NCH, P, T], BF16)
            io["k_contrib"] = dram_out("k_contrib", [D, T], BF16)
            io["v_contrib"] = dram_out("v_contrib", [T, D], BF16)
            io["o_mk2"] = dram_out("o_mk2", [1, 64])
            io["o_mq2"] = dram_out("o_mq2", [1, 64])
            io["h_dram"] = dram_out("h_dram", [ICH, P, T], BF16)
        if mode == "d":
            io["q_in"] = dram_in("q_in", [NCH, P, T], BF16)
            io["k_all"] = dram_in("k_all", [NCORES * D, T], BF16)
            io["v_all"] = dram_in("v_all", [NCORES * T, D], BF16)
            io["mk2_allT"] = dram_in("mk2_allT", [64, NCORES])
            io["mq2_in"] = dram_in("mq2_in", [1, 64])
            io["h_dram"] = dram_in("h_dram", [ICH, P, T], BF16)
            io["o_x"] = dram_out("o_x", [NCH, P, T])

    ksr = bass.get_kernel_semaphore_range()
    nmono = len(nc._monotonic_sems) if hasattr(nc, "_monotonic_sems") else 0
    base = ksr.start + 2 + (1 if nc._bir_kernel_barrier_sem is not None else 0) + nmono
    clear_range = range(base, ksr.stop)
    g.dma_reset(clear_range)
    g.sem_clear(clear_range)
    nc.all_engine_barrier()

    x_sb = stack.enter_context(nc.sbuf_tensor("x_sb", [P, NCH, T], F32))
    pool = stack.enter_context(nc.sbuf_tensor("pool", [P, 3 * NCH * T], BF16))
    xn3 = pool[:, 0:NCH * T].rearrange("p (c t) -> p c t", c=NCH)
    bufA = pool[:, NCH * T:2 * NCH * T].rearrange("p (c t) -> p c t", c=NCH)
    bufB = pool[:, 2 * NCH * T:3 * NCH * T].rearrange("p (c t) -> p c t", c=NCH)
    hh_view = pool[:, 0:ICH * TH].rearrange("p (c t) -> p c t", c=ICH)
    wbuf = stack.enter_context(nc.sbuf_tensor("wbuf", [P, 2, NCH, P], BF16))
    scr = stack.enter_context(nc.sbuf_tensor("scr", [P, 15, TH], F32))
    persist = stack.enter_context(nc.sbuf_tensor("persist", [P, 256], F32))
    sqa = stack.enter_context(nc.sbuf_tensor("sqa", [P, 2, TH], BF16))
    sqb = stack.enter_context(nc.sbuf_tensor("sqb", [P, 2, TH], BF16))
    stage = stack.enter_context(nc.sbuf_tensor("stage", [P, 2, TH], BF16))
    bias_sb = stack.enter_context(nc.sbuf_tensor("bias_sb", [P, 160], F32))
    ones1 = stack.enter_context(nc.sbuf_tensor("ones1", [1, P], BF16))
    onescol = stack.enter_context(nc.sbuf_tensor("onescol", [P, 1], BF16))
    pall_t = stack.enter_context(nc.psum_tensor("pall", [P, 8, 512], F32))
    pall = lambda k: pall_t[:, k, :]
    banks = Banks(pr)

    g.memset(ones1[:, :], 1.0)
    g.memset(onescol[:, :], 1.0)
    ws = WStream(pr, [wbuf[:, 0, :, :], wbuf[:, 1, :, :]])
    wsv = WStream(pr, [bufA[:, :, 0:512], bufA[:, :, 512:1024]])

    BOFF = {}
    boff = [0]

    def bias_load(name, width):
        BOFF[name] = boff[0]
        pr.dma("sq", bias_sb[:, boff[0]:boff[0] + width], io[name])
        boff[0] += width

    def bslot(name, i):
        return bias_sb[:, BOFF[name] + i:BOFF[name] + i + 1]

    for c in range(NCH):
        pr.dma("sq", x_sb[:, c, :], io["x_in"][c])
    if do("a"):
        bias_load("b_qkv0", 48)
        bias_load("gb0", 32)
    if do("b"):
        bias_load("b_out0", NCH)
    if do("c"):
        bias_load("b_qk1", 32)
    if do("d"):
        bias_load("b_v1", NCH)
        bias_load("b_proj1", NCH)
    for eng in (v, s, t, g):
        pr.wait(eng, "sq")
    pr.barrier()

    sqa_flat = sqa[:, :, :].rearrange("p a t -> p (a t)")

    def ln_stats(kind, mean_row, rstd_row):
        for bk in (4, 5, 6, 7):
            banks.pe_acquire(bk)
        mm_done = 0
        for c in range(NCH):
            if kind == "x":
                pr.wait(v, "pe", mm_done)
                inst = v.tensor_copy(sqa_flat, x_sb[:, c, :])
                pr.inc("ve", inst)
                xb = sqa_flat
            else:
                xb = xn3[:, c, :]
            pr.wait(s, "pe", mm_done)
            if kind == "x":
                pr.wait(s, "ve")
            s.activation(sqb[:, 0, :], xb[:, 0:TH], AF.Square)
            inst = s.activation(sqb[:, 1, :], xb[:, TH:T], AF.Square)
            pr.inc("ac", inst)
            pr.wait(t, "ac")
            if kind == "x":
                pr.wait(t, "ve")
            for hf in range(2):
                t.matmul(pall(4 + hf)[0:1, :], onescol[:, :], xb[:, hf * TH:(hf + 1) * TH],
                         start=(c == 0), stop=(c == NCH - 1))
                inst = t.matmul(pall(6 + hf)[0:1, :], onescol[:, :], sqb[:, hf, :],
                                start=(c == 0), stop=(c == NCH - 1))
            pr.inc("pe", inst)
            mm_done = pr.n["pe"]
        pr.wait(s, "pe")
        for hf in range(2):
            s.activation(mean_row[0:1, hf * TH:(hf + 1) * TH], pall(4 + hf)[0:1, :],
                         AF.Identity, scale=1.0 / D)
            inst = s.activation(rstd_row[0:1, hf * TH:(hf + 1) * TH], pall(6 + hf)[0:1, :],
                                AF.Identity, scale=1.0 / D)
        pr.inc("ac", inst)
        for bk in (4, 5, 6, 7):
            banks.consumed(bk, "ac")
        pr.wait(v, "ac")
        vv = scr[0:1, 14, 0:TH]
        for hf in range(2):
            mr = mean_row[0:1, hf * TH:(hf + 1) * TH]
            rr = rstd_row[0:1, hf * TH:(hf + 1) * TH]
            v.tensor_tensor(vv, mr, mr, ALU.mult)
            v.tensor_tensor(rr, rr, vv, ALU.subtract)
            inst = v.tensor_scalar(rr, rr, EPS, None, ALU.add)
        pr.inc("ve", inst)
        pr.wait(s, "ve")
        s.activation(rstd_row, rstd_row, AF.Ln)
        inst = s.activation(rstd_row, rstd_row, AF.Exp, scale=-0.5)
        pr.inc("ac", inst)
        pr.wait(v, "ac")

    def bcast_row_f32(row_f32, width, out_slot, bank=3):
        rb = sqa[0:1, 0, 0:width]
        inst = v.tensor_copy(rb, row_f32)
        pr.inc("ve", inst)
        pr.wait(t, "ve")
        banks.pe_acquire(bank)
        inst = t.matmul(pall(bank)[:, 0:width], ones1[:, :], rb, start=True, stop=True)
        pr.inc("pe", inst)
        banks.pe_filled(bank)
        banks.consumer_wait(v, bank)
        inst = v.tensor_copy(out_slot, pall(bank)[:, 0:width])
        pr.inc("ve", inst)
        banks.consumed(bank, "ve")

    def mm_tile(oc_src, rhs_getter, n_tk, evac_fn, bank_of):
        k = ws.load(oc_src)
        ws.use(k)
        for tk in range(n_tk):
            bk = bank_of(tk)
            banks.pe_acquire(bk)
            for kc in range(NCH):
                inst = t.matmul(pall(bk), wbuf[:, k, kc, :], rhs_getter(kc, tk),
                                start=(kc == 0), stop=(kc == NCH - 1))
            pr.inc("pe", inst)
            banks.pe_filled(bk)
            evac_fn(tk, bk)
        ws.done(k)

    mean_bc = scr[:, 0:2, :].rearrange("p a t -> p (a t)")
    rstd_bc = scr[:, 2:4, :].rearrange("p a t -> p (a t)")
    tf2 = scr[:, 4:6, :].rearrange("p a t -> p (a t)")
    rowA = scr[0:1, 12:14, :].rearrange("p a t -> p (a t)")
    rowB = scr[0:1, 10:12, :].rearrange("p a t -> p (a t)")

    def ln_apply_bcasts(mean_row, rstd_row):
        for hf in range(2):
            bcast_row_f32(mean_row[0:1, hf * TH:(hf + 1) * TH], TH, scr[:, 0 + hf, :])
            bcast_row_f32(rstd_row[0:1, hf * TH:(hf + 1) * TH], TH, scr[:, 2 + hf, :])

    if do("a"):
        ln_stats("x", rowA, rowB)
        ln_apply_bcasts(rowA, rowB)
        for c in range(NCH):
            v.tensor_tensor(tf2, x_sb[:, c, :], mean_bc, ALU.subtract)
            v.tensor_tensor(tf2, tf2, rstd_bc, ALU.mult)
            v.tensor_copy(xn3[:, c, :], tf2)
            inst = v.tensor_tensor(x_sb[:, c, :], x_sb[:, c, :], tf2, ALU.add)
        pr.inc("ve", inst)
        pr.barrier()

        ln_stats("xn", rowA, rowB)
        ln_apply_bcasts(rowA, rowB)
        for c in range(NCH):
            v.tensor_tensor(tf2, xn3[:, c, :], mean_bc, ALU.subtract)
            v.tensor_tensor(tf2, tf2, rstd_bc, ALU.mult)
            inst = v.tensor_scalar(bufA[:, c, :], tf2, bslot("gb0", c),
                                   bslot("gb0", 16 + c), ALU.mult, ALU.add)
        pr.inc("ve", inst)
        pr.barrier()

        for rc in range(4):
            pr.dma("sq", scr[:, 0 + rc, :], io["cos0"][:, rc, :])
            pr.dma("sq", scr[:, 4 + rc, :], io["sin0"][:, rc, :])
        pr.wait(v, "sq")

        for hf in range(2):
            cols = slice(hf * TH, (hf + 1) * TH)
            for oc in range(32):
                dst = bufB[:, oc % 16, 0:TH] if oc < 16 else bufB[:, oc % 16, TH:T]

                def evac(tk, bk, oc=oc, dst=dst):
                    banks.consumer_wait(s, bk)
                    inst = s.activation(dst, pall(bk), AF.Identity, bias=bslot("b_qkv0", oc))
                    pr.inc("ac", inst)
                    banks.consumed(bk, "ac")
                mm_tile(io["w_qkv0"][oc], lambda kc, tk: bufA[:, kc, cols], 1,
                        evac, lambda tk, oc=oc: oc % 4)
            pr.wait(v, "ac")
            sE, sO, sT = scr[:, 8, :], scr[:, 9, :], scr[:, 10, :]
            for part in (slice(0, TH), slice(TH, T)):
                for rc in range(4):
                    Ke = bufB[:, rc, part]
                    Ko = bufB[:, rc + 4, part]
                    cosr = scr[:, 0 + rc, :]
                    sinr = scr[:, 4 + rc, :]
                    v.tensor_tensor(sE, Ke, cosr, ALU.mult)
                    v.tensor_tensor(sT, Ko, sinr, ALU.mult)
                    v.tensor_tensor(sE, sE, sT, ALU.subtract)
                    v.tensor_tensor(sO, Ko, cosr, ALU.mult)
                    v.tensor_tensor(sT, Ke, sinr, ALU.mult)
                    v.tensor_tensor(sO, sO, sT, ALU.add)
                    v.tensor_copy(Ke, sE)
                    inst = v.tensor_copy(Ko, sO)
            pr.inc("ve", inst)
            for ofs, bank in ((0, 4), (TH, 5)):
                banks.pe_acquire(bank)
                mmv = [0, 0]
                for c in range(NCH):
                    pr.wait(s, "ve")
                    pr.wait(s, "pe", mmv[c % 2])
                    inst = s.activation(sqb[:, c % 2, :], bufB[:, c, ofs:ofs + TH], AF.Square)
                    pr.inc("ac", inst)
                    pr.wait(t, "ac")
                    inst = t.matmul(pall(bank)[0:1, :], onescol[:, :], sqb[:, c % 2, :],
                                    start=(c == 0), stop=(c == NCH - 1))
                    pr.inc("pe", inst)
                    mmv[c % 2] = pr.n["pe"]
                banks.pe_filled(bank)
            nk2 = scr[0:1, 12, :]
            nv2 = scr[0:1, 13, :]
            banks.consumer_wait(v, 4)
            banks.consumer_wait(v, 5)
            v.tensor_copy(nk2, pall(4)[0:1, :])
            inst = v.tensor_copy(nv2, pall(5)[0:1, :])
            pr.inc("ve", inst)
            banks.consumed(4, "ve")
            banks.consumed(5, "ve")
            inst = v.tensor_tensor(nk2, nk2, nv2, ALU.mult)
            pr.inc("ve", inst)
            pr.wait(s, "ve")
            s.activation(nk2, nk2, AF.Ln)
            inst = s.activation(nk2, nk2, AF.Exp, scale=-0.5)
            pr.inc("ac", inst)
            pr.wait(v, "ac")
            bcast_row_f32(nk2, TH, scr[:, 11, :])
            pP = scr[:, 8, :]
            for c in range(NCH):
                v.tensor_tensor(pP, bufB[:, c, 0:TH], bufB[:, c, TH:T], ALU.mult)
                v.tensor_tensor(pP, pP, scr[:, 11, :], ALU.mult)
                for b2 in range(2):
                    b = hf * 2 + b2
                    inst = v.reduce_sum(persist[:, c * 4 + b:c * 4 + b + 1],
                                        pP[:, b2 * SLOC:(b2 + 1) * SLOC], AX.X)
            pr.inc("ve", inst)
            pr.barrier()

        if mode == "a":
            pr.dma("sq", io["o_a"], persist[:, 0:64])

        for oc in range(32, 48):
            def evacq(tk, bk, oc=oc):
                banks.consumer_wait(s, bk)
                inst = s.activation(bufB[:, oc - 32, tk * TH:(tk + 1) * TH], pall(bk),
                                    AF.Identity, bias=bslot("b_qkv0", oc))
                pr.inc("ac", inst)
                banks.consumed(bk, "ac")
            mm_tile(io["w_qkv0"][oc],
                    lambda kc, tk: bufA[:, kc, tk * TH:(tk + 1) * TH], 2,
                    evacq, lambda tk, oc=oc: (oc * 2 + tk) % 4)
        if fused:
            pr.dma("gw", io["a_cc_in"], persist[:, 0:64])
            pr.wait(g, "gw")
            inst = g.collective_compute(
                "AllReduce", ALU.add, replica_groups=[list(range(NCORES))],
                ins=[io["a_cc_in"].tensor.ap().opt()],
                outs=[io["a_cc_out"].tensor.ap().opt()])
            pr.inc("cc", inst)
        pr.barrier()
        if mode == "a":
            for c in range(NCH):
                pr.dma("sq", io["o_x"][c], x_sb[:, c, :])
                pr.dma("sq", io["o_xn"][c], xn3[:, c, :])
                pr.dma("sq", io["o_q"][c], bufB[:, c, :])
            pr.wait(sy, "sq")

    if do("b"):
        if mode == "b":
            for c in range(NCH):
                pr.dma("sq", xn3[:, c, :], io["xn_in"][c])
                pr.dma("sq", bufB[:, c, :], io["q_in"][c])
            pr.dma("sq", scr[:, 0, :].rearrange("p (r c) -> p r c", r=NCORES),
                   io["a_all"].rearrange("r p c -> p r c"))
            pr.wait(v, "sq")
            av = scr[:, 0, :].rearrange("p (r c) -> p r c", r=NCORES)
            v.tensor_copy(persist[:, 0:64], av[:, 0, :])
            for r in range(1, NCORES):
                inst = v.tensor_tensor(persist[:, 0:64], persist[:, 0:64],
                                       av[:, r, :], ALU.add)
            pr.inc("ve", inst)
        if fused:
            pr.wait(sy, "cc")
            pr.dma("sq", persist[:, 0:64], io["a_cc_out"])
            pr.wait(v, "sq")
            if DEBUG_FUSED:
                pr.dma("sq", io["dbg"][:, 0:64], persist[:, 0:64])
        inst = v.tensor_scalar(persist[:, 0:64], persist[:, 0:64], 0.5, 0.5,
                               ALU.mult, ALU.add)
        pr.inc("ve", inst)
        for c in range(NCH):
            for b in range(NB):
                sl = bufB[:, c, b * SLOC:(b + 1) * SLOC]
                inst = v.tensor_scalar(sl, sl, persist[:, c * 4 + b:c * 4 + b + 1],
                                       None, ALU.mult)
        pr.inc("ve", inst)
        pr.barrier()

        for oc in range(NCH):
            def evaco(tk, bk, oc=oc):
                banks.consumer_wait(v, bk)
                xs = x_sb[:, oc, tk * TH:(tk + 1) * TH]
                inst = v.scalar_tensor_tensor(xs, pall(bk), bslot("b_out0", oc), xs,
                                              ALU.add, ALU.add)
                pr.inc("ve", inst)
                banks.consumed(bk, "ve")
            mm_tile(io["w_out0"][oc],
                    lambda kc, tk: bufB[:, kc, tk * TH:(tk + 1) * TH], 2,
                    evaco, lambda tk, oc=oc: (oc * 2 + tk) % 4)
        pr.barrier()

        stage_free = [0, 0]
        for c0 in range(NCH):
            for j in range(4):
                slot_i = c0 * 4 + j
                for gg in range(2):
                    k = ws.load(io["w_gu0"][slot_i, gg])
                    ws.use(k)
                    for tk in range(2):
                        bk = gg * 2 + tk
                        banks.pe_acquire(bk)
                        for kc in range(NCH):
                            inst = t.matmul(pall(bk), wbuf[:, k, kc, :],
                                            xn3[:, kc, tk * TH:(tk + 1) * TH],
                                            start=(kc == 0), stop=(kc == NCH - 1))
                        pr.inc("pe", inst)
                        banks.pe_filled(bk)
                    ws.done(k)
                for tk in range(2):
                    gbk, ubk = tk, 2 + tk
                    banks.consumer_wait(s, gbk)
                    pr.wait(s, "ve")
                    inst = s.activation(scr[:, 8 + tk, :], pall(gbk), AF.Silu)
                    pr.inc("ac", inst)
                    banks.consumed(gbk, "ac")
                    banks.consumer_wait(v, ubk)
                    pr.wait(v, "ac")
                    inst = v.tensor_tensor(scr[:, 2 * j + tk, :], scr[:, 8 + tk, :],
                                           pall(ubk), ALU.mult)
                    pr.inc("ve", inst)
                    banks.consumed(ubk, "ve")
            for tk in range(2):
                hs = [scr[:, 2 * j + tk, :] for j in range(4)]
                s0, s1, s2 = scr[:, 10, :], scr[:, 11, :], scr[:, 12, :]
                ab = [scr[:, 8, :], scr[:, 9, :], scr[:, 13, :], scr[:, 14, :]]
                pr.wait(s, "ve")
                for j in range(4):
                    inst = s.activation(ab[j], hs[j], AF.Abs)
                pr.inc("ac", inst)
                pr.wait(v, "ac")
                v.tensor_tensor(s0, ab[0], ab[1], ALU.max)
                v.tensor_tensor(s1, ab[2], ab[3], ALU.max)
                v.tensor_tensor(s2, s0, s1, ALU.min)
                v.tensor_tensor(s0, ab[0], ab[1], ALU.min)
                v.tensor_tensor(s1, ab[2], ab[3], ALU.min)
                v.tensor_tensor(s0, s0, s1, ALU.max)
                v.tensor_tensor(s2, s2, s0, ALU.max)
                for j in range(4):
                    st = (tk * 4 + j) % 2
                    v.tensor_tensor(ab[j], ab[j], s2, ALU.is_ge)
                    v.tensor_tensor(hs[j], hs[j], ab[j], ALU.mult)
                    pr.wait(v, "sq", stage_free[st])
                    inst = v.tensor_copy(stage[:, st, :], hs[j])
                    pr.inc("ve", inst)
                    pr.wait(sy, "ve")
                    pr.dma("sq", io["h_dram"][j * 16 + c0, :, tk * TH:(tk + 1) * TH],
                           stage[:, st, :])
                    stage_free[st] = pr.n["sq"]
        pr.barrier()

        for th in range(2):
            for grp in range(8):
                pr.dma("sq", hh_view[:, grp * 8:(grp + 1) * 8, :],
                       io["h_dram"].rearrange("k p t -> p k t")[
                           :, grp * 8:(grp + 1) * 8, th * TH:(th + 1) * TH])
            pr.wait(t, "sq")
            for oc in range(NCH):
                bk = oc % 4
                banks.pe_acquire(bk)
                first = True
                for qtr in range(4):
                    k = ws.load(io["w_down0"][oc, :, qtr * 16:(qtr + 1) * 16, :])
                    ws.use(k)
                    for kk in range(NCH):
                        inst = t.matmul(pall(bk), wbuf[:, k, kk, :],
                                        hh_view[:, qtr * 16 + kk, :],
                                        start=first, stop=(qtr == 3 and kk == NCH - 1))
                        first = False
                    pr.inc("pe", inst)
                    ws.done(k)
                banks.pe_filled(bk)
                banks.consumer_wait(v, bk)
                xs = x_sb[:, oc, th * TH:(th + 1) * TH]
                inst = v.tensor_tensor(xs, xs, pall(bk), ALU.add)
                pr.inc("ve", inst)
                banks.consumed(bk, "ve")
            pr.barrier()
        if fused and DEBUG_FUSED:
            for c in range(NCH):
                pr.dma("sq", io["o_x2"][c], x_sb[:, c, :])
            pr.wait(sy, "sq")
        if mode == "b":
            for c in range(NCH):
                pr.dma("sq", io["o_x"][c], x_sb[:, c, :])
            pr.wait(sy, "sq")

    if do("c"):
        ln_stats("x", rowA, rowB)
        ln_apply_bcasts(rowA, rowB)
        for c in range(NCH):
            v.tensor_tensor(tf2, x_sb[:, c, :], mean_bc, ALU.subtract)
            v.tensor_tensor(tf2, tf2, rstd_bc, ALU.mult)
            inst = v.tensor_copy(xn3[:, c, :], tf2)
        pr.inc("ve", inst)
        pr.barrier()
        if fused and DEBUG_FUSED:
            for c in range(NCH):
                g.dma_start(out=io["o_xn1"][c], in_=xn3[:, c, :]).then_inc(pr.h["gw"], 16)
                pr.n["gw"] += 16
            pr.wait(g, "gw")

        inst = v.tensor_scalar(bias_sb[:, BOFF["b_qk1"] + 16:BOFF["b_qk1"] + 32],
                               bias_sb[:, BOFF["b_qk1"] + 16:BOFF["b_qk1"] + 32],
                               RSCALE, None, ALU.mult)
        pr.inc("ve", inst)
        pr.wait(s, "ve")

        stage_free = [0, 0]
        sq_free = [0, 0]
        for oc in range(NCH):
            k = ws.load(io["w_qk1"][oc])
            ws.use(k)
            for tk in range(2):
                bk = tk
                banks.pe_acquire(bk)
                for kc in range(NCH):
                    inst = t.matmul(pall(bk), wbuf[:, k, kc, :],
                                    xn3[:, kc, tk * TH:(tk + 1) * TH],
                                    start=(kc == 0), stop=(kc == NCH - 1))
                pr.inc("pe", inst)
                banks.pe_filled(bk)
                banks.consumer_wait(s, bk)
                pr.wait(s, "sq", stage_free[tk])
                inst = s.activation(stage[:, tk, :], pall(bk), AF.Identity,
                                    bias=bslot("b_qk1", oc))
                pr.inc("ac", inst)
                banks.consumed(bk, "ac")
                pr.wait(sy, "ac")
                pr.dma("sq", io["k_contrib"][oc * P:(oc + 1) * P, tk * TH:(tk + 1) * TH],
                       stage[:, tk, :])
                stage_free[tk] = pr.n["sq"]
                pr.wait(s, "pe", sq_free[tk])
                inst = s.activation(sqb[:, tk, :], stage[:, tk, :], AF.Square)
                pr.inc("ac", inst)
                pr.wait(t, "ac")
                banks.pe_acquire(6 + tk)
                inst = t.matmul(pall(6 + tk)[0:1, :], onescol[:, :], sqb[:, tk, :],
                                start=True, stop=True)
                pr.inc("pe", inst)
                sq_free[tk] = pr.n["pe"]
                banks.pe_filled(6 + tk)
                banks.consumer_wait(v, 6 + tk)
                for b2 in range(2):
                    b = tk * 2 + b2
                    col = 64 + oc * 4 + b
                    inst = v.reduce_max(persist[0:1, col:col + 1],
                                        pall(6 + tk)[0:1, b2 * SLOC:(b2 + 1) * SLOC], AX.X)
                pr.inc("ve", inst)
                banks.consumed(6 + tk, "ve")
            ws.done(k)
        for oc in range(NCH, 32):
            k = ws.load(io["w_qk1"][oc])
            ws.use(k)
            for tk in range(2):
                bk = 2 + tk
                banks.pe_acquire(bk)
                for kc in range(NCH):
                    inst = t.matmul(pall(bk), wbuf[:, k, kc, :],
                                    xn3[:, kc, tk * TH:(tk + 1) * TH],
                                    start=(kc == 0), stop=(kc == NCH - 1))
                pr.inc("pe", inst)
                banks.pe_filled(bk)
                banks.consumer_wait(s, bk)
                inst = s.activation(bufB[:, oc - NCH, tk * TH:(tk + 1) * TH], pall(bk),
                                    AF.Identity, bias=bslot("b_qk1", oc), scale=RSCALE)
                pr.inc("ac", inst)
                banks.consumed(bk, "ac")
                pr.wait(s, "pe", sq_free[tk])
                inst = s.activation(sqb[:, tk, :], bufB[:, oc - NCH, tk * TH:(tk + 1) * TH],
                                    AF.Square)
                pr.inc("ac", inst)
                pr.wait(t, "ac")
                banks.pe_acquire(6 + tk)
                inst = t.matmul(pall(6 + tk)[0:1, :], onescol[:, :], sqb[:, tk, :],
                                start=True, stop=True)
                pr.inc("pe", inst)
                sq_free[tk] = pr.n["pe"]
                banks.pe_filled(6 + tk)
                banks.consumer_wait(v, 6 + tk)
                for b2 in range(2):
                    b = tk * 2 + b2
                    col = 128 + (oc - NCH) * 4 + b
                    inst = v.reduce_max(persist[0:1, col:col + 1],
                                        pall(6 + tk)[0:1, b2 * SLOC:(b2 + 1) * SLOC], AX.X)
                pr.inc("ve", inst)
                banks.consumed(6 + tk, "ve")
            ws.done(k)
        pr.barrier()

        vstage_free = [0, 0]
        for vs in range(4):
            k = wsv.load(io["w_v1"][:, :, vs * TH:(vs + 1) * TH])
            pr.wait(t, "gw", wsv.ready[k])
            for tm in range(8):
                bk = tm % 4
                banks.pe_acquire(bk)
                for kc in range(NCH):
                    inst = t.matmul(pall(bk), xn3[:, kc, tm * P:(tm + 1) * P],
                                    wsv.aps[k][:, kc, :], start=(kc == 0), stop=(kc == NCH - 1))
                pr.inc("pe", inst)
                banks.pe_filled(bk)
                banks.consumer_wait(v, bk)
                pr.wait(v, "sq", vstage_free[tm % 2])
                inst = v.tensor_copy(stage[:, tm % 2, :], pall(bk))
                pr.inc("ve", inst)
                banks.consumed(bk, "ve")
                pr.wait(sy, "ve")
                pr.dma("sq", io["v_contrib"][tm * P:(tm + 1) * P, vs * TH:(vs + 1) * TH],
                       stage[:, tm % 2, :])
                vstage_free[tm % 2] = pr.n["sq"]
            wsv.done(k)
        pr.barrier()
        if mode == "c":
            pr.dma("sq", io["o_mk2"], persist[0:1, 64:128])
            pr.dma("sq", io["o_mq2"], persist[0:1, 128:192])
            for c in range(NCH):
                pr.dma("sq", io["o_q"][c], bufB[:, c, :])
        h1_free = [0, 0]
        for ic in range(ICH):
            for gg in range(2):
                k = ws.load(io["w_gu1"][ic, gg])
                ws.use(k)
                for tk in range(2):
                    bk = gg * 2 + tk
                    banks.pe_acquire(bk)
                    for kc in range(NCH):
                        inst = t.matmul(pall(bk), wbuf[:, k, kc, :],
                                        xn3[:, kc, tk * TH:(tk + 1) * TH],
                                        start=(kc == 0), stop=(kc == NCH - 1))
                    pr.inc("pe", inst)
                    banks.pe_filled(bk)
                ws.done(k)
            for tk in range(2):
                gbk, ubk = tk, 2 + tk
                banks.consumer_wait(s, gbk)
                inst = s.activation(scr[:, 8 + tk, :], pall(gbk), AF.Silu)
                pr.inc("ac", inst)
                banks.consumed(gbk, "ac")
                banks.consumer_wait(v, ubk)
                pr.wait(v, "ac")
                pr.wait(v, "sq", h1_free[tk])
                inst = v.tensor_tensor(stage[:, tk, :], scr[:, 8 + tk, :],
                                       pall(ubk), ALU.mult)
                pr.inc("ve", inst)
                banks.consumed(ubk, "ve")
                pr.wait(sy, "ve")
                pr.dma("sq", io["h_dram"][ic, :, tk * TH:(tk + 1) * TH], stage[:, tk, :])
                h1_free[tk] = pr.n["sq"]
        pr.barrier()
        if fused:
            pr.wait(g, "sq")
            pr.dma("gw", io["mk2_cc_in"], persist[0:1, 64:128])
            pr.wait(g, "gw")
            for kind, op, iname, oname in (
                    ("AllGather", ALU.bypass, "k_contrib", "k_all"),
                    ("AllGather", ALU.bypass, "v_contrib", "v_all"),
                    ("AllReduce", ALU.max, "mk2_cc_in", "mk2_cc_out")):
                inst = g.collective_compute(
                    kind, op, replica_groups=[list(range(NCORES))],
                    ins=[io[iname].tensor.ap().opt()], outs=[io[oname].tensor.ap().opt()])
                pr.inc("cc", inst)
        if mode == "c":
            pr.wait(sy, "sq")

    if do("d"):
        flat = pool[:, 0:NCH * T]
        ktile = flat[:, 0:4096].rearrange("p (c t) -> p c t", c=NCH)
        vtile = flat[:, 4096:8192].rearrange("p (c t) -> p c t", c=NCH)
        attT = flat[:, 8192:12288].rearrange("p (c t) -> p c t", c=NCH)
        maskT = flat[:, 12288:16384].rearrange("p (c t) -> p c t", c=NCH)
        if mode == "d":
            for c in range(NCH):
                pr.dma("sq", bufB[:, c, :], io["q_in"][c])
            pr.dma("sq", scr[0:1, 0, :], io["mk2_allT"])
            pr.dma("sq", persist[0:1, 128:192], io["mq2_in"])
            pr.wait(v, "sq")
            inst = v.reduce_max(persist[0:1, 64:128],
                                scr[0:1, 0, :].rearrange("p (c r) -> p c r", r=NCORES),
                                AX.X)
            pr.inc("ve", inst)
        if fused:
            pr.wait(sy, "cc")
            if DEBUG_FUSED:
                pr.dma("sq", io["o_kall"], io["k_all"])
                pr.dma("sq", io["o_vall"], io["v_all"])
                pr.dma("sq", io["o_h1"], io["h_dram"])
            pr.dma("sq", persist[0:1, 64:128], io["mk2_cc_out"])
            pr.wait(v, "sq")
        pr.dma("gw", maskT.rearrange("p c t -> p (c t)"),
               io["masks"].rearrange("p c t -> p (c t)"))
        mg = persist[0:1, 192:256]
        v.tensor_tensor(mg, persist[0:1, 64:128], persist[0:1, 128:192], ALU.mult)
        inst = v.tensor_scalar(mg, mg, 1e-30, None, ALU.max)
        pr.inc("ve", inst)
        pr.wait(s, "ve")
        s.activation(mg, mg, AF.Ln)
        inst = s.activation(mg, mg, AF.Exp, scale=0.5)
        pr.inc("ac", inst)
        pr.wait(v, "ac")
        v.tensor_scalar(mg, mg, -1.0, None, ALU.mult)
        inst = v.tensor_copy(sqa[0:1, 0, 0:64], mg)
        pr.inc("ve", inst)
        pr.wait(t, "ve")
        banks.pe_acquire(6)
        inst = t.matmul(pall(6)[:, 0:64], ones1[:, :], sqa[0:1, 0, 0:64],
                        start=True, stop=True)
        pr.inc("pe", inst)
        banks.pe_filled(6)
        banks.consumer_wait(v, 6)
        negbc = scr[:, 11, 0:64]
        inst = v.tensor_copy(negbc, pall(6)[:, 0:64])
        pr.inc("ve", inst)
        banks.consumed(6, "ve")
        if DEBUG_FUSED and fused:
            pr.wait(sy, "ve")
            pr.dma("sq", io["dbg"][:, 64:128], persist[:, 64:128])
            pr.dma("sq", io["dbg"][:, 128:192], persist[:, 128:192])
            pr.dma("sq", io["dbg"][:, 192:256], negbc)
        pr.wait(v, "gw")
        pr.barrier()

        for b in range(NB):
            for hg in range(8):
                pr.wait(sy, "pe")
                for kc in range(NCH):
                    slot = kc // 2
                    coff = b * SLOC + (kc % 2) * P
                    pr.dma("sq", ktile[:, kc, :].rearrange("p (hh t) -> p hh t", hh=2),
                           io["k_all"][slot * D + hg * SLOC:slot * D + (hg + 1) * SLOC,
                                       coff:coff + P].rearrange("(hh p) t -> p hh t", hh=2))
                    pr.dma("sq", vtile[:, kc, :],
                           io["v_all"][slot * T + coff:slot * T + coff + P,
                                       hg * SLOC:(hg + 1) * SLOC])
                for hh in range(2):
                    h = hg * 2 + hh
                    q = bufB[:, h, b * SLOC:(b + 1) * SLOC]
                    idx = h * 4 + b
                    pr.wait(t, "sq")
                    for grp in range(4):
                        b0 = (grp % 2) * 2
                        banks.pe_acquire(b0)
                        banks.pe_acquire(b0 + 1)
                        for i in range(4):
                            kc = grp * 4 + i
                            inst = t.matmul(
                                pall_t[:, b0 + i // 2, (i % 2) * SLOC:(i % 2 + 1) * SLOC],
                                ktile[:, kc, hh * P:(hh + 1) * P], q,
                                start=True, stop=True)
                        pr.inc("pe", inst)
                        banks.pe_filled(b0)
                        banks.pe_filled(b0 + 1)
                        banks.consumer_wait(s, b0)
                        inst = s.activation(
                            attT[:, grp * 4:(grp + 1) * 4, :].rearrange("p c t -> p (c t)"),
                            pall_t[:, b0:b0 + 2, :].rearrange("p c t -> p (c t)"),
                            AF.Exp, bias=negbc[:, idx:idx + 1])
                        pr.inc("ac", inst)
                        banks.consumed(b0, "ac")
                        banks.consumed(b0 + 1, "ac")
                        pr.wait(v, "ac")
                        av_ = attT[:, grp * 4:(grp + 1) * 4, :].rearrange("p c t -> p (c t)")
                        mv_ = maskT[:, grp * 4:(grp + 1) * 4, :].rearrange("p c t -> p (c t)")
                        inst = v.tensor_tensor(av_, av_, mv_, ALU.mult)
                        pr.inc("ve", inst)
                    pr.wait(t, "ve")
                    banks.pe_acquire(5)
                    for kc in range(NCH):
                        inst = t.matmul(pall(5)[0:1, 0:SLOC], onescol[:, :], attT[:, kc, :],
                                        start=(kc == 0), stop=(kc == NCH - 1))
                    pr.inc("pe", inst)
                    banks.pe_filled(5)
                    banks.pe_acquire(4)
                    for kc in range(NCH):
                        inst = t.matmul(pall(4)[:, 0:SLOC],
                                        vtile[:, kc, hh * P:(hh + 1) * P], attT[:, kc, :],
                                        start=(kc == 0), stop=(kc == NCH - 1))
                    pr.inc("pe", inst)
                    banks.pe_filled(4)
                    banks.consumer_wait(v, 5)
                    sinv = scr[0:1, 14, 0:SLOC]
                    inst = v.reciprocal(sinv, pall(5)[0:1, 0:SLOC])
                    pr.inc("ve", inst)
                    banks.consumed(5, "ve")
                    inst = v.tensor_copy(sqa[0:1, 1, 0:SLOC], sinv)
                    pr.inc("ve", inst)
                    pr.wait(t, "ve")
                    banks.pe_acquire(6)
                    inst = t.matmul(pall(6)[:, 0:SLOC], ones1[:, :], sqa[0:1, 1, 0:SLOC],
                                    start=True, stop=True)
                    pr.inc("pe", inst)
                    banks.pe_filled(6)
                    banks.consumer_wait(v, 6)
                    sbc = scr[:, 13, 0:SLOC]
                    inst = v.tensor_copy(sbc, pall(6)[:, 0:SLOC])
                    pr.inc("ve", inst)
                    banks.consumed(6, "ve")
                    banks.consumer_wait(v, 4)
                    tfy = scr[:, 12, 0:SLOC]
                    v.tensor_tensor(tfy, pall(4)[:, 0:SLOC], sbc, ALU.mult)
                    inst = v.tensor_scalar(bufA[:, h, b * SLOC:(b + 1) * SLOC], tfy,
                                           bslot("b_v1", h), None, ALU.add)
                    pr.inc("ve", inst)
                    banks.consumed(4, "ve")
        pr.barrier()
        if DEBUG_FUSED and fused:
            g.dma_start(out=io["dbg"][:, 256:512], in_=bufA[:, 0, 0:256]).then_inc(pr.h["gw"], 16)
            pr.n["gw"] += 16
            for c in range(NCH):
                g.dma_start(out=io["o_y1"][c], in_=bufA[:, c, :]).then_inc(pr.h["gw"], 16)
                pr.n["gw"] += 16
            pr.wait(g, "gw")

        for oc in range(NCH):
            def evacp(tk, bk, oc=oc):
                banks.consumer_wait(v, bk)
                xs = x_sb[:, oc, tk * TH:(tk + 1) * TH]
                inst = v.scalar_tensor_tensor(xs, pall(bk), bslot("b_proj1", oc), xs,
                                              ALU.add, ALU.add)
                pr.inc("ve", inst)
                banks.consumed(bk, "ve")
            mm_tile(io["w_proj1"][oc],
                    lambda kc, tk: bufA[:, kc, tk * TH:(tk + 1) * TH], 2,
                    evacp, lambda tk, oc=oc: (oc * 2 + tk) % 4)
        pr.barrier()

        for th in range(2):
            for grp in range(8):
                pr.dma("sq", hh_view[:, grp * 8:(grp + 1) * 8, :],
                       io["h_dram"].rearrange("k p t -> p k t")[
                           :, grp * 8:(grp + 1) * 8, th * TH:(th + 1) * TH])
            pr.wait(t, "sq")
            for oc in range(NCH):
                bk = oc % 4
                banks.pe_acquire(bk)
                first = True
                for qtr in range(4):
                    k = ws.load(io["w_down1"][oc, :, qtr * 16:(qtr + 1) * 16, :])
                    ws.use(k)
                    for kk in range(NCH):
                        inst = t.matmul(pall(bk), wbuf[:, k, kk, :],
                                        hh_view[:, qtr * 16 + kk, :],
                                        start=first, stop=(qtr == 3 and kk == NCH - 1))
                        first = False
                    pr.inc("pe", inst)
                    ws.done(k)
                banks.pe_filled(bk)
                banks.consumer_wait(v, bk)
                xs = x_sb[:, oc, th * TH:(th + 1) * TH]
                inst = v.tensor_tensor(xs, xs, pall(bk), ALU.add)
                pr.inc("ve", inst)
                banks.consumed(bk, "ve")
            pr.barrier()

        for c in range(NCH):
            pr.dma("sq", io["o_x"][c], x_sb[:, c, :])
        pr.wait(sy, "sq")

    nc.all_engine_barrier()
    stack.close()
    return nc



def kernel(**inputs):
    X = np.asarray(inputs["L0_attn_X"])
    W = prep_weights(inputs)
    cores = [prep_core(c, X) for c in range(NCORES)]
    core_ids = list(range(NCORES))

    if MODE == "fused":
        nc = build("fused")
        in_maps = []
        for c in range(NCORES):
            m = {"x_in": cores[c]["x_in"], "cos0": cores[c]["cos0"],
                 "sin0": cores[c]["sin0"], "masks": cores[c]["masks"]}
            for k in ("w_qkv0", "b_qkv0", "gb0", "w_out0", "b_out0", "w_gu0", "w_down0",
                      "w_qk1", "b_qk1", "w_v1", "b_v1", "w_proj1", "b_proj1",
                      "w_gu1", "w_down1"):
                m[k] = W[k]
            in_maps.append(m)
        res = run_bass_kernel_spmd(nc, in_maps, core_ids=core_ids)
        outs = [r["o_x"] for r in res.results]
    else:
        nc = build("a")
        in_maps = [{"x_in": cores[c]["x_in"], "w_qkv0": W["w_qkv0"],
                    "b_qkv0": W["b_qkv0"], "gb0": W["gb0"],
                    "cos0": cores[c]["cos0"], "sin0": cores[c]["sin0"]}
                   for c in range(NCORES)]
        ra = run_bass_kernel_spmd(nc, in_maps, core_ids=core_ids).results
        a_all = np.ascontiguousarray(np.stack([ra[c]["o_a"] for c in range(NCORES)]))

        nc = build("b")
        in_maps = [{"x_in": ra[c]["o_x"], "xn_in": np.asarray(ra[c]["o_xn"], BF),
                    "q_in": np.asarray(ra[c]["o_q"], BF), "a_all": a_all,
                    "w_out0": W["w_out0"], "b_out0": W["b_out0"],
                    "w_gu0": W["w_gu0"], "w_down0": W["w_down0"]}
                   for c in range(NCORES)]
        rb = run_bass_kernel_spmd(nc, in_maps, core_ids=core_ids).results

        nc = build("c")
        in_maps = [{"x_in": rb[c]["o_x"], "w_qk1": W["w_qk1"], "b_qk1": W["b_qk1"],
                    "w_v1": W["w_v1"], "w_gu1": W["w_gu1"]}
                   for c in range(NCORES)]
        rc = run_bass_kernel_spmd(nc, in_maps, core_ids=core_ids).results
        k_all = np.ascontiguousarray(
            np.concatenate([np.asarray(rc[c]["k_contrib"], BF) for c in range(NCORES)], 0))
        v_all = np.ascontiguousarray(
            np.concatenate([np.asarray(rc[c]["v_contrib"], BF) for c in range(NCORES)], 0))
        mk2_allT = np.ascontiguousarray(
            np.stack([rc[c]["o_mk2"][0] for c in range(NCORES)]).T)

        nc = build("d")
        in_maps = [{"x_in": rb[c]["o_x"], "q_in": np.asarray(rc[c]["o_q"], BF),
                    "k_all": k_all, "v_all": v_all, "mk2_allT": mk2_allT,
                    "mq2_in": rc[c]["o_mq2"], "h_dram": np.asarray(rc[c]["h_dram"], BF),
                    "masks": cores[c]["masks"], "b_v1": W["b_v1"],
                    "w_proj1": W["w_proj1"], "b_proj1": W["b_proj1"],
                    "w_down1": W["w_down1"]}
                   for c in range(NCORES)]
        rd = run_bass_kernel_spmd(nc, in_maps, core_ids=core_ids).results
        outs = [rd[c]["o_x"] for c in range(NCORES)]

    full = np.empty((NB, SEQ, D), np.float32)
    for c in range(NCORES):
        o = np.asarray(outs[c], np.float32).reshape(NCH, P, NB, SLOC)
        full[:, SLOC * c:SLOC * (c + 1), :] = o.transpose(2, 3, 0, 1).reshape(NB, SLOC, D)
    return full


# version 04
# speedup vs baseline: 1.1072x; 1.1072x over previous
import numpy as np
import contextlib
import ml_dtypes
import concourse.bass as bass
import concourse.mybir as mybir
from concourse.bass_utils import run_bass_kernel_spmd

F32 = mybir.dt.float32
BF16 = mybir.dt.bfloat16
AF = mybir.ActivationFunctionType
ALU = mybir.AluOpType
AX = mybir.AxisListType

P = 128
NCORES = 8
D = 2048
NB = 4
SEQ = 2048
SLOC = SEQ // NCORES
T = NB * SLOC
TH = T // 2
NCH = D // P
IFF = 8192
ICH = IFF // P
EPS = 1e-5
RSCALE = float(1.0 / np.sqrt(128.0))
BF = ml_dtypes.bfloat16

MODE = "fused"
DEBUG_FUSED = False



def _perm_rot():
    return np.concatenate([np.arange(0, 1024, 2), np.arange(1, 1024, 2),
                           np.arange(1024, 2048)])


def _perm_24():
    q = np.arange(IFF)
    return (q % 2048) * 4 + q // 2048


def _tiles(w, row_idx, col_idx):
    wp = np.asarray(w)[np.ix_(row_idx, col_idx)]
    nk = len(row_idx) // P
    no = len(col_idx) // P
    return np.ascontiguousarray(
        wp.reshape(nk, P, no, P).transpose(2, 1, 0, 3)).astype(np.float32)


def prep_weights(ins):
    pr_ = _perm_rot()
    p24 = _perm_24()
    rows = np.arange(D)
    c = {}
    W_in = np.asarray(ins["L0_attn_W_in"])
    b_in = np.asarray(ins["L0_attn_b_in"])
    colperm = np.concatenate([2048 + pr_, 4096 + pr_, 0 + pr_])
    c["w_qkv0"] = _tiles(W_in, rows, colperm)
    c["b_qkv0"] = np.ascontiguousarray(b_in[colperm].reshape(48, P).T).astype(np.float32)
    c["gb0"] = np.ascontiguousarray(np.concatenate(
        [np.asarray(ins["L0_attn_gamma"]).reshape(NCH, P).T,
         np.asarray(ins["L0_attn_beta"]).reshape(NCH, P).T], axis=1)).astype(np.float32)
    c["w_out0"] = _tiles(np.asarray(ins["L0_attn_W_out"]), pr_, rows)
    c["b_out0"] = np.ascontiguousarray(
        np.asarray(ins["L0_attn_b_out"]).reshape(NCH, P).T).astype(np.float32)
    wg = np.asarray(ins["L0_ffn_w_gate"])
    wu = np.asarray(ins["L0_ffn_w_up"])
    tg = _tiles(wg, rows, p24)
    tu = _tiles(wu, rows, p24)
    gu = np.empty((ICH, 2, P, NCH, P), np.float32)
    slot = 0
    for c0 in range(16):
        for j in range(4):
            ic = j * 16 + c0
            gu[slot, 0] = tg[ic]
            gu[slot, 1] = tu[ic]
            slot += 1
    c["w_gu0"] = np.ascontiguousarray(gu)
    c["w_down0"] = _tiles(np.asarray(ins["L0_ffn_w_down"]), p24, rows)
    wa = np.asarray(ins["L1_attn_w_attn"])
    ba = np.asarray(ins["L1_attn_b_attn"])
    ccols = np.concatenate([2048 + rows, 0 + rows])
    c["w_qk1"] = _tiles(wa, rows, ccols)
    c["b_qk1"] = np.ascontiguousarray(
        np.concatenate([ba[2048:4096].reshape(NCH, P).T,
                        ba[0:2048].reshape(NCH, P).T], axis=1)).astype(np.float32)
    c["w_v1"] = np.ascontiguousarray(
        wa[:, 4096:6144].reshape(NCH, P, D).transpose(1, 0, 2)).astype(np.float32)
    c["b_v1"] = np.ascontiguousarray(
        ba[4096:6144].reshape(NCH, P).T).astype(np.float32)
    c["w_proj1"] = _tiles(np.asarray(ins["L1_attn_w_proj"]), rows, rows)
    c["b_proj1"] = np.ascontiguousarray(
        np.asarray(ins["L1_attn_b_proj"]).reshape(NCH, P).T).astype(np.float32)
    wgu = np.asarray(ins["L1_ffn_w_gate_up"])
    gu1 = np.empty((ICH, 2, P, NCH, P), np.float32)
    for g in range(2):
        tt = _tiles(wgu[g * IFF:(g + 1) * IFF, :].T, rows, np.arange(IFF))
        for ic in range(ICH):
            gu1[ic, g] = tt[ic]
    c["w_gu1"] = np.ascontiguousarray(gu1)
    c["w_down1"] = _tiles(np.asarray(ins["L1_ffn_w_down"]).T, np.arange(IFF), rows)
    return c


def prep_core(cid, X):
    d = {}
    Xc = np.asarray(X)[:, SLOC * cid:SLOC * (cid + 1), :]
    d["x_in"] = np.ascontiguousarray(
        Xc.transpose(2, 0, 1).reshape(NCH, P, T)).astype(np.float32)
    pos = (SLOC * cid + np.arange(SLOC)).astype(np.float64)
    inv = 1.0 / (10000.0 ** (np.arange(0, 1024, 2, dtype=np.float64) / 1024.0))
    fr = pos[:, None] * inv[None, :]
    cos = np.cos(fr).T.reshape(4, P, SLOC).transpose(1, 0, 2)
    sin = np.sin(fr).T.reshape(4, P, SLOC).transpose(1, 0, 2)
    d["cos0"] = np.ascontiguousarray(np.tile(cos, (1, 1, 2))).astype(np.float32)
    d["sin0"] = np.ascontiguousarray(np.tile(sin, (1, 1, 2))).astype(np.float32)
    kg = np.arange(SEQ).reshape(NCH, P)
    qg = SLOC * cid + np.arange(SLOC)
    m = (kg[:, :, None] <= qg[None, None, :]).astype(np.float32)
    d["masks"] = np.ascontiguousarray(m.transpose(1, 0, 2))
    return d



class Prog:
    def __init__(self, nc, stack):
        self.nc = nc
        self.g, self.t, self.v, self.s, self.sy = (
            nc.gpsimd, nc.tensor, nc.vector, nc.scalar, nc.sync)
        self.h = {}
        self.n = {}
        for name in ("gw", "sq", "sc", "pe", "ac", "ve", "cc"):
            self.h[name] = stack.enter_context(nc.semaphore(name))
            self.n[name] = 0

    def inc(self, name, inst, k=1):
        inst.then_inc(self.h[name], k)
        self.n[name] += k
        return self.n[name]

    def wait(self, eng, name, val=None):
        eng.wait_ge(self.h[name], self.n[name] if val is None else val)

    def dma(self, q, out, in_, **kw):
        eng = {"gw": self.g, "sq": self.sy, "sc": self.s}[q]
        inst = eng.dma_start(out=out, in_=in_, **kw)
        return self.inc(q, inst, 16)

    def barrier(self):
        self.nc.all_engine_barrier()


class WStream:
    def __init__(self, pr, slot_aps):
        self.pr = pr
        self.aps = slot_aps
        self.freed = [0] * len(slot_aps)
        self.ready = [0] * len(slot_aps)
        self.i = 0

    def load(self, src):
        k = self.i % len(self.aps)
        if self.freed[k]:
            self.pr.wait(self.pr.g, "pe", self.freed[k])
        self.pr.dma("gw", self.aps[k], src)
        self.ready[k] = self.pr.n["gw"]
        self.i += 1
        return k

    def use(self, k):
        self.pr.wait(self.pr.t, "gw", self.ready[k])

    def done(self, k):
        self.freed[k] = self.pr.n["pe"]


class Banks:
    def __init__(self, pr):
        self.pr = pr
        self.mark = [None] * 8
        self.fill = [None] * 8

    def pe_acquire(self, idx):
        if self.mark[idx] is not None:
            self.pr.wait(self.pr.t, *self.mark[idx])

    def pe_filled(self, idx):
        self.fill[idx] = ("pe", self.pr.n["pe"])

    def consumer_wait(self, eng, idx):
        self.pr.wait(eng, *self.fill[idx])

    def consumed(self, idx, sem):
        self.mark[idx] = (sem, self.pr.n[sem])


def build(mode):
    nc = bass.Bass("TRN2", target_bir_lowering=False, debug=False)
    stack = contextlib.ExitStack()
    pr = Prog(nc, stack)
    g, t, v, s, sy = pr.g, pr.t, pr.v, pr.s, pr.sy
    fused = mode == "fused"
    do = lambda *phases: fused or mode in phases

    def dram_in(name, shape, dt=F32):
        return nc.dram_tensor(name, shape, dt, kind="ExternalInput").ap()

    def dram_out(name, shape, dt=F32):
        return nc.dram_tensor(name, shape, dt, kind="ExternalOutput").ap()

    def dram_int(name, shape, dt=F32):
        return nc.dram_tensor(name, shape, dt).ap()

    io = {}
    io["x_in"] = dram_in("x_in", [NCH, P, T])
    if do("a"):
        io["w_qkv0"] = dram_in("w_qkv0", [48, P, NCH, P])
        io["b_qkv0"] = dram_in("b_qkv0", [P, 48])
        io["gb0"] = dram_in("gb0", [P, 32])
        io["cos0"] = dram_in("cos0", [P, 4, TH])
        io["sin0"] = dram_in("sin0", [P, 4, TH])
    if do("b"):
        io["w_out0"] = dram_in("w_out0", [NCH, P, NCH, P])
        io["b_out0"] = dram_in("b_out0", [P, NCH])
        io["w_gu0"] = dram_in("w_gu0", [ICH, 2, P, NCH, P])
        io["w_down0"] = dram_in("w_down0", [NCH, P, ICH, P])
    if do("c"):
        io["w_qk1"] = dram_in("w_qk1", [32, P, NCH, P])
        io["b_qk1"] = dram_in("b_qk1", [P, 32])
        io["w_v1"] = dram_in("w_v1", [P, NCH, D])
        io["w_gu1"] = dram_in("w_gu1", [ICH, 2, P, NCH, P])
    if do("d"):
        io["masks"] = dram_in("masks", [P, NCH, SLOC])
        io["b_v1"] = dram_in("b_v1", [P, NCH])
        io["w_proj1"] = dram_in("w_proj1", [NCH, P, NCH, P])
        io["b_proj1"] = dram_in("b_proj1", [P, NCH])
        io["w_down1"] = dram_in("w_down1", [NCH, P, ICH, P])

    if fused:
        io["a_cc_in"] = dram_int("a_cc_in", [P, 64])
        io["a_cc_out"] = dram_int("a_cc_out", [P, 64])
        io["mk2_cc_in"] = dram_int("mk2_cc_in", [1, 64])
        io["mk2_cc_out"] = dram_int("mk2_cc_out", [1, 64])
        io["k_contrib"] = dram_int("k_contrib", [D, T], BF16)
        io["v_contrib"] = dram_int("v_contrib", [T, D], BF16)
        io["k_all"] = dram_int("k_all", [NCORES * D, T], BF16)
        io["v_all"] = dram_int("v_all", [NCORES * T, D], BF16)
        io["h_dram"] = dram_int("h_dram", [ICH, P, T], BF16)
        io["o_x"] = dram_out("o_x", [NCH, P, T])
        if DEBUG_FUSED:
            io["dbg"] = dram_out("dbg", [P, 512])
            io["o_x2"] = dram_out("o_x2", [NCH, P, T])
            io["o_y1"] = dram_out("o_y1", [NCH, P, T])
            io["o_xn1"] = dram_out("o_xn1", [NCH, P, T])
            io["o_kall"] = dram_out("o_kall", [NCORES * D, T], BF16)
            io["o_vall"] = dram_out("o_vall", [NCORES * T, D], BF16)
            io["o_h1"] = dram_out("o_h1", [ICH, P, T], BF16)
    else:
        if mode == "a":
            io["o_x"] = dram_out("o_x", [NCH, P, T])
            io["o_xn"] = dram_out("o_xn", [NCH, P, T], BF16)
            io["o_q"] = dram_out("o_q", [NCH, P, T], BF16)
            io["o_a"] = dram_out("o_a", [P, 64])
        if mode == "b":
            io["xn_in"] = dram_in("xn_in", [NCH, P, T], BF16)
            io["q_in"] = dram_in("q_in", [NCH, P, T], BF16)
            io["a_all"] = dram_in("a_all", [NCORES, P, 64])
            io["h_dram"] = dram_int("h_dram", [ICH, P, T], BF16)
            io["o_x"] = dram_out("o_x", [NCH, P, T])
        if mode == "c":
            io["o_q"] = dram_out("o_q", [NCH, P, T], BF16)
            io["k_contrib"] = dram_out("k_contrib", [D, T], BF16)
            io["v_contrib"] = dram_out("v_contrib", [T, D], BF16)
            io["o_mk2"] = dram_out("o_mk2", [1, 64])
            io["o_mq2"] = dram_out("o_mq2", [1, 64])
            io["h_dram"] = dram_out("h_dram", [ICH, P, T], BF16)
        if mode == "d":
            io["q_in"] = dram_in("q_in", [NCH, P, T], BF16)
            io["k_all"] = dram_in("k_all", [NCORES * D, T], BF16)
            io["v_all"] = dram_in("v_all", [NCORES * T, D], BF16)
            io["mk2_allT"] = dram_in("mk2_allT", [64, NCORES])
            io["mq2_in"] = dram_in("mq2_in", [1, 64])
            io["h_dram"] = dram_in("h_dram", [ICH, P, T], BF16)
            io["o_x"] = dram_out("o_x", [NCH, P, T])

    ksr = bass.get_kernel_semaphore_range()
    nmono = len(nc._monotonic_sems) if hasattr(nc, "_monotonic_sems") else 0
    base = ksr.start + 2 + (1 if nc._bir_kernel_barrier_sem is not None else 0) + nmono
    clear_range = range(base, ksr.stop)
    g.dma_reset(clear_range)
    g.sem_clear(clear_range)
    nc.all_engine_barrier()

    x_sb = stack.enter_context(nc.sbuf_tensor("x_sb", [P, NCH, T], F32))
    pool = stack.enter_context(nc.sbuf_tensor("pool", [P, 3 * NCH * T], BF16))
    xn3 = pool[:, 0:NCH * T].rearrange("p (c t) -> p c t", c=NCH)
    bufA = pool[:, NCH * T:2 * NCH * T].rearrange("p (c t) -> p c t", c=NCH)
    bufB = pool[:, 2 * NCH * T:3 * NCH * T].rearrange("p (c t) -> p c t", c=NCH)
    hh_view = pool[:, 0:ICH * TH].rearrange("p (c t) -> p c t", c=ICH)
    wbuf = stack.enter_context(nc.sbuf_tensor("wbuf", [P, 2, NCH, P], BF16))
    scr = stack.enter_context(nc.sbuf_tensor("scr", [P, 15, TH], F32))
    persist = stack.enter_context(nc.sbuf_tensor("persist", [P, 256], F32))
    sqa = stack.enter_context(nc.sbuf_tensor("sqa", [P, 2, TH], BF16))
    sqb = stack.enter_context(nc.sbuf_tensor("sqb", [P, 2, TH], BF16))
    stage = stack.enter_context(nc.sbuf_tensor("stage", [P, 2, TH], BF16))
    bias_sb = stack.enter_context(nc.sbuf_tensor("bias_sb", [P, 160], F32))
    ones1 = stack.enter_context(nc.sbuf_tensor("ones1", [1, P], BF16))
    onescol = stack.enter_context(nc.sbuf_tensor("onescol", [P, 1], BF16))
    pall_t = stack.enter_context(nc.psum_tensor("pall", [P, 8, 512], F32))
    pall = lambda k: pall_t[:, k, :]
    banks = Banks(pr)

    g.memset(ones1[:, :], 1.0)
    g.memset(onescol[:, :], 1.0)
    ws = WStream(pr, [wbuf[:, 0, :, :], wbuf[:, 1, :, :]])
    wsv = WStream(pr, [bufA[:, :, 0:512], bufA[:, :, 512:1024]])

    BOFF = {}
    boff = [0]

    def bias_load(name, width):
        BOFF[name] = boff[0]
        pr.dma("sq", bias_sb[:, boff[0]:boff[0] + width], io[name])
        boff[0] += width

    def bslot(name, i):
        return bias_sb[:, BOFF[name] + i:BOFF[name] + i + 1]

    for c in range(NCH):
        pr.dma("sq", x_sb[:, c, :], io["x_in"][c])
    if do("a"):
        bias_load("b_qkv0", 48)
        bias_load("gb0", 32)
    if do("b"):
        bias_load("b_out0", NCH)
    if do("c"):
        bias_load("b_qk1", 32)
    if do("d"):
        bias_load("b_v1", NCH)
        bias_load("b_proj1", NCH)
    for eng in (v, s, t):
        pr.wait(eng, "sq")

    sqa_flat = sqa[:, :, :].rearrange("p a t -> p (a t)")

    def ln_stats(kind, mean_row, rstd_row):
        for bk in (4, 5, 6, 7):
            banks.pe_acquire(bk)
        mm_done = 0
        for c in range(NCH):
            if kind == "x":
                pr.wait(v, "pe", mm_done)
                inst = v.tensor_copy(sqa_flat, x_sb[:, c, :])
                pr.inc("ve", inst)
                xb = sqa_flat
            else:
                xb = xn3[:, c, :]
            pr.wait(s, "pe", mm_done)
            if kind == "x":
                pr.wait(s, "ve")
            s.activation(sqb[:, 0, :], xb[:, 0:TH], AF.Square)
            inst = s.activation(sqb[:, 1, :], xb[:, TH:T], AF.Square)
            pr.inc("ac", inst)
            pr.wait(t, "ac")
            if kind == "x":
                pr.wait(t, "ve")
            for hf in range(2):
                t.matmul(pall(4 + hf)[0:1, :], onescol[:, :], xb[:, hf * TH:(hf + 1) * TH],
                         start=(c == 0), stop=(c == NCH - 1))
                inst = t.matmul(pall(6 + hf)[0:1, :], onescol[:, :], sqb[:, hf, :],
                                start=(c == 0), stop=(c == NCH - 1))
            pr.inc("pe", inst)
            mm_done = pr.n["pe"]
        pr.wait(s, "pe")
        for hf in range(2):
            s.activation(mean_row[0:1, hf * TH:(hf + 1) * TH], pall(4 + hf)[0:1, :],
                         AF.Identity, scale=1.0 / D)
            inst = s.activation(rstd_row[0:1, hf * TH:(hf + 1) * TH], pall(6 + hf)[0:1, :],
                                AF.Identity, scale=1.0 / D)
        pr.inc("ac", inst)
        for bk in (4, 5, 6, 7):
            banks.consumed(bk, "ac")
        pr.wait(v, "ac")
        vv = scr[0:1, 14, 0:TH]
        for hf in range(2):
            mr = mean_row[0:1, hf * TH:(hf + 1) * TH]
            rr = rstd_row[0:1, hf * TH:(hf + 1) * TH]
            v.tensor_tensor(vv, mr, mr, ALU.mult)
            v.tensor_tensor(rr, rr, vv, ALU.subtract)
            inst = v.tensor_scalar(rr, rr, EPS, None, ALU.add)
        pr.inc("ve", inst)
        pr.wait(s, "ve")
        s.activation(rstd_row, rstd_row, AF.Ln)
        inst = s.activation(rstd_row, rstd_row, AF.Exp, scale=-0.5)
        pr.inc("ac", inst)
        pr.wait(v, "ac")

    def bcast_row_f32(row_f32, width, out_slot, bank=3):
        rb = sqa[0:1, 0, 0:width]
        inst = v.tensor_copy(rb, row_f32)
        pr.inc("ve", inst)
        pr.wait(t, "ve")
        banks.pe_acquire(bank)
        inst = t.matmul(pall(bank)[:, 0:width], ones1[:, :], rb, start=True, stop=True)
        pr.inc("pe", inst)
        banks.pe_filled(bank)
        banks.consumer_wait(v, bank)
        inst = v.tensor_copy(out_slot, pall(bank)[:, 0:width])
        pr.inc("ve", inst)
        banks.consumed(bank, "ve")

    def mm_tile(oc_src, rhs_getter, n_tk, evac_fn, bank_of):
        k = ws.load(oc_src)
        ws.use(k)
        for tk in range(n_tk):
            bk = bank_of(tk)
            banks.pe_acquire(bk)
            for kc in range(NCH):
                inst = t.matmul(pall(bk), wbuf[:, k, kc, :], rhs_getter(kc, tk),
                                start=(kc == 0), stop=(kc == NCH - 1))
            pr.inc("pe", inst)
            banks.pe_filled(bk)
            evac_fn(tk, bk)
        ws.done(k)

    mean_bc = scr[:, 0:2, :].rearrange("p a t -> p (a t)")
    rstd_bc = scr[:, 2:4, :].rearrange("p a t -> p (a t)")
    tf2 = scr[:, 4:6, :].rearrange("p a t -> p (a t)")
    rowA = scr[0:1, 12:14, :].rearrange("p a t -> p (a t)")
    rowB = scr[0:1, 10:12, :].rearrange("p a t -> p (a t)")

    def ln_apply_bcasts(mean_row, rstd_row):
        for hf in range(2):
            bcast_row_f32(mean_row[0:1, hf * TH:(hf + 1) * TH], TH, scr[:, 0 + hf, :])
            bcast_row_f32(rstd_row[0:1, hf * TH:(hf + 1) * TH], TH, scr[:, 2 + hf, :])

    if do("a"):
        ln_stats("x", rowA, rowB)
        ln_apply_bcasts(rowA, rowB)
        for c in range(NCH):
            v.tensor_tensor(tf2, x_sb[:, c, :], mean_bc, ALU.subtract)
            v.tensor_tensor(tf2, tf2, rstd_bc, ALU.mult)
            v.tensor_copy(xn3[:, c, :], tf2)
            inst = v.tensor_tensor(x_sb[:, c, :], x_sb[:, c, :], tf2, ALU.add)
        pr.inc("ve", inst)
        pr.barrier()

        ln_stats("xn", rowA, rowB)
        ln_apply_bcasts(rowA, rowB)
        for c in range(NCH):
            v.tensor_tensor(tf2, xn3[:, c, :], mean_bc, ALU.subtract)
            v.tensor_tensor(tf2, tf2, rstd_bc, ALU.mult)
            inst = v.tensor_scalar(bufA[:, c, :], tf2, bslot("gb0", c),
                                   bslot("gb0", 16 + c), ALU.mult, ALU.add)
        pr.inc("ve", inst)
        pr.barrier()

        for rc in range(4):
            pr.dma("sq", scr[:, 0 + rc, :], io["cos0"][:, rc, :])
            pr.dma("sq", scr[:, 4 + rc, :], io["sin0"][:, rc, :])
        pr.wait(v, "sq")

        for hf in range(2):
            cols = slice(hf * TH, (hf + 1) * TH)
            for oc in range(32):
                dst = bufB[:, oc % 16, 0:TH] if oc < 16 else bufB[:, oc % 16, TH:T]

                def evac(tk, bk, oc=oc, dst=dst):
                    banks.consumer_wait(s, bk)
                    inst = s.activation(dst, pall(bk), AF.Identity, bias=bslot("b_qkv0", oc))
                    pr.inc("ac", inst)
                    banks.consumed(bk, "ac")
                mm_tile(io["w_qkv0"][oc], lambda kc, tk: bufA[:, kc, cols], 1,
                        evac, lambda tk, oc=oc: oc % 4)
            pr.wait(v, "ac")
            sE, sO, sT = scr[:, 8, :], scr[:, 9, :], scr[:, 10, :]
            for part in (slice(0, TH), slice(TH, T)):
                for rc in range(4):
                    Ke = bufB[:, rc, part]
                    Ko = bufB[:, rc + 4, part]
                    cosr = scr[:, 0 + rc, :]
                    sinr = scr[:, 4 + rc, :]
                    v.tensor_tensor(sE, Ke, cosr, ALU.mult)
                    v.tensor_tensor(sT, Ko, sinr, ALU.mult)
                    v.tensor_tensor(sE, sE, sT, ALU.subtract)
                    v.tensor_tensor(sO, Ko, cosr, ALU.mult)
                    v.tensor_tensor(sT, Ke, sinr, ALU.mult)
                    v.tensor_tensor(sO, sO, sT, ALU.add)
                    v.tensor_copy(Ke, sE)
                    inst = v.tensor_copy(Ko, sO)
            pr.inc("ve", inst)
            for ofs, bank in ((0, 4), (TH, 5)):
                banks.pe_acquire(bank)
                mmv = [0, 0]
                for c in range(NCH):
                    pr.wait(s, "ve")
                    pr.wait(s, "pe", mmv[c % 2])
                    inst = s.activation(sqb[:, c % 2, :], bufB[:, c, ofs:ofs + TH], AF.Square)
                    pr.inc("ac", inst)
                    pr.wait(t, "ac")
                    inst = t.matmul(pall(bank)[0:1, :], onescol[:, :], sqb[:, c % 2, :],
                                    start=(c == 0), stop=(c == NCH - 1))
                    pr.inc("pe", inst)
                    mmv[c % 2] = pr.n["pe"]
                banks.pe_filled(bank)
            nk2 = scr[0:1, 12, :]
            nv2 = scr[0:1, 13, :]
            banks.consumer_wait(v, 4)
            banks.consumer_wait(v, 5)
            v.tensor_copy(nk2, pall(4)[0:1, :])
            inst = v.tensor_copy(nv2, pall(5)[0:1, :])
            pr.inc("ve", inst)
            banks.consumed(4, "ve")
            banks.consumed(5, "ve")
            inst = v.tensor_tensor(nk2, nk2, nv2, ALU.mult)
            pr.inc("ve", inst)
            pr.wait(s, "ve")
            s.activation(nk2, nk2, AF.Ln)
            inst = s.activation(nk2, nk2, AF.Exp, scale=-0.5)
            pr.inc("ac", inst)
            pr.wait(v, "ac")
            bcast_row_f32(nk2, TH, scr[:, 11, :])
            pP = scr[:, 8, :]
            for c in range(NCH):
                v.tensor_tensor(pP, bufB[:, c, 0:TH], bufB[:, c, TH:T], ALU.mult)
                v.tensor_tensor(pP, pP, scr[:, 11, :], ALU.mult)
                for b2 in range(2):
                    b = hf * 2 + b2
                    inst = v.reduce_sum(persist[:, c * 4 + b:c * 4 + b + 1],
                                        pP[:, b2 * SLOC:(b2 + 1) * SLOC], AX.X)
            pr.inc("ve", inst)
            pr.barrier()

        if mode == "a":
            pr.dma("sq", io["o_a"], persist[:, 0:64])

        for oc in range(32, 48):
            def evacq(tk, bk, oc=oc):
                banks.consumer_wait(s, bk)
                inst = s.activation(bufB[:, oc - 32, tk * TH:(tk + 1) * TH], pall(bk),
                                    AF.Identity, bias=bslot("b_qkv0", oc))
                pr.inc("ac", inst)
                banks.consumed(bk, "ac")
            mm_tile(io["w_qkv0"][oc],
                    lambda kc, tk: bufA[:, kc, tk * TH:(tk + 1) * TH], 2,
                    evacq, lambda tk, oc=oc: (oc * 2 + tk) % 4)
        if fused:
            pr.dma("gw", io["a_cc_in"], persist[:, 0:64])
            pr.wait(g, "gw")
            inst = g.collective_compute(
                "AllReduce", ALU.add, replica_groups=[list(range(NCORES))],
                ins=[io["a_cc_in"].tensor.ap().opt()],
                outs=[io["a_cc_out"].tensor.ap().opt()])
            pr.inc("cc", inst)
        pr.barrier()
        if mode == "a":
            for c in range(NCH):
                pr.dma("sq", io["o_x"][c], x_sb[:, c, :])
                pr.dma("sq", io["o_xn"][c], xn3[:, c, :])
                pr.dma("sq", io["o_q"][c], bufB[:, c, :])
            pr.wait(sy, "sq")

    if do("b"):
        if mode == "b":
            for c in range(NCH):
                pr.dma("sq", xn3[:, c, :], io["xn_in"][c])
                pr.dma("sq", bufB[:, c, :], io["q_in"][c])
            pr.dma("sq", scr[:, 0, :].rearrange("p (r c) -> p r c", r=NCORES),
                   io["a_all"].rearrange("r p c -> p r c"))
            pr.wait(v, "sq")
            av = scr[:, 0, :].rearrange("p (r c) -> p r c", r=NCORES)
            v.tensor_copy(persist[:, 0:64], av[:, 0, :])
            for r in range(1, NCORES):
                inst = v.tensor_tensor(persist[:, 0:64], persist[:, 0:64],
                                       av[:, r, :], ALU.add)
            pr.inc("ve", inst)
        if fused:
            pr.wait(sy, "cc")
            pr.dma("sq", persist[:, 0:64], io["a_cc_out"])
            pr.wait(v, "sq")
            if DEBUG_FUSED:
                pr.dma("sq", io["dbg"][:, 0:64], persist[:, 0:64])
        inst = v.tensor_scalar(persist[:, 0:64], persist[:, 0:64], 0.5, 0.5,
                               ALU.mult, ALU.add)
        pr.inc("ve", inst)
        for c in range(NCH):
            for b in range(NB):
                sl = bufB[:, c, b * SLOC:(b + 1) * SLOC]
                inst = v.tensor_scalar(sl, sl, persist[:, c * 4 + b:c * 4 + b + 1],
                                       None, ALU.mult)
        pr.inc("ve", inst)
        pr.barrier()

        for oc in range(NCH):
            def evaco(tk, bk, oc=oc):
                banks.consumer_wait(v, bk)
                xs = x_sb[:, oc, tk * TH:(tk + 1) * TH]
                inst = v.scalar_tensor_tensor(xs, pall(bk), bslot("b_out0", oc), xs,
                                              ALU.add, ALU.add)
                pr.inc("ve", inst)
                banks.consumed(bk, "ve")
            mm_tile(io["w_out0"][oc],
                    lambda kc, tk: bufB[:, kc, tk * TH:(tk + 1) * TH], 2,
                    evaco, lambda tk, oc=oc: (oc * 2 + tk) % 4)
        pr.barrier()

        stage_free = [0, 0]
        for c0 in range(NCH):
            for j in range(4):
                slot_i = c0 * 4 + j
                for gg in range(2):
                    k = ws.load(io["w_gu0"][slot_i, gg])
                    ws.use(k)
                    for tk in range(2):
                        bk = gg * 2 + tk
                        banks.pe_acquire(bk)
                        for kc in range(NCH):
                            inst = t.matmul(pall(bk), wbuf[:, k, kc, :],
                                            xn3[:, kc, tk * TH:(tk + 1) * TH],
                                            start=(kc == 0), stop=(kc == NCH - 1))
                        pr.inc("pe", inst)
                        banks.pe_filled(bk)
                    ws.done(k)
                for tk in range(2):
                    gbk, ubk = tk, 2 + tk
                    banks.consumer_wait(s, gbk)
                    pr.wait(s, "ve")
                    inst = s.activation(scr[:, 8 + tk, :], pall(gbk), AF.Silu)
                    pr.inc("ac", inst)
                    banks.consumed(gbk, "ac")
                    banks.consumer_wait(v, ubk)
                    pr.wait(v, "ac")
                    inst = v.tensor_tensor(scr[:, 2 * j + tk, :], scr[:, 8 + tk, :],
                                           pall(ubk), ALU.mult)
                    pr.inc("ve", inst)
                    banks.consumed(ubk, "ve")
            for tk in range(2):
                hs = [scr[:, 2 * j + tk, :] for j in range(4)]
                s0, s1, s2 = scr[:, 10, :], scr[:, 11, :], scr[:, 12, :]
                ab = [scr[:, 8, :], scr[:, 9, :], scr[:, 13, :], scr[:, 14, :]]
                pr.wait(s, "ve")
                for j in range(4):
                    inst = s.activation(ab[j], hs[j], AF.Abs)
                pr.inc("ac", inst)
                pr.wait(v, "ac")
                v.tensor_tensor(s0, ab[0], ab[1], ALU.max)
                v.tensor_tensor(s1, ab[2], ab[3], ALU.max)
                v.tensor_tensor(s2, s0, s1, ALU.min)
                v.tensor_tensor(s0, ab[0], ab[1], ALU.min)
                v.tensor_tensor(s1, ab[2], ab[3], ALU.min)
                v.tensor_tensor(s0, s0, s1, ALU.max)
                v.tensor_tensor(s2, s2, s0, ALU.max)
                for j in range(4):
                    st = (tk * 4 + j) % 2
                    v.tensor_tensor(ab[j], ab[j], s2, ALU.is_ge)
                    v.tensor_tensor(hs[j], hs[j], ab[j], ALU.mult)
                    pr.wait(v, "sq", stage_free[st])
                    inst = v.tensor_copy(stage[:, st, :], hs[j])
                    pr.inc("ve", inst)
                    pr.wait(sy, "ve")
                    pr.dma("sq", io["h_dram"][j * 16 + c0, :, tk * TH:(tk + 1) * TH],
                           stage[:, st, :])
                    stage_free[st] = pr.n["sq"]
        pr.barrier()

        for th in range(2):
            for grp in range(8):
                pr.dma("sq", hh_view[:, grp * 8:(grp + 1) * 8, :],
                       io["h_dram"].rearrange("k p t -> p k t")[
                           :, grp * 8:(grp + 1) * 8, th * TH:(th + 1) * TH])
            pr.wait(t, "sq")
            for oc in range(NCH):
                bk = oc % 4
                banks.pe_acquire(bk)
                first = True
                for qtr in range(4):
                    k = ws.load(io["w_down0"][oc, :, qtr * 16:(qtr + 1) * 16, :])
                    ws.use(k)
                    for kk in range(NCH):
                        inst = t.matmul(pall(bk), wbuf[:, k, kk, :],
                                        hh_view[:, qtr * 16 + kk, :],
                                        start=first, stop=(qtr == 3 and kk == NCH - 1))
                        first = False
                    pr.inc("pe", inst)
                    ws.done(k)
                banks.pe_filled(bk)
                banks.consumer_wait(v, bk)
                xs = x_sb[:, oc, th * TH:(th + 1) * TH]
                inst = v.tensor_tensor(xs, xs, pall(bk), ALU.add)
                pr.inc("ve", inst)
                banks.consumed(bk, "ve")
            pr.barrier()
        if fused and DEBUG_FUSED:
            for c in range(NCH):
                pr.dma("sq", io["o_x2"][c], x_sb[:, c, :])
            pr.wait(sy, "sq")
        if mode == "b":
            for c in range(NCH):
                pr.dma("sq", io["o_x"][c], x_sb[:, c, :])
            pr.wait(sy, "sq")

    if do("c"):
        ln_stats("x", rowA, rowB)
        ln_apply_bcasts(rowA, rowB)
        for c in range(NCH):
            v.tensor_tensor(tf2, x_sb[:, c, :], mean_bc, ALU.subtract)
            v.tensor_tensor(tf2, tf2, rstd_bc, ALU.mult)
            inst = v.tensor_copy(xn3[:, c, :], tf2)
        pr.inc("ve", inst)
        pr.barrier()
        if fused and DEBUG_FUSED:
            for c in range(NCH):
                g.dma_start(out=io["o_xn1"][c], in_=xn3[:, c, :]).then_inc(pr.h["gw"], 16)
                pr.n["gw"] += 16
            pr.wait(g, "gw")

        inst = v.tensor_scalar(bias_sb[:, BOFF["b_qk1"] + 16:BOFF["b_qk1"] + 32],
                               bias_sb[:, BOFF["b_qk1"] + 16:BOFF["b_qk1"] + 32],
                               RSCALE, None, ALU.mult)
        pr.inc("ve", inst)
        pr.wait(s, "ve")

        stage_free = [0, 0]
        sq_free = [0, 0]
        for oc in range(NCH):
            k = ws.load(io["w_qk1"][oc])
            ws.use(k)
            for tk in range(2):
                bk = tk
                banks.pe_acquire(bk)
                for kc in range(NCH):
                    inst = t.matmul(pall(bk), wbuf[:, k, kc, :],
                                    xn3[:, kc, tk * TH:(tk + 1) * TH],
                                    start=(kc == 0), stop=(kc == NCH - 1))
                pr.inc("pe", inst)
                banks.pe_filled(bk)
                banks.consumer_wait(s, bk)
                pr.wait(s, "sq", stage_free[tk])
                inst = s.activation(stage[:, tk, :], pall(bk), AF.Identity,
                                    bias=bslot("b_qk1", oc))
                pr.inc("ac", inst)
                banks.consumed(bk, "ac")
                pr.wait(sy, "ac")
                pr.dma("sq", io["k_contrib"][oc * P:(oc + 1) * P, tk * TH:(tk + 1) * TH],
                       stage[:, tk, :])
                stage_free[tk] = pr.n["sq"]
                pr.wait(s, "pe", sq_free[tk])
                inst = s.activation(sqb[:, tk, :], stage[:, tk, :], AF.Square)
                pr.inc("ac", inst)
                pr.wait(t, "ac")
                banks.pe_acquire(6 + tk)
                inst = t.matmul(pall(6 + tk)[0:1, :], onescol[:, :], sqb[:, tk, :],
                                start=True, stop=True)
                pr.inc("pe", inst)
                sq_free[tk] = pr.n["pe"]
                banks.pe_filled(6 + tk)
                banks.consumer_wait(v, 6 + tk)
                for b2 in range(2):
                    b = tk * 2 + b2
                    col = 64 + oc * 4 + b
                    inst = v.reduce_max(persist[0:1, col:col + 1],
                                        pall(6 + tk)[0:1, b2 * SLOC:(b2 + 1) * SLOC], AX.X)
                pr.inc("ve", inst)
                banks.consumed(6 + tk, "ve")
            ws.done(k)
        for oc in range(NCH, 32):
            k = ws.load(io["w_qk1"][oc])
            ws.use(k)
            for tk in range(2):
                bk = 2 + tk
                banks.pe_acquire(bk)
                for kc in range(NCH):
                    inst = t.matmul(pall(bk), wbuf[:, k, kc, :],
                                    xn3[:, kc, tk * TH:(tk + 1) * TH],
                                    start=(kc == 0), stop=(kc == NCH - 1))
                pr.inc("pe", inst)
                banks.pe_filled(bk)
                banks.consumer_wait(s, bk)
                inst = s.activation(bufB[:, oc - NCH, tk * TH:(tk + 1) * TH], pall(bk),
                                    AF.Identity, bias=bslot("b_qk1", oc), scale=RSCALE)
                pr.inc("ac", inst)
                banks.consumed(bk, "ac")
                pr.wait(s, "pe", sq_free[tk])
                inst = s.activation(sqb[:, tk, :], bufB[:, oc - NCH, tk * TH:(tk + 1) * TH],
                                    AF.Square)
                pr.inc("ac", inst)
                pr.wait(t, "ac")
                banks.pe_acquire(6 + tk)
                inst = t.matmul(pall(6 + tk)[0:1, :], onescol[:, :], sqb[:, tk, :],
                                start=True, stop=True)
                pr.inc("pe", inst)
                sq_free[tk] = pr.n["pe"]
                banks.pe_filled(6 + tk)
                banks.consumer_wait(v, 6 + tk)
                for b2 in range(2):
                    b = tk * 2 + b2
                    col = 128 + (oc - NCH) * 4 + b
                    inst = v.reduce_max(persist[0:1, col:col + 1],
                                        pall(6 + tk)[0:1, b2 * SLOC:(b2 + 1) * SLOC], AX.X)
                pr.inc("ve", inst)
                banks.consumed(6 + tk, "ve")
            ws.done(k)
        pr.barrier()

        vstage_free = [0, 0]
        for vs in range(4):
            k = wsv.load(io["w_v1"][:, :, vs * TH:(vs + 1) * TH])
            pr.wait(t, "gw", wsv.ready[k])
            for tm in range(8):
                bk = tm % 4
                banks.pe_acquire(bk)
                for kc in range(NCH):
                    inst = t.matmul(pall(bk), xn3[:, kc, tm * P:(tm + 1) * P],
                                    wsv.aps[k][:, kc, :], start=(kc == 0), stop=(kc == NCH - 1))
                pr.inc("pe", inst)
                banks.pe_filled(bk)
                banks.consumer_wait(v, bk)
                pr.wait(v, "sq", vstage_free[tm % 2])
                inst = v.tensor_copy(stage[:, tm % 2, :], pall(bk))
                pr.inc("ve", inst)
                banks.consumed(bk, "ve")
                pr.wait(sy, "ve")
                pr.dma("sq", io["v_contrib"][tm * P:(tm + 1) * P, vs * TH:(vs + 1) * TH],
                       stage[:, tm % 2, :])
                vstage_free[tm % 2] = pr.n["sq"]
            wsv.done(k)
        pr.barrier()
        if mode == "c":
            pr.dma("sq", io["o_mk2"], persist[0:1, 64:128])
            pr.dma("sq", io["o_mq2"], persist[0:1, 128:192])
            for c in range(NCH):
                pr.dma("sq", io["o_q"][c], bufB[:, c, :])
        h1_free = [0, 0]
        for ic in range(ICH):
            for gg in range(2):
                k = ws.load(io["w_gu1"][ic, gg])
                ws.use(k)
                for tk in range(2):
                    bk = gg * 2 + tk
                    banks.pe_acquire(bk)
                    for kc in range(NCH):
                        inst = t.matmul(pall(bk), wbuf[:, k, kc, :],
                                        xn3[:, kc, tk * TH:(tk + 1) * TH],
                                        start=(kc == 0), stop=(kc == NCH - 1))
                    pr.inc("pe", inst)
                    banks.pe_filled(bk)
                ws.done(k)
            for tk in range(2):
                gbk, ubk = tk, 2 + tk
                banks.consumer_wait(s, gbk)
                inst = s.activation(scr[:, 8 + tk, :], pall(gbk), AF.Silu)
                pr.inc("ac", inst)
                banks.consumed(gbk, "ac")
                banks.consumer_wait(v, ubk)
                pr.wait(v, "ac")
                pr.wait(v, "sq", h1_free[tk])
                inst = v.tensor_tensor(stage[:, tk, :], scr[:, 8 + tk, :],
                                       pall(ubk), ALU.mult)
                pr.inc("ve", inst)
                banks.consumed(ubk, "ve")
                pr.wait(sy, "ve")
                pr.dma("sq", io["h_dram"][ic, :, tk * TH:(tk + 1) * TH], stage[:, tk, :])
                h1_free[tk] = pr.n["sq"]
        pr.barrier()
        if fused:
            pr.wait(g, "sq")
            pr.dma("gw", io["mk2_cc_in"], persist[0:1, 64:128])
            pr.wait(g, "gw")
            for kind, op, iname, oname in (
                    ("AllGather", ALU.bypass, "k_contrib", "k_all"),
                    ("AllGather", ALU.bypass, "v_contrib", "v_all"),
                    ("AllReduce", ALU.max, "mk2_cc_in", "mk2_cc_out")):
                inst = g.collective_compute(
                    kind, op, replica_groups=[list(range(NCORES))],
                    ins=[io[iname].tensor.ap().opt()], outs=[io[oname].tensor.ap().opt()])
                pr.inc("cc", inst)
        if mode == "c":
            pr.wait(sy, "sq")

    if do("d"):
        flat = pool[:, 0:NCH * T]
        maskT = scr[:, 0:4, :].rearrange("p a t -> p (a t)").bitcast(BF16).rearrange(
            "p (c t) -> p c t", c=NCH)
        if mode == "d":
            for c in range(NCH):
                pr.dma("sq", bufB[:, c, :], io["q_in"][c])
            pr.dma("sq", scr[0:1, 4, :], io["mk2_allT"])
            pr.dma("sq", persist[0:1, 128:192], io["mq2_in"])
            pr.wait(v, "sq")
            inst = v.reduce_max(persist[0:1, 64:128],
                                scr[0:1, 4, :].rearrange("p (c r) -> p c r", r=NCORES),
                                AX.X)
            pr.inc("ve", inst)
        if fused:
            pr.wait(sy, "cc")
            if DEBUG_FUSED:
                pr.dma("sq", io["o_kall"], io["k_all"])
                pr.dma("sq", io["o_vall"], io["v_all"])
                pr.dma("sq", io["o_h1"], io["h_dram"])
            pr.dma("sq", persist[0:1, 64:128], io["mk2_cc_out"])
            pr.wait(v, "sq")
        pr.dma("gw", maskT.rearrange("p c t -> p (c t)"),
               io["masks"].rearrange("p c t -> p (c t)"))
        mg = persist[0:1, 192:256]
        v.tensor_tensor(mg, persist[0:1, 64:128], persist[0:1, 128:192], ALU.mult)
        inst = v.tensor_scalar(mg, mg, 1e-30, None, ALU.max)
        pr.inc("ve", inst)
        pr.wait(s, "ve")
        s.activation(mg, mg, AF.Ln)
        inst = s.activation(mg, mg, AF.Exp, scale=0.5)
        pr.inc("ac", inst)
        pr.wait(v, "ac")
        v.tensor_scalar(mg, mg, -1.0, None, ALU.mult)
        inst = v.tensor_copy(sqa[0:1, 0, 0:64], mg)
        pr.inc("ve", inst)
        pr.wait(t, "ve")
        banks.pe_acquire(6)
        inst = t.matmul(pall(6)[:, 0:64], ones1[:, :], sqa[0:1, 0, 0:64],
                        start=True, stop=True)
        pr.inc("pe", inst)
        banks.pe_filled(6)
        banks.consumer_wait(v, 6)
        negbc = persist[:, 0:64]
        inst = v.tensor_copy(negbc, pall(6)[:, 0:64])
        pr.inc("ve", inst)
        banks.consumed(6, "ve")
        if DEBUG_FUSED and fused:
            pr.wait(sy, "ve")
            pr.dma("sq", io["dbg"][:, 64:128], persist[:, 64:128])
            pr.dma("sq", io["dbg"][:, 128:192], persist[:, 128:192])
            pr.dma("sq", io["dbg"][:, 192:256], negbc)
        pr.wait(v, "gw")
        pr.barrier()

        kt = [flat[:, 0:4096].rearrange("p (c t) -> p c t", c=NCH),
              flat[:, 4096:8192].rearrange("p (c t) -> p c t", c=NCH)]
        vt = [flat[:, 8192:12288].rearrange("p (c t) -> p c t", c=NCH),
              flat[:, 12288:16384].rearrange("p (c t) -> p c t", c=NCH)]
        attT2 = [scr[:, 7:11, :].rearrange("p a t -> p (a t)").bitcast(BF16),
                 scr[:, 11:15, :].rearrange("p a t -> p (a t)").bitcast(BF16)]
        attTv = [a.rearrange("p (c t) -> p c t", c=NCH) for a in attT2]
        units = [(b, hg) for b in range(NB) for hg in range(8)]
        loaded = [(0, 0), (0, 0)]
        attdone = [0, 0]

        def prefetch(i):
            bb, hg = units[i]
            B = i % 2
            pr.wait(sy, "pe")
            for slot2 in range(8):
                for hh in range(2):
                    pr.dma("sq", kt[B][:, 2 * slot2:2 * slot2 + 2, hh * P:(hh + 1) * P],
                           io["k_all"][slot2 * D + hg * SLOC + hh * P:
                                       slot2 * D + hg * SLOC + (hh + 1) * P,
                                       bb * SLOC:(bb + 1) * SLOC].rearrange(
                               "p (kcc kk) -> p kcc kk", kcc=2))
                    pr.dma("sq", vt[B][:, 2 * slot2:2 * slot2 + 2, hh * P:(hh + 1) * P],
                           io["v_all"][slot2 * T + bb * SLOC:slot2 * T + (bb + 1) * SLOC,
                                       hg * SLOC + hh * P:hg * SLOC + (hh + 1) * P
                                       ].rearrange("(kcc p) dd -> p kcc dd", kcc=2))
            loaded[B] = (pr.n["sq"], pr.n["gw"])

        pend = None

        def finalize(pend):
            n, h, bb, avbank, ab = pend
            att = attTv[ab]
            banks.pe_acquire(5)
            for kc in range(NCH):
                inst = t.matmul(pall(5)[0:1, 0:SLOC], onescol[:, :], att[:, kc, :],
                                start=(kc == 0), stop=(kc == NCH - 1))
            pr.inc("pe", inst)
            banks.pe_filled(5)
            B = (n // 2) % 2
            hh = n % 2
            banks.pe_acquire(avbank)
            for kc in range(NCH):
                inst = t.matmul(pall(avbank)[:, 0:SLOC],
                                vt[B][:, kc, hh * P:(hh + 1) * P], att[:, kc, :],
                                start=(kc == 0), stop=(kc == NCH - 1))
            pr.inc("pe", inst)
            banks.pe_filled(avbank)
            attdone[ab] = pr.n["pe"]
            banks.consumer_wait(v, 5)
            sinv = scr[0:1, 6, 0:SLOC]
            inst = v.reciprocal(sinv, pall(5)[0:1, 0:SLOC])
            pr.inc("ve", inst)
            banks.consumed(5, "ve")
            inst = v.tensor_copy(sqa[0:1, 1, 0:SLOC], sinv)
            pr.inc("ve", inst)
            pr.wait(t, "ve")
            banks.pe_acquire(6)
            inst = t.matmul(pall(6)[:, 0:SLOC], ones1[:, :], sqa[0:1, 1, 0:SLOC],
                            start=True, stop=True)
            pr.inc("pe", inst)
            banks.pe_filled(6)
            banks.consumer_wait(v, 6)
            sbc = scr[:, 5, 0:SLOC]
            inst = v.tensor_copy(sbc, pall(6)[:, 0:SLOC])
            pr.inc("ve", inst)
            banks.consumed(6, "ve")
            banks.consumer_wait(v, avbank)
            tfy = scr[:, 4, 0:SLOC]
            v.tensor_tensor(tfy, pall(avbank)[:, 0:SLOC], sbc, ALU.mult)
            inst = v.tensor_scalar(bufA[:, h, bb * SLOC:(bb + 1) * SLOC], tfy,
                                   bslot("b_v1", h), None, ALU.add)
            pr.inc("ve", inst)
            banks.consumed(avbank, "ve")

        def emit_scores(i, hh):
            bb, hg = units[i]
            B = i % 2
            n = i * 2 + hh
            ab = n % 2
            h = hg * 2 + hh
            idx = h * 4 + bb
            q = bufB[:, h, bb * SLOC:(bb + 1) * SLOC]
            pr.wait(t, "sq", loaded[B][0])
            pr.wait(t, "gw", loaded[B][1])
            for grp in range(4):
                b0 = (grp % 2) * 2
                banks.pe_acquire(b0)
                banks.pe_acquire(b0 + 1)
                for ii in range(4):
                    kc = grp * 4 + ii
                    inst = t.matmul(
                        pall_t[:, b0 + ii // 2, (ii % 2) * SLOC:(ii % 2 + 1) * SLOC],
                        kt[B][:, kc, hh * P:(hh + 1) * P], q, start=True, stop=True)
                pr.inc("pe", inst)
                banks.pe_filled(b0)
                banks.pe_filled(b0 + 1)
                banks.consumer_wait(s, b0)
                pr.wait(s, "pe", attdone[ab])
                inst = s.activation(
                    attTv[ab][:, grp * 4:(grp + 1) * 4, :].rearrange("p c t -> p (c t)"),
                    pall_t[:, b0:b0 + 2, :].rearrange("p c t -> p (c t)"),
                    AF.Exp, bias=negbc[:, idx:idx + 1])
                pr.inc("ac", inst)
                banks.consumed(b0, "ac")
                banks.consumed(b0 + 1, "ac")
                pr.wait(v, "ac")
                av_ = attTv[ab][:, grp * 4:(grp + 1) * 4, :].rearrange("p c t -> p (c t)")
                mv_ = maskT[:, grp * 4:(grp + 1) * 4, :].rearrange("p c t -> p (c t)")
                inst = v.tensor_tensor(av_, av_, mv_, ALU.mult)
                pr.inc("ve", inst)
            pr.wait(t, "ve")
            return (n, h, bb, 4 if n % 2 == 0 else 7, ab)

        prefetch(0)
        for i in range(len(units)):
            pend_new = emit_scores(i, 0)
            if pend is not None:
                finalize(pend)
            pend = pend_new
            if i + 1 < len(units):
                prefetch(i + 1)
            pend_new = emit_scores(i, 1)
            finalize(pend)
            pend = pend_new
        finalize(pend)
        pr.barrier()
        if DEBUG_FUSED and fused:
            g.dma_start(out=io["dbg"][:, 256:512], in_=bufA[:, 0, 0:256]).then_inc(pr.h["gw"], 16)
            pr.n["gw"] += 16
            for c in range(NCH):
                g.dma_start(out=io["o_y1"][c], in_=bufA[:, c, :]).then_inc(pr.h["gw"], 16)
                pr.n["gw"] += 16
            pr.wait(g, "gw")

        for oc in range(NCH):
            def evacp(tk, bk, oc=oc):
                banks.consumer_wait(v, bk)
                xs = x_sb[:, oc, tk * TH:(tk + 1) * TH]
                inst = v.scalar_tensor_tensor(xs, pall(bk), bslot("b_proj1", oc), xs,
                                              ALU.add, ALU.add)
                pr.inc("ve", inst)
                banks.consumed(bk, "ve")
            mm_tile(io["w_proj1"][oc],
                    lambda kc, tk: bufA[:, kc, tk * TH:(tk + 1) * TH], 2,
                    evacp, lambda tk, oc=oc: (oc * 2 + tk) % 4)
        pr.barrier()

        for th in range(2):
            for grp in range(8):
                pr.dma("sq", hh_view[:, grp * 8:(grp + 1) * 8, :],
                       io["h_dram"].rearrange("k p t -> p k t")[
                           :, grp * 8:(grp + 1) * 8, th * TH:(th + 1) * TH])
            pr.wait(t, "sq")
            for oc in range(NCH):
                bk = oc % 4
                banks.pe_acquire(bk)
                first = True
                for qtr in range(4):
                    k = ws.load(io["w_down1"][oc, :, qtr * 16:(qtr + 1) * 16, :])
                    ws.use(k)
                    for kk in range(NCH):
                        inst = t.matmul(pall(bk), wbuf[:, k, kk, :],
                                        hh_view[:, qtr * 16 + kk, :],
                                        start=first, stop=(qtr == 3 and kk == NCH - 1))
                        first = False
                    pr.inc("pe", inst)
                    ws.done(k)
                banks.pe_filled(bk)
                banks.consumer_wait(v, bk)
                xs = x_sb[:, oc, th * TH:(th + 1) * TH]
                inst = v.tensor_tensor(xs, xs, pall(bk), ALU.add)
                pr.inc("ve", inst)
                banks.consumed(bk, "ve")
            pr.barrier()

        for c in range(NCH):
            pr.dma("sq", io["o_x"][c], x_sb[:, c, :])
        pr.wait(sy, "sq")

    nc.all_engine_barrier()
    stack.close()
    return nc



def kernel(**inputs):
    X = np.asarray(inputs["L0_attn_X"])
    W = prep_weights(inputs)
    cores = [prep_core(c, X) for c in range(NCORES)]
    core_ids = list(range(NCORES))

    if MODE == "fused":
        nc = build("fused")
        in_maps = []
        for c in range(NCORES):
            m = {"x_in": cores[c]["x_in"], "cos0": cores[c]["cos0"],
                 "sin0": cores[c]["sin0"], "masks": cores[c]["masks"]}
            for k in ("w_qkv0", "b_qkv0", "gb0", "w_out0", "b_out0", "w_gu0", "w_down0",
                      "w_qk1", "b_qk1", "w_v1", "b_v1", "w_proj1", "b_proj1",
                      "w_gu1", "w_down1"):
                m[k] = W[k]
            in_maps.append(m)
        res = run_bass_kernel_spmd(nc, in_maps, core_ids=core_ids)
        outs = [r["o_x"] for r in res.results]
    else:
        nc = build("a")
        in_maps = [{"x_in": cores[c]["x_in"], "w_qkv0": W["w_qkv0"],
                    "b_qkv0": W["b_qkv0"], "gb0": W["gb0"],
                    "cos0": cores[c]["cos0"], "sin0": cores[c]["sin0"]}
                   for c in range(NCORES)]
        ra = run_bass_kernel_spmd(nc, in_maps, core_ids=core_ids).results
        a_all = np.ascontiguousarray(np.stack([ra[c]["o_a"] for c in range(NCORES)]))

        nc = build("b")
        in_maps = [{"x_in": ra[c]["o_x"], "xn_in": np.asarray(ra[c]["o_xn"], BF),
                    "q_in": np.asarray(ra[c]["o_q"], BF), "a_all": a_all,
                    "w_out0": W["w_out0"], "b_out0": W["b_out0"],
                    "w_gu0": W["w_gu0"], "w_down0": W["w_down0"]}
                   for c in range(NCORES)]
        rb = run_bass_kernel_spmd(nc, in_maps, core_ids=core_ids).results

        nc = build("c")
        in_maps = [{"x_in": rb[c]["o_x"], "w_qk1": W["w_qk1"], "b_qk1": W["b_qk1"],
                    "w_v1": W["w_v1"], "w_gu1": W["w_gu1"]}
                   for c in range(NCORES)]
        rc = run_bass_kernel_spmd(nc, in_maps, core_ids=core_ids).results
        k_all = np.ascontiguousarray(
            np.concatenate([np.asarray(rc[c]["k_contrib"], BF) for c in range(NCORES)], 0))
        v_all = np.ascontiguousarray(
            np.concatenate([np.asarray(rc[c]["v_contrib"], BF) for c in range(NCORES)], 0))
        mk2_allT = np.ascontiguousarray(
            np.stack([rc[c]["o_mk2"][0] for c in range(NCORES)]).T)

        nc = build("d")
        in_maps = [{"x_in": rb[c]["o_x"], "q_in": np.asarray(rc[c]["o_q"], BF),
                    "k_all": k_all, "v_all": v_all, "mk2_allT": mk2_allT,
                    "mq2_in": rc[c]["o_mq2"], "h_dram": np.asarray(rc[c]["h_dram"], BF),
                    "masks": cores[c]["masks"], "b_v1": W["b_v1"],
                    "w_proj1": W["w_proj1"], "b_proj1": W["b_proj1"],
                    "w_down1": W["w_down1"]}
                   for c in range(NCORES)]
        rd = run_bass_kernel_spmd(nc, in_maps, core_ids=core_ids).results
        outs = [rd[c]["o_x"] for c in range(NCORES)]

    full = np.empty((NB, SEQ, D), np.float32)
    for c in range(NCORES):
        o = np.asarray(outs[c], np.float32).reshape(NCH, P, NB, SLOC)
        full[:, SLOC * c:SLOC * (c + 1), :] = o.transpose(2, 3, 0, 1).reshape(NB, SLOC, D)
    return full


# version 08
# speedup vs baseline: 1.1625x; 1.0500x over previous
import numpy as np
import contextlib
import ml_dtypes
import concourse.bass as bass
import concourse.mybir as mybir
from concourse.bass_utils import run_bass_kernel_spmd

F32 = mybir.dt.float32
BF16 = mybir.dt.bfloat16
AF = mybir.ActivationFunctionType
ALU = mybir.AluOpType
AX = mybir.AxisListType

P = 128
NCORES = 8
D = 2048
NB = 4
SEQ = 2048
SLOC = SEQ // NCORES
T = NB * SLOC
TH = T // 2
NCH = D // P
IFF = 8192
ICH = IFF // P
EPS = 1e-5
RSCALE = float(1.0 / np.sqrt(128.0))
BF = ml_dtypes.bfloat16

MODE = "fused"
DEBUG_FUSED = False



def _perm_rot():
    return np.concatenate([np.arange(0, 1024, 2), np.arange(1, 1024, 2),
                           np.arange(1024, 2048)])


def _perm_24():
    q = np.arange(IFF)
    return (q % 2048) * 4 + q // 2048


def _tiles(w, row_idx, col_idx):
    wp = np.asarray(w)[np.ix_(row_idx, col_idx)]
    nk = len(row_idx) // P
    no = len(col_idx) // P
    return np.ascontiguousarray(
        wp.reshape(nk, P, no, P).transpose(2, 1, 0, 3)).astype(np.float32)


def prep_weights(ins):
    pr_ = _perm_rot()
    p24 = _perm_24()
    rows = np.arange(D)
    c = {}
    W_in = np.asarray(ins["L0_attn_W_in"])
    b_in = np.asarray(ins["L0_attn_b_in"])
    colperm = np.concatenate([2048 + pr_, 4096 + pr_, 0 + pr_])
    c["w_qkv0"] = _tiles(W_in, rows, colperm)
    c["b_qkv0"] = np.ascontiguousarray(b_in[colperm].reshape(48, P).T).astype(np.float32)
    c["gb0"] = np.ascontiguousarray(np.concatenate(
        [np.asarray(ins["L0_attn_gamma"]).reshape(NCH, P).T,
         np.asarray(ins["L0_attn_beta"]).reshape(NCH, P).T], axis=1)).astype(np.float32)
    c["w_out0"] = _tiles(np.asarray(ins["L0_attn_W_out"]), pr_, rows)
    c["b_out0"] = np.ascontiguousarray(
        np.asarray(ins["L0_attn_b_out"]).reshape(NCH, P).T).astype(np.float32)
    wg = np.asarray(ins["L0_ffn_w_gate"])
    wu = np.asarray(ins["L0_ffn_w_up"])
    tg = _tiles(wg, rows, p24)
    tu = _tiles(wu, rows, p24)
    gu = np.empty((ICH, 2, P, NCH, P), np.float32)
    slot = 0
    for c0 in range(16):
        for j in range(4):
            ic = j * 16 + c0
            gu[slot, 0] = tg[ic]
            gu[slot, 1] = tu[ic]
            slot += 1
    c["w_gu0"] = np.ascontiguousarray(gu)
    c["w_down0"] = _tiles(np.asarray(ins["L0_ffn_w_down"]), p24, rows)
    wa = np.asarray(ins["L1_attn_w_attn"])
    ba = np.asarray(ins["L1_attn_b_attn"])
    ccols = np.concatenate([2048 + rows, 0 + rows])
    c["w_qk1"] = _tiles(wa, rows, ccols)
    c["b_qk1"] = np.ascontiguousarray(
        np.concatenate([ba[2048:4096].reshape(NCH, P).T,
                        ba[0:2048].reshape(NCH, P).T], axis=1)).astype(np.float32)
    c["w_v1"] = np.ascontiguousarray(
        wa[:, 4096:6144].reshape(NCH, P, D).transpose(1, 0, 2)).astype(np.float32)
    c["b_v1"] = np.ascontiguousarray(
        ba[4096:6144].reshape(NCH, P).T).astype(np.float32)
    c["w_proj1"] = _tiles(np.asarray(ins["L1_attn_w_proj"]), rows, rows)
    c["b_proj1"] = np.ascontiguousarray(
        np.asarray(ins["L1_attn_b_proj"]).reshape(NCH, P).T).astype(np.float32)
    wgu = np.asarray(ins["L1_ffn_w_gate_up"])
    gu1 = np.empty((ICH, 2, P, NCH, P), np.float32)
    for g in range(2):
        tt = _tiles(wgu[g * IFF:(g + 1) * IFF, :].T, rows, np.arange(IFF))
        for ic in range(ICH):
            gu1[ic, g] = tt[ic]
    c["w_gu1"] = np.ascontiguousarray(gu1)
    c["w_down1"] = _tiles(np.asarray(ins["L1_ffn_w_down"]).T, np.arange(IFF), rows)
    return c


def prep_core(cid, X):
    d = {}
    Xc = np.asarray(X)[:, SLOC * cid:SLOC * (cid + 1), :]
    d["x_in"] = np.ascontiguousarray(
        Xc.transpose(2, 0, 1).reshape(NCH, P, T)).astype(np.float32)
    pos = (SLOC * cid + np.arange(SLOC)).astype(np.float64)
    inv = 1.0 / (10000.0 ** (np.arange(0, 1024, 2, dtype=np.float64) / 1024.0))
    fr = pos[:, None] * inv[None, :]
    cos = np.cos(fr).T.reshape(4, P, SLOC).transpose(1, 0, 2)
    sin = np.sin(fr).T.reshape(4, P, SLOC).transpose(1, 0, 2)
    d["cos0"] = np.ascontiguousarray(np.tile(cos, (1, 1, 2))).astype(np.float32)
    d["sin0"] = np.ascontiguousarray(np.tile(sin, (1, 1, 2))).astype(np.float32)
    kg = np.arange(SEQ).reshape(NCH, P)
    qg = SLOC * cid + np.arange(SLOC)
    m = (kg[:, :, None] <= qg[None, None, :]).astype(np.float32)
    d["masks"] = np.ascontiguousarray(m.transpose(1, 0, 2))
    return d



class Prog:
    def __init__(self, nc, stack):
        self.nc = nc
        self.g, self.t, self.v, self.s, self.sy = (
            nc.gpsimd, nc.tensor, nc.vector, nc.scalar, nc.sync)
        self.h = {}
        self.n = {}
        for name in ("gw", "sq", "sc", "pe", "ac", "ve", "cc"):
            self.h[name] = stack.enter_context(nc.semaphore(name))
            self.n[name] = 0

    def inc(self, name, inst, k=1):
        inst.then_inc(self.h[name], k)
        self.n[name] += k
        return self.n[name]

    def wait(self, eng, name, val=None):
        eng.wait_ge(self.h[name], self.n[name] if val is None else val)

    def dma(self, q, out, in_, **kw):
        eng = {"gw": self.g, "sq": self.sy, "sc": self.s}[q]
        inst = eng.dma_start(out=out, in_=in_, **kw)
        return self.inc(q, inst, 16)

    def barrier(self):
        self.nc.all_engine_barrier()


class WStream:
    def __init__(self, pr, slot_aps):
        self.pr = pr
        self.aps = slot_aps
        self.freed = [0] * len(slot_aps)
        self.ready = [0] * len(slot_aps)
        self.i = 0

    def load(self, src):
        k = self.i % len(self.aps)
        if self.freed[k]:
            self.pr.wait(self.pr.g, "pe", self.freed[k])
        self.pr.dma("gw", self.aps[k], src)
        self.ready[k] = self.pr.n["gw"]
        self.i += 1
        return k

    def use(self, k):
        self.pr.wait(self.pr.t, "gw", self.ready[k])

    def done(self, k):
        self.freed[k] = self.pr.n["pe"]


class Banks:
    def __init__(self, pr):
        self.pr = pr
        self.mark = [None] * 8
        self.fill = [None] * 8

    def pe_acquire(self, idx):
        if self.mark[idx] is not None:
            self.pr.wait(self.pr.t, *self.mark[idx])

    def pe_filled(self, idx):
        self.fill[idx] = ("pe", self.pr.n["pe"])

    def consumer_wait(self, eng, idx):
        self.pr.wait(eng, *self.fill[idx])

    def consumed(self, idx, sem):
        self.mark[idx] = (sem, self.pr.n[sem])


def build(mode):
    nc = bass.Bass("TRN2", target_bir_lowering=False, debug=False)
    stack = contextlib.ExitStack()
    pr = Prog(nc, stack)
    g, t, v, s, sy = pr.g, pr.t, pr.v, pr.s, pr.sy
    fused = mode == "fused"
    do = lambda *phases: fused or mode in phases

    def dram_in(name, shape, dt=F32):
        return nc.dram_tensor(name, shape, dt, kind="ExternalInput").ap()

    def dram_out(name, shape, dt=F32):
        return nc.dram_tensor(name, shape, dt, kind="ExternalOutput").ap()

    def dram_int(name, shape, dt=F32):
        return nc.dram_tensor(name, shape, dt).ap()

    io = {}
    io["x_in"] = dram_in("x_in", [NCH, P, T])
    if do("a"):
        io["w_qkv0"] = dram_in("w_qkv0", [48, P, NCH, P])
        io["b_qkv0"] = dram_in("b_qkv0", [P, 48])
        io["gb0"] = dram_in("gb0", [P, 32])
        io["cos0"] = dram_in("cos0", [P, 4, TH])
        io["sin0"] = dram_in("sin0", [P, 4, TH])
    if do("b"):
        io["w_out0"] = dram_in("w_out0", [NCH, P, NCH, P])
        io["b_out0"] = dram_in("b_out0", [P, NCH])
        io["w_gu0"] = dram_in("w_gu0", [ICH, 2, P, NCH, P])
        io["w_down0"] = dram_in("w_down0", [NCH, P, ICH, P])
    if do("c"):
        io["w_qk1"] = dram_in("w_qk1", [32, P, NCH, P])
        io["b_qk1"] = dram_in("b_qk1", [P, 32])
        io["w_v1"] = dram_in("w_v1", [P, NCH, D])
        io["w_gu1"] = dram_in("w_gu1", [ICH, 2, P, NCH, P])
    if do("d"):
        io["masks"] = dram_in("masks", [P, NCH, SLOC])
        io["b_v1"] = dram_in("b_v1", [P, NCH])
        io["w_proj1"] = dram_in("w_proj1", [NCH, P, NCH, P])
        io["b_proj1"] = dram_in("b_proj1", [P, NCH])
        io["w_down1"] = dram_in("w_down1", [NCH, P, ICH, P])

    if fused:
        io["a_cc_in"] = dram_int("a_cc_in", [P, 64])
        io["a_cc_out"] = dram_int("a_cc_out", [P, 64])
        io["mk2_cc_in"] = dram_int("mk2_cc_in", [1, 64])
        io["mk2_cc_out"] = dram_int("mk2_cc_out", [1, 64])
        io["k_contrib"] = dram_int("k_contrib", [D, T], BF16)
        io["v_contrib"] = dram_int("v_contrib", [T, D], BF16)
        io["k_all"] = dram_int("k_all", [NCORES * D, T], BF16)
        io["v_all"] = dram_int("v_all", [NCORES * T, D], BF16)
        io["h_dram"] = dram_int("h_dram", [ICH, P, T], BF16)
        io["o_x"] = dram_out("o_x", [NCH, P, T])
        if DEBUG_FUSED:
            io["dbg"] = dram_out("dbg", [P, 512])
            io["o_x2"] = dram_out("o_x2", [NCH, P, T])
            io["o_y1"] = dram_out("o_y1", [NCH, P, T])
            io["o_xn1"] = dram_out("o_xn1", [NCH, P, T])
            io["o_kall"] = dram_out("o_kall", [NCORES * D, T], BF16)
            io["o_vall"] = dram_out("o_vall", [NCORES * T, D], BF16)
            io["o_h1"] = dram_out("o_h1", [ICH, P, T], BF16)
    else:
        if mode == "a":
            io["o_x"] = dram_out("o_x", [NCH, P, T])
            io["o_xn"] = dram_out("o_xn", [NCH, P, T], BF16)
            io["o_q"] = dram_out("o_q", [NCH, P, T], BF16)
            io["o_a"] = dram_out("o_a", [P, 64])
        if mode == "b":
            io["xn_in"] = dram_in("xn_in", [NCH, P, T], BF16)
            io["q_in"] = dram_in("q_in", [NCH, P, T], BF16)
            io["a_all"] = dram_in("a_all", [NCORES, P, 64])
            io["h_dram"] = dram_int("h_dram", [ICH, P, T], BF16)
            io["o_x"] = dram_out("o_x", [NCH, P, T])
        if mode == "c":
            io["o_q"] = dram_out("o_q", [NCH, P, T], BF16)
            io["k_contrib"] = dram_out("k_contrib", [D, T], BF16)
            io["v_contrib"] = dram_out("v_contrib", [T, D], BF16)
            io["o_mk2"] = dram_out("o_mk2", [1, 64])
            io["o_mq2"] = dram_out("o_mq2", [1, 64])
            io["h_dram"] = dram_out("h_dram", [ICH, P, T], BF16)
        if mode == "d":
            io["q_in"] = dram_in("q_in", [NCH, P, T], BF16)
            io["k_all"] = dram_in("k_all", [NCORES * D, T], BF16)
            io["v_all"] = dram_in("v_all", [NCORES * T, D], BF16)
            io["mk2_allT"] = dram_in("mk2_allT", [64, NCORES])
            io["mq2_in"] = dram_in("mq2_in", [1, 64])
            io["h_dram"] = dram_in("h_dram", [ICH, P, T], BF16)
            io["o_x"] = dram_out("o_x", [NCH, P, T])

    ksr = bass.get_kernel_semaphore_range()
    nmono = len(nc._monotonic_sems) if hasattr(nc, "_monotonic_sems") else 0
    base = ksr.start + 2 + (1 if nc._bir_kernel_barrier_sem is not None else 0) + nmono
    clear_range = range(base, ksr.stop)
    g.dma_reset(clear_range)
    g.sem_clear(clear_range)
    nc.all_engine_barrier()

    x_sb = stack.enter_context(nc.sbuf_tensor("x_sb", [P, NCH, T], F32))
    pool = stack.enter_context(nc.sbuf_tensor("pool", [P, 3 * NCH * T], BF16))
    xn3 = pool[:, 0:NCH * T].rearrange("p (c t) -> p c t", c=NCH)
    bufA = pool[:, NCH * T:2 * NCH * T].rearrange("p (c t) -> p c t", c=NCH)
    bufB = pool[:, 2 * NCH * T:3 * NCH * T].rearrange("p (c t) -> p c t", c=NCH)
    hh_view = pool[:, 0:ICH * TH].rearrange("p (c t) -> p c t", c=ICH)
    wbuf = stack.enter_context(nc.sbuf_tensor("wbuf", [P, 2, NCH, P], BF16))
    scr = stack.enter_context(nc.sbuf_tensor("scr", [P, 15, TH], F32))
    persist = stack.enter_context(nc.sbuf_tensor("persist", [P, 256], F32))
    sqa = stack.enter_context(nc.sbuf_tensor("sqa", [P, 2, TH], BF16))
    sqb = stack.enter_context(nc.sbuf_tensor("sqb", [P, 2, TH], BF16))
    stage = stack.enter_context(nc.sbuf_tensor("stage", [P, 2, TH], BF16))
    bias_sb = stack.enter_context(nc.sbuf_tensor("bias_sb", [P, 160], F32))
    ones1 = stack.enter_context(nc.sbuf_tensor("ones1", [1, P], BF16))
    onescol = stack.enter_context(nc.sbuf_tensor("onescol", [P, 1], BF16))
    pall_t = stack.enter_context(nc.psum_tensor("pall", [P, 8, 512], F32))
    pall = lambda k: pall_t[:, k, :]
    banks = Banks(pr)

    g.memset(ones1[:, :], 1.0)
    g.memset(onescol[:, :], 1.0)
    ws = WStream(pr, [wbuf[:, 0, :, :], wbuf[:, 1, :, :]])
    wsv = WStream(pr, [bufA[:, :, 0:512], bufA[:, :, 512:1024]])

    BOFF = {}
    boff = [0]

    def bias_load(name, width):
        BOFF[name] = boff[0]
        pr.dma("sq", bias_sb[:, boff[0]:boff[0] + width], io[name])
        boff[0] += width

    def bslot(name, i):
        return bias_sb[:, BOFF[name] + i:BOFF[name] + i + 1]

    for c in range(NCH):
        pr.dma("sq", x_sb[:, c, :], io["x_in"][c])
    if do("a"):
        bias_load("b_qkv0", 48)
        bias_load("gb0", 32)
    if do("b"):
        bias_load("b_out0", NCH)
    if do("c"):
        bias_load("b_qk1", 32)
    if do("d"):
        bias_load("b_v1", NCH)
        bias_load("b_proj1", NCH)
    for eng in (v, s, t):
        pr.wait(eng, "sq")

    sqa_flat = sqa[:, :, :].rearrange("p a t -> p (a t)")

    def ln_stats(kind, mean_row, rstd_row):
        for bk in (4, 5, 6, 7):
            banks.pe_acquire(bk)
        mm_done = 0
        for c in range(NCH):
            if kind == "x":
                pr.wait(v, "pe", mm_done)
                inst = v.tensor_copy(sqa_flat, x_sb[:, c, :])
                pr.inc("ve", inst)
                xb = sqa_flat
            else:
                xb = xn3[:, c, :]
            pr.wait(s, "pe", mm_done)
            if kind == "x":
                pr.wait(s, "ve")
            s.activation(sqb[:, 0, :], xb[:, 0:TH], AF.Square)
            inst = s.activation(sqb[:, 1, :], xb[:, TH:T], AF.Square)
            pr.inc("ac", inst)
            pr.wait(t, "ac")
            if kind == "x":
                pr.wait(t, "ve")
            for hf in range(2):
                t.matmul(pall(4 + hf)[0:1, :], onescol[:, :], xb[:, hf * TH:(hf + 1) * TH],
                         start=(c == 0), stop=(c == NCH - 1))
                inst = t.matmul(pall(6 + hf)[0:1, :], onescol[:, :], sqb[:, hf, :],
                                start=(c == 0), stop=(c == NCH - 1))
            pr.inc("pe", inst)
            mm_done = pr.n["pe"]
        pr.wait(s, "pe")
        for hf in range(2):
            s.activation(mean_row[0:1, hf * TH:(hf + 1) * TH], pall(4 + hf)[0:1, :],
                         AF.Identity, scale=1.0 / D)
            inst = s.activation(rstd_row[0:1, hf * TH:(hf + 1) * TH], pall(6 + hf)[0:1, :],
                                AF.Identity, scale=1.0 / D)
        pr.inc("ac", inst)
        for bk in (4, 5, 6, 7):
            banks.consumed(bk, "ac")
        pr.wait(v, "ac")
        vv = scr[0:1, 14, 0:TH]
        for hf in range(2):
            mr = mean_row[0:1, hf * TH:(hf + 1) * TH]
            rr = rstd_row[0:1, hf * TH:(hf + 1) * TH]
            v.tensor_tensor(vv, mr, mr, ALU.mult)
            v.tensor_tensor(rr, rr, vv, ALU.subtract)
            inst = v.tensor_scalar(rr, rr, EPS, None, ALU.add)
        pr.inc("ve", inst)
        pr.wait(s, "ve")
        s.activation(rstd_row, rstd_row, AF.Ln)
        inst = s.activation(rstd_row, rstd_row, AF.Exp, scale=-0.5)
        pr.inc("ac", inst)
        pr.wait(v, "ac")

    def bcast_row_f32(row_f32, width, out_slot, bank=3):
        rb = sqa[0:1, 0, 0:width]
        inst = v.tensor_copy(rb, row_f32)
        pr.inc("ve", inst)
        pr.wait(t, "ve")
        banks.pe_acquire(bank)
        inst = t.matmul(pall(bank)[:, 0:width], ones1[:, :], rb, start=True, stop=True)
        pr.inc("pe", inst)
        banks.pe_filled(bank)
        banks.consumer_wait(v, bank)
        inst = v.tensor_copy(out_slot, pall(bank)[:, 0:width])
        pr.inc("ve", inst)
        banks.consumed(bank, "ve")

    def mm_tile(oc_src, rhs_getter, n_tk, evac_fn, bank_of):
        k = ws.load(oc_src)
        ws.use(k)
        for tk in range(n_tk):
            bk = bank_of(tk)
            banks.pe_acquire(bk)
            for kc in range(NCH):
                inst = t.matmul(pall(bk), wbuf[:, k, kc, :], rhs_getter(kc, tk),
                                start=(kc == 0), stop=(kc == NCH - 1))
            pr.inc("pe", inst)
            banks.pe_filled(bk)
            evac_fn(tk, bk)
        ws.done(k)

    mean_bc = scr[:, 0:2, :].rearrange("p a t -> p (a t)")
    rstd_bc = scr[:, 2:4, :].rearrange("p a t -> p (a t)")
    tf2 = scr[:, 4:6, :].rearrange("p a t -> p (a t)")
    rowA = scr[0:1, 12:14, :].rearrange("p a t -> p (a t)")
    rowB = scr[0:1, 10:12, :].rearrange("p a t -> p (a t)")

    def ln_apply_bcasts(mean_row, rstd_row):
        for hf in range(2):
            bcast_row_f32(mean_row[0:1, hf * TH:(hf + 1) * TH], TH, scr[:, 0 + hf, :])
            bcast_row_f32(rstd_row[0:1, hf * TH:(hf + 1) * TH], TH, scr[:, 2 + hf, :])

    if do("a"):
        ln_stats("x", rowA, rowB)
        ln_apply_bcasts(rowA, rowB)
        for c in range(NCH):
            v.tensor_tensor(tf2, x_sb[:, c, :], mean_bc, ALU.subtract)
            v.tensor_tensor(tf2, tf2, rstd_bc, ALU.mult)
            v.tensor_copy(xn3[:, c, :], tf2)
            inst = v.tensor_tensor(x_sb[:, c, :], x_sb[:, c, :], tf2, ALU.add)
        pr.inc("ve", inst)
        pr.barrier()

        ln_stats("xn", rowA, rowB)
        ln_apply_bcasts(rowA, rowB)
        for c in range(NCH):
            v.tensor_tensor(tf2, xn3[:, c, :], mean_bc, ALU.subtract)
            v.tensor_tensor(tf2, tf2, rstd_bc, ALU.mult)
            inst = v.tensor_scalar(bufA[:, c, :], tf2, bslot("gb0", c),
                                   bslot("gb0", 16 + c), ALU.mult, ALU.add)
        pr.inc("ve", inst)
        pr.barrier()

        for rc in range(4):
            pr.dma("sq", scr[:, 0 + rc, :], io["cos0"][:, rc, :])
            pr.dma("sq", scr[:, 4 + rc, :], io["sin0"][:, rc, :])
        pr.wait(v, "sq")

        for hf in range(2):
            cols = slice(hf * TH, (hf + 1) * TH)
            for oc in range(32):
                dst = bufB[:, oc % 16, 0:TH] if oc < 16 else bufB[:, oc % 16, TH:T]

                def evac(tk, bk, oc=oc, dst=dst):
                    banks.consumer_wait(s, bk)
                    inst = s.activation(dst, pall(bk), AF.Identity, bias=bslot("b_qkv0", oc))
                    pr.inc("ac", inst)
                    banks.consumed(bk, "ac")
                mm_tile(io["w_qkv0"][oc], lambda kc, tk: bufA[:, kc, cols], 1,
                        evac, lambda tk, oc=oc: oc % 4)
            pr.wait(v, "ac")
            sE, sO, sT = scr[:, 8, :], scr[:, 9, :], scr[:, 10, :]
            for part in (slice(0, TH), slice(TH, T)):
                for rc in range(4):
                    Ke = bufB[:, rc, part]
                    Ko = bufB[:, rc + 4, part]
                    cosr = scr[:, 0 + rc, :]
                    sinr = scr[:, 4 + rc, :]
                    v.tensor_tensor(sE, Ke, cosr, ALU.mult)
                    v.tensor_tensor(sT, Ko, sinr, ALU.mult)
                    v.tensor_tensor(sE, sE, sT, ALU.subtract)
                    v.tensor_tensor(sO, Ko, cosr, ALU.mult)
                    v.tensor_tensor(sT, Ke, sinr, ALU.mult)
                    v.tensor_tensor(sO, sO, sT, ALU.add)
                    v.tensor_copy(Ke, sE)
                    inst = v.tensor_copy(Ko, sO)
            pr.inc("ve", inst)
            for ofs, bank in ((0, 4), (TH, 5)):
                banks.pe_acquire(bank)
                mmv = [0, 0]
                for c in range(NCH):
                    pr.wait(s, "ve")
                    pr.wait(s, "pe", mmv[c % 2])
                    inst = s.activation(sqb[:, c % 2, :], bufB[:, c, ofs:ofs + TH], AF.Square)
                    pr.inc("ac", inst)
                    pr.wait(t, "ac")
                    inst = t.matmul(pall(bank)[0:1, :], onescol[:, :], sqb[:, c % 2, :],
                                    start=(c == 0), stop=(c == NCH - 1))
                    pr.inc("pe", inst)
                    mmv[c % 2] = pr.n["pe"]
                banks.pe_filled(bank)
            nk2 = scr[0:1, 12, :]
            nv2 = scr[0:1, 13, :]
            banks.consumer_wait(v, 4)
            banks.consumer_wait(v, 5)
            v.tensor_copy(nk2, pall(4)[0:1, :])
            inst = v.tensor_copy(nv2, pall(5)[0:1, :])
            pr.inc("ve", inst)
            banks.consumed(4, "ve")
            banks.consumed(5, "ve")
            inst = v.tensor_tensor(nk2, nk2, nv2, ALU.mult)
            pr.inc("ve", inst)
            pr.wait(s, "ve")
            s.activation(nk2, nk2, AF.Ln)
            inst = s.activation(nk2, nk2, AF.Exp, scale=-0.5)
            pr.inc("ac", inst)
            pr.wait(v, "ac")
            bcast_row_f32(nk2, TH, scr[:, 11, :])
            pP = scr[:, 8, :]
            for c in range(NCH):
                v.tensor_tensor(pP, bufB[:, c, 0:TH], bufB[:, c, TH:T], ALU.mult)
                v.tensor_tensor(pP, pP, scr[:, 11, :], ALU.mult)
                for b2 in range(2):
                    b = hf * 2 + b2
                    inst = v.reduce_sum(persist[:, c * 4 + b:c * 4 + b + 1],
                                        pP[:, b2 * SLOC:(b2 + 1) * SLOC], AX.X)
            pr.inc("ve", inst)
            pr.barrier()

        if mode == "a":
            pr.dma("sq", io["o_a"], persist[:, 0:64])

        for oc in range(32, 48):
            def evacq(tk, bk, oc=oc):
                banks.consumer_wait(s, bk)
                inst = s.activation(bufB[:, oc - 32, tk * TH:(tk + 1) * TH], pall(bk),
                                    AF.Identity, bias=bslot("b_qkv0", oc))
                pr.inc("ac", inst)
                banks.consumed(bk, "ac")
            mm_tile(io["w_qkv0"][oc],
                    lambda kc, tk: bufA[:, kc, tk * TH:(tk + 1) * TH], 2,
                    evacq, lambda tk, oc=oc: (oc * 2 + tk) % 4)
        if fused:
            pr.dma("gw", io["a_cc_in"], persist[:, 0:64])
            pr.wait(g, "gw")
            inst = g.collective_compute(
                "AllReduce", ALU.add, replica_groups=[list(range(NCORES))],
                ins=[io["a_cc_in"].tensor.ap().opt()],
                outs=[io["a_cc_out"].tensor.ap().opt()])
            pr.inc("cc", inst)
            pr.wait(g, "cc")
        pr.barrier()
        if mode == "a":
            for c in range(NCH):
                pr.dma("sq", io["o_x"][c], x_sb[:, c, :])
                pr.dma("sq", io["o_xn"][c], xn3[:, c, :])
                pr.dma("sq", io["o_q"][c], bufB[:, c, :])
            pr.wait(sy, "sq")

    if do("b"):
        if mode == "b":
            for c in range(NCH):
                pr.dma("sq", xn3[:, c, :], io["xn_in"][c])
                pr.dma("sq", bufB[:, c, :], io["q_in"][c])
            pr.dma("sq", scr[:, 0, :].rearrange("p (r c) -> p r c", r=NCORES),
                   io["a_all"].rearrange("r p c -> p r c"))
            pr.wait(v, "sq")
            av = scr[:, 0, :].rearrange("p (r c) -> p r c", r=NCORES)
            v.tensor_copy(persist[:, 0:64], av[:, 0, :])
            for r in range(1, NCORES):
                inst = v.tensor_tensor(persist[:, 0:64], persist[:, 0:64],
                                       av[:, r, :], ALU.add)
            pr.inc("ve", inst)
        if fused:
            pr.wait(sy, "cc")
            pr.dma("sq", persist[:, 0:64], io["a_cc_out"])
            pr.wait(v, "sq")
            if DEBUG_FUSED:
                pr.dma("sq", io["dbg"][:, 0:64], persist[:, 0:64])
        inst = v.tensor_scalar(persist[:, 0:64], persist[:, 0:64], 0.5, 0.5,
                               ALU.mult, ALU.add)
        pr.inc("ve", inst)
        for c in range(NCH):
            for b in range(NB):
                sl = bufB[:, c, b * SLOC:(b + 1) * SLOC]
                inst = v.tensor_scalar(sl, sl, persist[:, c * 4 + b:c * 4 + b + 1],
                                       None, ALU.mult)
        pr.inc("ve", inst)
        pr.barrier()

        for oc in range(NCH):
            def evaco(tk, bk, oc=oc):
                banks.consumer_wait(v, bk)
                xs = x_sb[:, oc, tk * TH:(tk + 1) * TH]
                inst = v.scalar_tensor_tensor(xs, pall(bk), bslot("b_out0", oc), xs,
                                              ALU.add, ALU.add)
                pr.inc("ve", inst)
                banks.consumed(bk, "ve")
            mm_tile(io["w_out0"][oc],
                    lambda kc, tk: bufB[:, kc, tk * TH:(tk + 1) * TH], 2,
                    evaco, lambda tk, oc=oc: (oc * 2 + tk) % 4)
        pr.barrier()

        stage_free = [0, 0]
        for c0 in range(NCH):
            for j in range(4):
                slot_i = c0 * 4 + j
                for gg in range(2):
                    k = ws.load(io["w_gu0"][slot_i, gg])
                    ws.use(k)
                    for tk in range(2):
                        bk = gg * 2 + tk
                        banks.pe_acquire(bk)
                        for kc in range(NCH):
                            inst = t.matmul(pall(bk), wbuf[:, k, kc, :],
                                            xn3[:, kc, tk * TH:(tk + 1) * TH],
                                            start=(kc == 0), stop=(kc == NCH - 1))
                        pr.inc("pe", inst)
                        banks.pe_filled(bk)
                    ws.done(k)
                for tk in range(2):
                    gbk, ubk = tk, 2 + tk
                    banks.consumer_wait(s, gbk)
                    pr.wait(s, "ve")
                    inst = s.activation(scr[:, 8 + tk, :], pall(gbk), AF.Silu)
                    pr.inc("ac", inst)
                    banks.consumed(gbk, "ac")
                    banks.consumer_wait(v, ubk)
                    pr.wait(v, "ac")
                    inst = v.tensor_tensor(scr[:, 2 * j + tk, :], scr[:, 8 + tk, :],
                                           pall(ubk), ALU.mult)
                    pr.inc("ve", inst)
                    banks.consumed(ubk, "ve")
            for tk in range(2):
                hs = [scr[:, 2 * j + tk, :] for j in range(4)]
                s0, s1, s2 = scr[:, 10, :], scr[:, 11, :], scr[:, 12, :]
                ab = [scr[:, 8, :], scr[:, 9, :], scr[:, 13, :], scr[:, 14, :]]
                pr.wait(s, "ve")
                for j in range(4):
                    inst = s.activation(ab[j], hs[j], AF.Abs)
                pr.inc("ac", inst)
                pr.wait(v, "ac")
                v.tensor_tensor(s0, ab[0], ab[1], ALU.max)
                v.tensor_tensor(s1, ab[2], ab[3], ALU.max)
                v.tensor_tensor(s2, s0, s1, ALU.min)
                v.tensor_tensor(s0, ab[0], ab[1], ALU.min)
                v.tensor_tensor(s1, ab[2], ab[3], ALU.min)
                v.tensor_tensor(s0, s0, s1, ALU.max)
                v.tensor_tensor(s2, s2, s0, ALU.max)
                for j in range(4):
                    st = (tk * 4 + j) % 2
                    v.tensor_tensor(ab[j], ab[j], s2, ALU.is_ge)
                    v.tensor_tensor(hs[j], hs[j], ab[j], ALU.mult)
                    pr.wait(v, "sq", stage_free[st])
                    inst = v.tensor_copy(stage[:, st, :], hs[j])
                    pr.inc("ve", inst)
                    pr.wait(sy, "ve")
                    pr.dma("sq", io["h_dram"][j * 16 + c0, :, tk * TH:(tk + 1) * TH],
                           stage[:, st, :])
                    stage_free[st] = pr.n["sq"]
        pr.barrier()

        for th in range(2):
            for grp in range(8):
                pr.dma("sq", hh_view[:, grp * 8:(grp + 1) * 8, :],
                       io["h_dram"].rearrange("k p t -> p k t")[
                           :, grp * 8:(grp + 1) * 8, th * TH:(th + 1) * TH])
            pr.wait(t, "sq")
            for oc in range(NCH):
                bk = oc % 4
                banks.pe_acquire(bk)
                first = True
                for qtr in range(4):
                    k = ws.load(io["w_down0"][oc, :, qtr * 16:(qtr + 1) * 16, :])
                    ws.use(k)
                    for kk in range(NCH):
                        inst = t.matmul(pall(bk), wbuf[:, k, kk, :],
                                        hh_view[:, qtr * 16 + kk, :],
                                        start=first, stop=(qtr == 3 and kk == NCH - 1))
                        first = False
                    pr.inc("pe", inst)
                    ws.done(k)
                banks.pe_filled(bk)
                banks.consumer_wait(v, bk)
                xs = x_sb[:, oc, th * TH:(th + 1) * TH]
                inst = v.tensor_tensor(xs, xs, pall(bk), ALU.add)
                pr.inc("ve", inst)
                banks.consumed(bk, "ve")
            pr.barrier()
        if fused and DEBUG_FUSED:
            for c in range(NCH):
                pr.dma("sq", io["o_x2"][c], x_sb[:, c, :])
            pr.wait(sy, "sq")
        if mode == "b":
            for c in range(NCH):
                pr.dma("sq", io["o_x"][c], x_sb[:, c, :])
            pr.wait(sy, "sq")

    if do("c"):
        ln_stats("x", rowA, rowB)
        ln_apply_bcasts(rowA, rowB)
        for c in range(NCH):
            v.tensor_tensor(tf2, x_sb[:, c, :], mean_bc, ALU.subtract)
            v.tensor_tensor(tf2, tf2, rstd_bc, ALU.mult)
            inst = v.tensor_copy(xn3[:, c, :], tf2)
        pr.inc("ve", inst)
        pr.barrier()
        if fused and DEBUG_FUSED:
            for c in range(NCH):
                g.dma_start(out=io["o_xn1"][c], in_=xn3[:, c, :]).then_inc(pr.h["gw"], 16)
                pr.n["gw"] += 16
            pr.wait(g, "gw")

        inst = v.tensor_scalar(bias_sb[:, BOFF["b_qk1"] + 16:BOFF["b_qk1"] + 32],
                               bias_sb[:, BOFF["b_qk1"] + 16:BOFF["b_qk1"] + 32],
                               RSCALE, None, ALU.mult)
        pr.inc("ve", inst)
        pr.wait(s, "ve")

        stage_free = [0, 0]
        sq_free = [0, 0]
        for oc in range(NCH):
            k = ws.load(io["w_qk1"][oc])
            ws.use(k)
            for tk in range(2):
                bk = tk
                banks.pe_acquire(bk)
                for kc in range(NCH):
                    inst = t.matmul(pall(bk), wbuf[:, k, kc, :],
                                    xn3[:, kc, tk * TH:(tk + 1) * TH],
                                    start=(kc == 0), stop=(kc == NCH - 1))
                pr.inc("pe", inst)
                banks.pe_filled(bk)
                banks.consumer_wait(s, bk)
                pr.wait(s, "sq", stage_free[tk])
                inst = s.activation(stage[:, tk, :], pall(bk), AF.Identity,
                                    bias=bslot("b_qk1", oc))
                pr.inc("ac", inst)
                banks.consumed(bk, "ac")
                pr.wait(sy, "ac")
                pr.dma("sq", io["k_contrib"][oc * P:(oc + 1) * P, tk * TH:(tk + 1) * TH],
                       stage[:, tk, :])
                stage_free[tk] = pr.n["sq"]
                pr.wait(s, "pe", sq_free[tk])
                inst = s.activation(sqb[:, tk, :], stage[:, tk, :], AF.Square)
                pr.inc("ac", inst)
                pr.wait(t, "ac")
                banks.pe_acquire(6 + tk)
                inst = t.matmul(pall(6 + tk)[0:1, :], onescol[:, :], sqb[:, tk, :],
                                start=True, stop=True)
                pr.inc("pe", inst)
                sq_free[tk] = pr.n["pe"]
                banks.pe_filled(6 + tk)
                banks.consumer_wait(v, 6 + tk)
                for b2 in range(2):
                    b = tk * 2 + b2
                    col = 64 + oc * 4 + b
                    inst = v.reduce_max(persist[0:1, col:col + 1],
                                        pall(6 + tk)[0:1, b2 * SLOC:(b2 + 1) * SLOC], AX.X)
                pr.inc("ve", inst)
                banks.consumed(6 + tk, "ve")
            ws.done(k)
        for oc in range(NCH, 32):
            k = ws.load(io["w_qk1"][oc])
            ws.use(k)
            for tk in range(2):
                bk = 2 + tk
                banks.pe_acquire(bk)
                for kc in range(NCH):
                    inst = t.matmul(pall(bk), wbuf[:, k, kc, :],
                                    xn3[:, kc, tk * TH:(tk + 1) * TH],
                                    start=(kc == 0), stop=(kc == NCH - 1))
                pr.inc("pe", inst)
                banks.pe_filled(bk)
                banks.consumer_wait(s, bk)
                inst = s.activation(bufB[:, oc - NCH, tk * TH:(tk + 1) * TH], pall(bk),
                                    AF.Identity, bias=bslot("b_qk1", oc), scale=RSCALE)
                pr.inc("ac", inst)
                banks.consumed(bk, "ac")
                pr.wait(s, "pe", sq_free[tk])
                inst = s.activation(sqb[:, tk, :], bufB[:, oc - NCH, tk * TH:(tk + 1) * TH],
                                    AF.Square)
                pr.inc("ac", inst)
                pr.wait(t, "ac")
                banks.pe_acquire(6 + tk)
                inst = t.matmul(pall(6 + tk)[0:1, :], onescol[:, :], sqb[:, tk, :],
                                start=True, stop=True)
                pr.inc("pe", inst)
                sq_free[tk] = pr.n["pe"]
                banks.pe_filled(6 + tk)
                banks.consumer_wait(v, 6 + tk)
                for b2 in range(2):
                    b = tk * 2 + b2
                    col = 128 + (oc - NCH) * 4 + b
                    inst = v.reduce_max(persist[0:1, col:col + 1],
                                        pall(6 + tk)[0:1, b2 * SLOC:(b2 + 1) * SLOC], AX.X)
                pr.inc("ve", inst)
                banks.consumed(6 + tk, "ve")
            ws.done(k)
        pr.barrier()

        vstage_free = [0, 0]
        for vs in range(4):
            k = wsv.load(io["w_v1"][:, :, vs * TH:(vs + 1) * TH])
            pr.wait(t, "gw", wsv.ready[k])
            for tm in range(8):
                bk = tm % 4
                banks.pe_acquire(bk)
                for kc in range(NCH):
                    inst = t.matmul(pall(bk), xn3[:, kc, tm * P:(tm + 1) * P],
                                    wsv.aps[k][:, kc, :], start=(kc == 0), stop=(kc == NCH - 1))
                pr.inc("pe", inst)
                banks.pe_filled(bk)
                banks.consumer_wait(v, bk)
                pr.wait(v, "sq", vstage_free[tm % 2])
                inst = v.tensor_copy(stage[:, tm % 2, :], pall(bk))
                pr.inc("ve", inst)
                banks.consumed(bk, "ve")
                pr.wait(sy, "ve")
                pr.dma("sq", io["v_contrib"][tm * P:(tm + 1) * P, vs * TH:(vs + 1) * TH],
                       stage[:, tm % 2, :])
                vstage_free[tm % 2] = pr.n["sq"]
            wsv.done(k)
        pr.barrier()
        if mode == "c":
            pr.dma("sq", io["o_mk2"], persist[0:1, 64:128])
            pr.dma("sq", io["o_mq2"], persist[0:1, 128:192])
            for c in range(NCH):
                pr.dma("sq", io["o_q"][c], bufB[:, c, :])
        h1_free = [0, 0]
        for ic in range(ICH):
            for gg in range(2):
                k = ws.load(io["w_gu1"][ic, gg])
                ws.use(k)
                for tk in range(2):
                    bk = gg * 2 + tk
                    banks.pe_acquire(bk)
                    for kc in range(NCH):
                        inst = t.matmul(pall(bk), wbuf[:, k, kc, :],
                                        xn3[:, kc, tk * TH:(tk + 1) * TH],
                                        start=(kc == 0), stop=(kc == NCH - 1))
                    pr.inc("pe", inst)
                    banks.pe_filled(bk)
                ws.done(k)
            for tk in range(2):
                gbk, ubk = tk, 2 + tk
                banks.consumer_wait(s, gbk)
                inst = s.activation(scr[:, 8 + tk, :], pall(gbk), AF.Silu)
                pr.inc("ac", inst)
                banks.consumed(gbk, "ac")
                banks.consumer_wait(v, ubk)
                pr.wait(v, "ac")
                pr.wait(v, "sq", h1_free[tk])
                inst = v.tensor_tensor(stage[:, tk, :], scr[:, 8 + tk, :],
                                       pall(ubk), ALU.mult)
                pr.inc("ve", inst)
                banks.consumed(ubk, "ve")
                pr.wait(sy, "ve")
                pr.dma("sq", io["h_dram"][ic, :, tk * TH:(tk + 1) * TH], stage[:, tk, :])
                h1_free[tk] = pr.n["sq"]
        pr.barrier()
        if fused:
            pr.wait(g, "sq")
            pr.dma("gw", io["mk2_cc_in"], persist[0:1, 64:128])
            pr.wait(g, "gw")
            for kind, op, iname, oname in (
                    ("AllGather", ALU.bypass, "k_contrib", "k_all"),
                    ("AllGather", ALU.bypass, "v_contrib", "v_all"),
                    ("AllReduce", ALU.max, "mk2_cc_in", "mk2_cc_out")):
                inst = g.collective_compute(
                    kind, op, replica_groups=[list(range(NCORES))],
                    ins=[io[iname].tensor.ap().opt()], outs=[io[oname].tensor.ap().opt()])
                pr.inc("cc", inst)
        if mode == "c":
            pr.wait(sy, "sq")

    if do("d"):
        flat = pool[:, 0:NCH * T]
        maskT = scr[:, 0:4, :].rearrange("p a t -> p (a t)").bitcast(BF16).rearrange(
            "p (c t) -> p c t", c=NCH)
        if mode == "d":
            for c in range(NCH):
                pr.dma("sq", bufB[:, c, :], io["q_in"][c])
            pr.dma("sq", scr[0:1, 4, :], io["mk2_allT"])
            pr.dma("sq", persist[0:1, 128:192], io["mq2_in"])
            pr.wait(v, "sq")
            inst = v.reduce_max(persist[0:1, 64:128],
                                scr[0:1, 4, :].rearrange("p (c r) -> p c r", r=NCORES),
                                AX.X)
            pr.inc("ve", inst)
        if fused:
            pr.wait(sy, "cc")
            if DEBUG_FUSED:
                pr.dma("sq", io["o_kall"], io["k_all"])
                pr.dma("sq", io["o_vall"], io["v_all"])
                pr.dma("sq", io["o_h1"], io["h_dram"])
            pr.dma("sq", persist[0:1, 64:128], io["mk2_cc_out"])
            pr.wait(v, "sq")
        if fused:
            pr.wait(g, "cc")
        pr.dma("gw", maskT.rearrange("p c t -> p (c t)"),
               io["masks"].rearrange("p c t -> p (c t)"))
        mg = persist[0:1, 192:256]
        v.tensor_tensor(mg, persist[0:1, 64:128], persist[0:1, 128:192], ALU.mult)
        inst = v.tensor_scalar(mg, mg, 1e-30, None, ALU.max)
        pr.inc("ve", inst)
        pr.wait(s, "ve")
        s.activation(mg, mg, AF.Ln)
        inst = s.activation(mg, mg, AF.Exp, scale=0.5)
        pr.inc("ac", inst)
        pr.wait(v, "ac")
        v.tensor_scalar(mg, mg, -1.0, None, ALU.mult)
        inst = v.tensor_copy(sqa[0:1, 0, 0:64], mg)
        pr.inc("ve", inst)
        pr.wait(t, "ve")
        banks.pe_acquire(6)
        inst = t.matmul(pall(6)[:, 0:64], ones1[:, :], sqa[0:1, 0, 0:64],
                        start=True, stop=True)
        pr.inc("pe", inst)
        banks.pe_filled(6)
        banks.consumer_wait(v, 6)
        negbc = persist[:, 0:64]
        inst = v.tensor_copy(negbc, pall(6)[:, 0:64])
        pr.inc("ve", inst)
        banks.consumed(6, "ve")
        if DEBUG_FUSED and fused:
            pr.wait(sy, "ve")
            pr.dma("sq", io["dbg"][:, 64:128], persist[:, 64:128])
            pr.dma("sq", io["dbg"][:, 128:192], persist[:, 128:192])
            pr.dma("sq", io["dbg"][:, 192:256], negbc)
        pr.wait(v, "gw")
        pr.barrier()

        kt = [flat[:, 0:4096].rearrange("p (c t) -> p c t", c=NCH),
              flat[:, 4096:8192].rearrange("p (c t) -> p c t", c=NCH)]
        vt = [flat[:, 8192:12288].rearrange("p (c t) -> p c t", c=NCH),
              flat[:, 12288:16384].rearrange("p (c t) -> p c t", c=NCH)]
        attT2 = [scr[:, 7:11, :].rearrange("p a t -> p (a t)").bitcast(BF16),
                 scr[:, 11:15, :].rearrange("p a t -> p (a t)").bitcast(BF16)]
        attTv = [a.rearrange("p (c t) -> p c t", c=NCH) for a in attT2]
        units = [(b, hg) for b in range(NB) for hg in range(8)]
        loaded = [(0, 0), (0, 0)]
        attdone = [0, 0]

        def prefetch(i):
            bb, hg = units[i]
            B = i % 2
            pr.wait(sy, "pe")
            pr.wait(s, "pe")
            for slot2 in range(8):
                for hh in range(2):
                    pr.dma("sq", kt[B][:, 2 * slot2:2 * slot2 + 2, hh * P:(hh + 1) * P],
                           io["k_all"][slot2 * D + hg * SLOC + hh * P:
                                       slot2 * D + hg * SLOC + (hh + 1) * P,
                                       bb * SLOC:(bb + 1) * SLOC].rearrange(
                               "p (kcc kk) -> p kcc kk", kcc=2))
                    pr.dma("sc", vt[B][:, 2 * slot2:2 * slot2 + 2, hh * P:(hh + 1) * P],
                           io["v_all"][slot2 * T + bb * SLOC:slot2 * T + (bb + 1) * SLOC,
                                       hg * SLOC + hh * P:hg * SLOC + (hh + 1) * P
                                       ].rearrange("(kcc p) dd -> p kcc dd", kcc=2))
            loaded[B] = (pr.n["sq"], pr.n["sc"])

        pend = None

        def finalize(pend):
            n, h, bb, avbank, ab, maskve = pend
            att = attTv[ab]
            banks.pe_acquire(5)
            pr.wait(t, "ve", maskve)
            for kc in range(NCH):
                inst = t.matmul(pall(5)[0:1, 0:SLOC], onescol[:, :], att[:, kc, :],
                                start=(kc == 0), stop=(kc == NCH - 1))
            pr.inc("pe", inst)
            banks.pe_filled(5)
            B = (n // 2) % 2
            hh = n % 2
            banks.pe_acquire(avbank)
            for kc in range(NCH):
                inst = t.matmul(pall(avbank)[:, 0:SLOC],
                                vt[B][:, kc, hh * P:(hh + 1) * P], att[:, kc, :],
                                start=(kc == 0), stop=(kc == NCH - 1))
            pr.inc("pe", inst)
            banks.pe_filled(avbank)
            attdone[ab] = pr.n["pe"]
            banks.consumer_wait(v, 5)
            sinv = scr[0:1, 6, 0:SLOC]
            inst = v.reciprocal(sinv, pall(5)[0:1, 0:SLOC])
            pr.inc("ve", inst)
            banks.consumed(5, "ve")
            inst = v.tensor_copy(sqa[0:1, 1, 0:SLOC], sinv)
            pr.inc("ve", inst)
            pr.wait(t, "ve")
            banks.pe_acquire(6)
            inst = t.matmul(pall(6)[:, 0:SLOC], ones1[:, :], sqa[0:1, 1, 0:SLOC],
                            start=True, stop=True)
            pr.inc("pe", inst)
            banks.pe_filled(6)
            banks.consumer_wait(v, 6)
            sbc = scr[:, 5, 0:SLOC]
            inst = v.tensor_copy(sbc, pall(6)[:, 0:SLOC])
            pr.inc("ve", inst)
            banks.consumed(6, "ve")
            banks.consumer_wait(v, avbank)
            tfy = scr[:, 4, 0:SLOC]
            v.tensor_tensor(tfy, pall(avbank)[:, 0:SLOC], sbc, ALU.mult)
            inst = v.tensor_scalar(bufA[:, h, bb * SLOC:(bb + 1) * SLOC], tfy,
                                   bslot("b_v1", h), None, ALU.add)
            pr.inc("ve", inst)
            banks.consumed(avbank, "ve")

        def emit_scores(i, hh):
            bb, hg = units[i]
            B = i % 2
            n = i * 2 + hh
            ab = n % 2
            h = hg * 2 + hh
            idx = h * 4 + bb
            q = bufB[:, h, bb * SLOC:(bb + 1) * SLOC]
            pr.wait(t, "sq", loaded[B][0])
            pr.wait(t, "sc", loaded[B][1])
            for grp in range(4):
                b0 = (grp % 2) * 2
                banks.pe_acquire(b0)
                banks.pe_acquire(b0 + 1)
                for ii in range(4):
                    kc = grp * 4 + ii
                    inst = t.matmul(
                        pall_t[:, b0 + ii // 2, (ii % 2) * SLOC:(ii % 2 + 1) * SLOC],
                        kt[B][:, kc, hh * P:(hh + 1) * P], q, start=True, stop=True)
                pr.inc("pe", inst)
                banks.pe_filled(b0)
                banks.pe_filled(b0 + 1)
                banks.consumer_wait(s, b0)
                pr.wait(s, "pe", attdone[ab])
                inst = s.activation(
                    attTv[ab][:, grp * 4:(grp + 1) * 4, :].rearrange("p c t -> p (c t)"),
                    pall_t[:, b0:b0 + 2, :].rearrange("p c t -> p (c t)"),
                    AF.Exp, bias=negbc[:, idx:idx + 1])
                pr.inc("ac", inst)
                banks.consumed(b0, "ac")
                banks.consumed(b0 + 1, "ac")
                pr.wait(v, "ac")
                av_ = attTv[ab][:, grp * 4:(grp + 1) * 4, :].rearrange("p c t -> p (c t)")
                mv_ = maskT[:, grp * 4:(grp + 1) * 4, :].rearrange("p c t -> p (c t)")
                inst = v.tensor_tensor(av_, av_, mv_, ALU.mult)
                pr.inc("ve", inst)
            return (n, h, bb, 4 if n % 2 == 0 else 7, ab, pr.n["ve"])

        prefetch(0)
        for i in range(len(units)):
            pend_new = emit_scores(i, 0)
            if pend is not None:
                finalize(pend)
            pend = pend_new
            if i + 1 < len(units):
                prefetch(i + 1)
            pend_new = emit_scores(i, 1)
            finalize(pend)
            pend = pend_new
        finalize(pend)
        pr.barrier()
        if DEBUG_FUSED and fused:
            g.dma_start(out=io["dbg"][:, 256:512], in_=bufA[:, 0, 0:256]).then_inc(pr.h["gw"], 16)
            pr.n["gw"] += 16
            for c in range(NCH):
                g.dma_start(out=io["o_y1"][c], in_=bufA[:, c, :]).then_inc(pr.h["gw"], 16)
                pr.n["gw"] += 16
            pr.wait(g, "gw")

        for oc in range(NCH):
            def evacp(tk, bk, oc=oc):
                banks.consumer_wait(v, bk)
                xs = x_sb[:, oc, tk * TH:(tk + 1) * TH]
                inst = v.scalar_tensor_tensor(xs, pall(bk), bslot("b_proj1", oc), xs,
                                              ALU.add, ALU.add)
                pr.inc("ve", inst)
                banks.consumed(bk, "ve")
            mm_tile(io["w_proj1"][oc],
                    lambda kc, tk: bufA[:, kc, tk * TH:(tk + 1) * TH], 2,
                    evacp, lambda tk, oc=oc: (oc * 2 + tk) % 4)
        pr.barrier()

        wstage = [scr[:, 0:4, :].rearrange("p a t -> p (a t)"),
                  scr[:, 4:8, :].rearrange("p a t -> p (a t)")]
        wflat = [wbuf[:, 0, :, :].rearrange("p c m -> p (c m)")[:, 0:2048],
                 wbuf[:, 1, :, :].rearrange("p c m -> p (c m)")[:, 0:2048]]
        st_cast = [0, 0]
        st_ready = [0, 0]
        bf_ready = [0, 0]
        bf_freed = [0, 0]

        def vc_load(i, src_ap):
            k = i % 2
            if st_cast[k]:
                pr.wait(sy, "ve", st_cast[k])
            pr.dma("sq", wstage[k].rearrange("p (c m) -> p c m", c=NCH), src_ap)
            st_ready[k] = pr.n["sq"]
            return k

        def vc_cast(k):
            pr.wait(v, "sq", st_ready[k])
            if bf_freed[k]:
                pr.wait(v, "pe", bf_freed[k])
            inst = v.tensor_copy(wflat[k], wstage[k])
            pr.inc("ve", inst)
            st_cast[k] = pr.n["ve"]
            bf_ready[k] = pr.n["ve"]

        vci = 0
        for th in range(2):
            for grp in range(8):
                pr.dma("sq", hh_view[:, grp * 8:(grp + 1) * 8, :],
                       io["h_dram"].rearrange("k p t -> p k t")[
                           :, grp * 8:(grp + 1) * 8, th * TH:(th + 1) * TH])
            pr.wait(t, "sq")
            for oc in range(NCH):
                bk = oc % 4
                banks.pe_acquire(bk)
                first = True
                for qtr in range(4):
                    k = vc_load(vci, io["w_down1"][oc, :, qtr * 16:(qtr + 1) * 16, :])
                    vci += 1
                    vc_cast(k)
                    pr.wait(t, "ve", bf_ready[k])
                    for kk in range(NCH):
                        inst = t.matmul(pall(bk), wbuf[:, k, kk, :],
                                        hh_view[:, qtr * 16 + kk, :],
                                        start=first, stop=(qtr == 3 and kk == NCH - 1))
                        first = False
                    pr.inc("pe", inst)
                    bf_freed[k] = pr.n["pe"]
                banks.pe_filled(bk)
                banks.consumer_wait(v, bk)
                xs = x_sb[:, oc, th * TH:(th + 1) * TH]
                inst = v.tensor_tensor(xs, xs, pall(bk), ALU.add)
                pr.inc("ve", inst)
                banks.consumed(bk, "ve")
            pr.barrier()
        for c in range(NCH):
            pr.dma("sq", io["o_x"][c], x_sb[:, c, :])
        pr.wait(sy, "sq")

    nc.all_engine_barrier()
    stack.close()
    return nc



def kernel(**inputs):
    X = np.asarray(inputs["L0_attn_X"])
    W = prep_weights(inputs)
    cores = [prep_core(c, X) for c in range(NCORES)]
    core_ids = list(range(NCORES))

    if MODE == "fused":
        nc = build("fused")
        in_maps = []
        for c in range(NCORES):
            m = {"x_in": cores[c]["x_in"], "cos0": cores[c]["cos0"],
                 "sin0": cores[c]["sin0"], "masks": cores[c]["masks"]}
            for k in ("w_qkv0", "b_qkv0", "gb0", "w_out0", "b_out0", "w_gu0", "w_down0",
                      "w_qk1", "b_qk1", "w_v1", "b_v1", "w_proj1", "b_proj1",
                      "w_gu1", "w_down1"):
                m[k] = W[k]
            in_maps.append(m)
        res = run_bass_kernel_spmd(nc, in_maps, core_ids=core_ids)
        outs = [r["o_x"] for r in res.results]
    else:
        nc = build("a")
        in_maps = [{"x_in": cores[c]["x_in"], "w_qkv0": W["w_qkv0"],
                    "b_qkv0": W["b_qkv0"], "gb0": W["gb0"],
                    "cos0": cores[c]["cos0"], "sin0": cores[c]["sin0"]}
                   for c in range(NCORES)]
        ra = run_bass_kernel_spmd(nc, in_maps, core_ids=core_ids).results
        a_all = np.ascontiguousarray(np.stack([ra[c]["o_a"] for c in range(NCORES)]))

        nc = build("b")
        in_maps = [{"x_in": ra[c]["o_x"], "xn_in": np.asarray(ra[c]["o_xn"], BF),
                    "q_in": np.asarray(ra[c]["o_q"], BF), "a_all": a_all,
                    "w_out0": W["w_out0"], "b_out0": W["b_out0"],
                    "w_gu0": W["w_gu0"], "w_down0": W["w_down0"]}
                   for c in range(NCORES)]
        rb = run_bass_kernel_spmd(nc, in_maps, core_ids=core_ids).results

        nc = build("c")
        in_maps = [{"x_in": rb[c]["o_x"], "w_qk1": W["w_qk1"], "b_qk1": W["b_qk1"],
                    "w_v1": W["w_v1"], "w_gu1": W["w_gu1"]}
                   for c in range(NCORES)]
        rc = run_bass_kernel_spmd(nc, in_maps, core_ids=core_ids).results
        k_all = np.ascontiguousarray(
            np.concatenate([np.asarray(rc[c]["k_contrib"], BF) for c in range(NCORES)], 0))
        v_all = np.ascontiguousarray(
            np.concatenate([np.asarray(rc[c]["v_contrib"], BF) for c in range(NCORES)], 0))
        mk2_allT = np.ascontiguousarray(
            np.stack([rc[c]["o_mk2"][0] for c in range(NCORES)]).T)

        nc = build("d")
        in_maps = [{"x_in": rb[c]["o_x"], "q_in": np.asarray(rc[c]["o_q"], BF),
                    "k_all": k_all, "v_all": v_all, "mk2_allT": mk2_allT,
                    "mq2_in": rc[c]["o_mq2"], "h_dram": np.asarray(rc[c]["h_dram"], BF),
                    "masks": cores[c]["masks"], "b_v1": W["b_v1"],
                    "w_proj1": W["w_proj1"], "b_proj1": W["b_proj1"],
                    "w_down1": W["w_down1"]}
                   for c in range(NCORES)]
        rd = run_bass_kernel_spmd(nc, in_maps, core_ids=core_ids).results
        outs = [rd[c]["o_x"] for c in range(NCORES)]

    full = np.empty((NB, SEQ, D), np.float32)
    for c in range(NCORES):
        o = np.asarray(outs[c], np.float32).reshape(NCH, P, NB, SLOC)
        full[:, SLOC * c:SLOC * (c + 1), :] = o.transpose(2, 3, 0, 1).reshape(NB, SLOC, D)
    return full


# version 09
# speedup vs baseline: 1.1626x; 1.0001x over previous
import numpy as np
import contextlib
import ml_dtypes
import concourse.bass as bass
import concourse.mybir as mybir
from concourse.bass_utils import run_bass_kernel_spmd

F32 = mybir.dt.float32
BF16 = mybir.dt.bfloat16
AF = mybir.ActivationFunctionType
ALU = mybir.AluOpType
AX = mybir.AxisListType

P = 128
NCORES = 8
D = 2048
NB = 4
SEQ = 2048
SLOC = SEQ // NCORES
T = NB * SLOC
TH = T // 2
NCH = D // P
IFF = 8192
ICH = IFF // P
EPS = 1e-5
RSCALE = float(1.0 / np.sqrt(128.0))
BF = ml_dtypes.bfloat16

MODE = "fused"
DEBUG_FUSED = False



def _perm_rot():
    return np.concatenate([np.arange(0, 1024, 2), np.arange(1, 1024, 2),
                           np.arange(1024, 2048)])


def _perm_24():
    q = np.arange(IFF)
    return (q % 2048) * 4 + q // 2048


def _tiles(w, row_idx, col_idx):
    wp = np.asarray(w)[np.ix_(row_idx, col_idx)]
    nk = len(row_idx) // P
    no = len(col_idx) // P
    return np.ascontiguousarray(
        wp.reshape(nk, P, no, P).transpose(2, 1, 0, 3)).astype(np.float32)


def prep_weights(ins):
    pr_ = _perm_rot()
    p24 = _perm_24()
    rows = np.arange(D)
    c = {}
    W_in = np.asarray(ins["L0_attn_W_in"])
    b_in = np.asarray(ins["L0_attn_b_in"])
    colperm = np.concatenate([2048 + pr_, 4096 + pr_, 0 + pr_])
    c["w_qkv0"] = _tiles(W_in, rows, colperm)
    c["b_qkv0"] = np.ascontiguousarray(b_in[colperm].reshape(48, P).T).astype(np.float32)
    c["gb0"] = np.ascontiguousarray(np.concatenate(
        [np.asarray(ins["L0_attn_gamma"]).reshape(NCH, P).T,
         np.asarray(ins["L0_attn_beta"]).reshape(NCH, P).T], axis=1)).astype(np.float32)
    c["w_out0"] = _tiles(np.asarray(ins["L0_attn_W_out"]), pr_, rows)
    c["b_out0"] = np.ascontiguousarray(
        np.asarray(ins["L0_attn_b_out"]).reshape(NCH, P).T).astype(np.float32)
    wg = np.asarray(ins["L0_ffn_w_gate"])
    wu = np.asarray(ins["L0_ffn_w_up"])
    tg = _tiles(wg, rows, p24)
    tu = _tiles(wu, rows, p24)
    gu = np.empty((ICH, 2, P, NCH, P), np.float32)
    slot = 0
    for c0 in range(16):
        for j in range(4):
            ic = j * 16 + c0
            gu[slot, 0] = tg[ic]
            gu[slot, 1] = tu[ic]
            slot += 1
    c["w_gu0"] = np.ascontiguousarray(gu)
    c["w_down0"] = _tiles(np.asarray(ins["L0_ffn_w_down"]), p24, rows)
    wa = np.asarray(ins["L1_attn_w_attn"])
    ba = np.asarray(ins["L1_attn_b_attn"])
    ccols = np.concatenate([2048 + rows, 0 + rows])
    c["w_qk1"] = _tiles(wa, rows, ccols)
    c["b_qk1"] = np.ascontiguousarray(
        np.concatenate([ba[2048:4096].reshape(NCH, P).T,
                        ba[0:2048].reshape(NCH, P).T], axis=1)).astype(np.float32)
    c["w_v1"] = np.ascontiguousarray(
        wa[:, 4096:6144].reshape(NCH, P, D).transpose(1, 0, 2)).astype(np.float32)
    c["b_v1"] = np.ascontiguousarray(
        ba[4096:6144].reshape(NCH, P).T).astype(np.float32)
    c["w_proj1"] = _tiles(np.asarray(ins["L1_attn_w_proj"]), rows, rows)
    c["b_proj1"] = np.ascontiguousarray(
        np.asarray(ins["L1_attn_b_proj"]).reshape(NCH, P).T).astype(np.float32)
    wgu = np.asarray(ins["L1_ffn_w_gate_up"])
    gu1 = np.empty((ICH, 2, P, NCH, P), np.float32)
    for g in range(2):
        tt = _tiles(wgu[g * IFF:(g + 1) * IFF, :].T, rows, np.arange(IFF))
        for ic in range(ICH):
            gu1[ic, g] = tt[ic]
    c["w_gu1"] = np.ascontiguousarray(gu1)
    c["w_down1"] = _tiles(np.asarray(ins["L1_ffn_w_down"]).T, np.arange(IFF), rows)
    return c


def prep_core(cid, X):
    d = {}
    Xc = np.asarray(X)[:, SLOC * cid:SLOC * (cid + 1), :]
    d["x_in"] = np.ascontiguousarray(
        Xc.transpose(2, 0, 1).reshape(NCH, P, T)).astype(np.float32)
    pos = (SLOC * cid + np.arange(SLOC)).astype(np.float64)
    inv = 1.0 / (10000.0 ** (np.arange(0, 1024, 2, dtype=np.float64) / 1024.0))
    fr = pos[:, None] * inv[None, :]
    cos = np.cos(fr).T.reshape(4, P, SLOC).transpose(1, 0, 2)
    sin = np.sin(fr).T.reshape(4, P, SLOC).transpose(1, 0, 2)
    d["cos0"] = np.ascontiguousarray(np.tile(cos, (1, 1, 2))).astype(np.float32)
    d["sin0"] = np.ascontiguousarray(np.tile(sin, (1, 1, 2))).astype(np.float32)
    kg = np.arange(SEQ).reshape(NCH, P)
    qg = SLOC * cid + np.arange(SLOC)
    m = (kg[:, :, None] <= qg[None, None, :]).astype(np.float32)
    d["masks"] = np.ascontiguousarray(m.transpose(1, 0, 2))
    return d



class Prog:
    def __init__(self, nc, stack):
        self.nc = nc
        self.g, self.t, self.v, self.s, self.sy = (
            nc.gpsimd, nc.tensor, nc.vector, nc.scalar, nc.sync)
        self.h = {}
        self.n = {}
        for name in ("gw", "sq", "sc", "pe", "ac", "ve", "cc"):
            self.h[name] = stack.enter_context(nc.semaphore(name))
            self.n[name] = 0

    def inc(self, name, inst, k=1):
        inst.then_inc(self.h[name], k)
        self.n[name] += k
        return self.n[name]

    def wait(self, eng, name, val=None):
        eng.wait_ge(self.h[name], self.n[name] if val is None else val)

    def dma(self, q, out, in_, **kw):
        eng = {"gw": self.g, "sq": self.sy, "sc": self.s}[q]
        inst = eng.dma_start(out=out, in_=in_, **kw)
        return self.inc(q, inst, 16)

    def barrier(self):
        self.nc.all_engine_barrier()


class WStream:
    def __init__(self, pr, slot_aps):
        self.pr = pr
        self.aps = slot_aps
        self.freed = [0] * len(slot_aps)
        self.ready = [0] * len(slot_aps)
        self.i = 0

    def load(self, src):
        k = self.i % len(self.aps)
        if self.freed[k]:
            self.pr.wait(self.pr.g, "pe", self.freed[k])
        self.pr.dma("gw", self.aps[k], src)
        self.ready[k] = self.pr.n["gw"]
        self.i += 1
        return k

    def use(self, k):
        self.pr.wait(self.pr.t, "gw", self.ready[k])

    def done(self, k):
        self.freed[k] = self.pr.n["pe"]


class Banks:
    def __init__(self, pr):
        self.pr = pr
        self.mark = [None] * 8
        self.fill = [None] * 8

    def pe_acquire(self, idx):
        if self.mark[idx] is not None:
            self.pr.wait(self.pr.t, *self.mark[idx])

    def pe_filled(self, idx):
        self.fill[idx] = ("pe", self.pr.n["pe"])

    def consumer_wait(self, eng, idx):
        self.pr.wait(eng, *self.fill[idx])

    def consumed(self, idx, sem):
        self.mark[idx] = (sem, self.pr.n[sem])


def build(mode):
    nc = bass.Bass("TRN2", target_bir_lowering=False, debug=False)
    stack = contextlib.ExitStack()
    pr = Prog(nc, stack)
    g, t, v, s, sy = pr.g, pr.t, pr.v, pr.s, pr.sy
    fused = mode == "fused"
    do = lambda *phases: fused or mode in phases

    def dram_in(name, shape, dt=F32):
        return nc.dram_tensor(name, shape, dt, kind="ExternalInput").ap()

    def dram_out(name, shape, dt=F32):
        return nc.dram_tensor(name, shape, dt, kind="ExternalOutput").ap()

    def dram_int(name, shape, dt=F32):
        return nc.dram_tensor(name, shape, dt).ap()

    io = {}
    io["x_in"] = dram_in("x_in", [NCH, P, T])
    if do("a"):
        io["w_qkv0"] = dram_in("w_qkv0", [48, P, NCH, P])
        io["b_qkv0"] = dram_in("b_qkv0", [P, 48])
        io["gb0"] = dram_in("gb0", [P, 32])
        io["cos0"] = dram_in("cos0", [P, 4, TH])
        io["sin0"] = dram_in("sin0", [P, 4, TH])
    if do("b"):
        io["w_out0"] = dram_in("w_out0", [NCH, P, NCH, P])
        io["b_out0"] = dram_in("b_out0", [P, NCH])
        io["w_gu0"] = dram_in("w_gu0", [ICH, 2, P, NCH, P])
        io["w_down0"] = dram_in("w_down0", [NCH, P, ICH, P])
    if do("c"):
        io["w_qk1"] = dram_in("w_qk1", [32, P, NCH, P])
        io["b_qk1"] = dram_in("b_qk1", [P, 32])
        io["w_v1"] = dram_in("w_v1", [P, NCH, D])
        io["w_gu1"] = dram_in("w_gu1", [ICH, 2, P, NCH, P])
    if do("d"):
        io["masks"] = dram_in("masks", [P, NCH, SLOC])
        io["b_v1"] = dram_in("b_v1", [P, NCH])
        io["w_proj1"] = dram_in("w_proj1", [NCH, P, NCH, P])
        io["b_proj1"] = dram_in("b_proj1", [P, NCH])
        io["w_down1"] = dram_in("w_down1", [NCH, P, ICH, P])

    if fused:
        io["a_cc_in"] = dram_int("a_cc_in", [P, 64])
        io["a_cc_out"] = dram_int("a_cc_out", [P, 64])
        io["mk2_cc_in"] = dram_int("mk2_cc_in", [1, 64])
        io["mk2_cc_out"] = dram_int("mk2_cc_out", [1, 64])
        io["k_contrib"] = dram_int("k_contrib", [D, T], BF16)
        io["v_contrib"] = dram_int("v_contrib", [T, D], BF16)
        io["k_all"] = dram_int("k_all", [NCORES * D, T], BF16)
        io["v_all"] = dram_int("v_all", [NCORES * T, D], BF16)
        io["h_dram"] = dram_int("h_dram", [ICH, P, T], BF16)
        io["o_x"] = dram_out("o_x", [NCH, P, T])
        if DEBUG_FUSED:
            io["dbg"] = dram_out("dbg", [P, 512])
            io["o_x2"] = dram_out("o_x2", [NCH, P, T])
            io["o_y1"] = dram_out("o_y1", [NCH, P, T])
            io["o_xn1"] = dram_out("o_xn1", [NCH, P, T])
            io["o_kall"] = dram_out("o_kall", [NCORES * D, T], BF16)
            io["o_vall"] = dram_out("o_vall", [NCORES * T, D], BF16)
            io["o_h1"] = dram_out("o_h1", [ICH, P, T], BF16)
    else:
        if mode == "a":
            io["o_x"] = dram_out("o_x", [NCH, P, T])
            io["o_xn"] = dram_out("o_xn", [NCH, P, T], BF16)
            io["o_q"] = dram_out("o_q", [NCH, P, T], BF16)
            io["o_a"] = dram_out("o_a", [P, 64])
        if mode == "b":
            io["xn_in"] = dram_in("xn_in", [NCH, P, T], BF16)
            io["q_in"] = dram_in("q_in", [NCH, P, T], BF16)
            io["a_all"] = dram_in("a_all", [NCORES, P, 64])
            io["h_dram"] = dram_int("h_dram", [ICH, P, T], BF16)
            io["o_x"] = dram_out("o_x", [NCH, P, T])
        if mode == "c":
            io["o_q"] = dram_out("o_q", [NCH, P, T], BF16)
            io["k_contrib"] = dram_out("k_contrib", [D, T], BF16)
            io["v_contrib"] = dram_out("v_contrib", [T, D], BF16)
            io["o_mk2"] = dram_out("o_mk2", [1, 64])
            io["o_mq2"] = dram_out("o_mq2", [1, 64])
            io["h_dram"] = dram_out("h_dram", [ICH, P, T], BF16)
        if mode == "d":
            io["q_in"] = dram_in("q_in", [NCH, P, T], BF16)
            io["k_all"] = dram_in("k_all", [NCORES * D, T], BF16)
            io["v_all"] = dram_in("v_all", [NCORES * T, D], BF16)
            io["mk2_allT"] = dram_in("mk2_allT", [64, NCORES])
            io["mq2_in"] = dram_in("mq2_in", [1, 64])
            io["h_dram"] = dram_in("h_dram", [ICH, P, T], BF16)
            io["o_x"] = dram_out("o_x", [NCH, P, T])

    ksr = bass.get_kernel_semaphore_range()
    nmono = len(nc._monotonic_sems) if hasattr(nc, "_monotonic_sems") else 0
    base = ksr.start + 2 + (1 if nc._bir_kernel_barrier_sem is not None else 0) + nmono
    clear_range = range(base, ksr.stop)
    g.dma_reset(clear_range)
    g.sem_clear(clear_range)
    nc.all_engine_barrier()

    x_sb = stack.enter_context(nc.sbuf_tensor("x_sb", [P, NCH, T], F32))
    pool = stack.enter_context(nc.sbuf_tensor("pool", [P, 3 * NCH * T], BF16))
    xn3 = pool[:, 0:NCH * T].rearrange("p (c t) -> p c t", c=NCH)
    bufA = pool[:, NCH * T:2 * NCH * T].rearrange("p (c t) -> p c t", c=NCH)
    bufB = pool[:, 2 * NCH * T:3 * NCH * T].rearrange("p (c t) -> p c t", c=NCH)
    hh_view = pool[:, 0:ICH * TH].rearrange("p (c t) -> p c t", c=ICH)
    wbuf = stack.enter_context(nc.sbuf_tensor("wbuf", [P, 2, NCH, P], BF16))
    scr = stack.enter_context(nc.sbuf_tensor("scr", [P, 15, TH], F32))
    persist = stack.enter_context(nc.sbuf_tensor("persist", [P, 256], F32))
    sqa = stack.enter_context(nc.sbuf_tensor("sqa", [P, 2, TH], BF16))
    sqb = stack.enter_context(nc.sbuf_tensor("sqb", [P, 2, TH], BF16))
    stage = stack.enter_context(nc.sbuf_tensor("stage", [P, 2, TH], BF16))
    bias_sb = stack.enter_context(nc.sbuf_tensor("bias_sb", [P, 160], F32))
    ones1 = stack.enter_context(nc.sbuf_tensor("ones1", [1, P], BF16))
    onescol = stack.enter_context(nc.sbuf_tensor("onescol", [P, 1], BF16))
    pall_t = stack.enter_context(nc.psum_tensor("pall", [P, 8, 512], F32))
    pall = lambda k: pall_t[:, k, :]
    banks = Banks(pr)

    g.memset(ones1[:, :], 1.0)
    inst = g.memset(onescol[:, :], 1.0)
    pr.inc("cc", inst)
    ws = WStream(pr, [wbuf[:, 0, :, :], wbuf[:, 1, :, :]])
    wsv = WStream(pr, [bufA[:, :, 0:512], bufA[:, :, 512:1024]])

    BOFF = {}
    boff = [0]

    def bias_load(name, width):
        BOFF[name] = boff[0]
        pr.dma("sq", bias_sb[:, boff[0]:boff[0] + width], io[name])
        boff[0] += width

    def bslot(name, i):
        return bias_sb[:, BOFF[name] + i:BOFF[name] + i + 1]

    for c in range(NCH):
        pr.dma("sq", x_sb[:, c, :], io["x_in"][c])
    if do("a"):
        bias_load("b_qkv0", 48)
        bias_load("gb0", 32)
    if do("b"):
        bias_load("b_out0", NCH)
    if do("c"):
        bias_load("b_qk1", 32)
    if do("d"):
        bias_load("b_v1", NCH)
        bias_load("b_proj1", NCH)
    for eng in (v, s, t):
        pr.wait(eng, "sq")
        pr.wait(eng, "cc")

    sqa_flat = sqa[:, :, :].rearrange("p a t -> p (a t)")

    def ln_stats(kind, mean_row, rstd_row):
        for bk in (4, 5, 6, 7):
            banks.pe_acquire(bk)
        mm_done = 0
        for c in range(NCH):
            if kind == "x":
                pr.wait(v, "pe", mm_done)
                inst = v.tensor_copy(sqa_flat, x_sb[:, c, :])
                pr.inc("ve", inst)
                xb = sqa_flat
            else:
                xb = xn3[:, c, :]
            pr.wait(s, "pe", mm_done)
            if kind == "x":
                pr.wait(s, "ve")
            s.activation(sqb[:, 0, :], xb[:, 0:TH], AF.Square)
            inst = s.activation(sqb[:, 1, :], xb[:, TH:T], AF.Square)
            pr.inc("ac", inst)
            pr.wait(t, "ac")
            if kind == "x":
                pr.wait(t, "ve")
            for hf in range(2):
                t.matmul(pall(4 + hf)[0:1, :], onescol[:, :], xb[:, hf * TH:(hf + 1) * TH],
                         start=(c == 0), stop=(c == NCH - 1))
                inst = t.matmul(pall(6 + hf)[0:1, :], onescol[:, :], sqb[:, hf, :],
                                start=(c == 0), stop=(c == NCH - 1))
            pr.inc("pe", inst)
            mm_done = pr.n["pe"]
        pr.wait(s, "pe")
        for hf in range(2):
            s.activation(mean_row[0:1, hf * TH:(hf + 1) * TH], pall(4 + hf)[0:1, :],
                         AF.Identity, scale=1.0 / D)
            inst = s.activation(rstd_row[0:1, hf * TH:(hf + 1) * TH], pall(6 + hf)[0:1, :],
                                AF.Identity, scale=1.0 / D)
        pr.inc("ac", inst)
        for bk in (4, 5, 6, 7):
            banks.consumed(bk, "ac")
        pr.wait(v, "ac")
        vv = scr[0:1, 14, 0:TH]
        for hf in range(2):
            mr = mean_row[0:1, hf * TH:(hf + 1) * TH]
            rr = rstd_row[0:1, hf * TH:(hf + 1) * TH]
            v.tensor_tensor(vv, mr, mr, ALU.mult)
            v.tensor_tensor(rr, rr, vv, ALU.subtract)
            inst = v.tensor_scalar(rr, rr, EPS, None, ALU.add)
        pr.inc("ve", inst)
        pr.wait(s, "ve")
        s.activation(rstd_row, rstd_row, AF.Ln)
        inst = s.activation(rstd_row, rstd_row, AF.Exp, scale=-0.5)
        pr.inc("ac", inst)
        pr.wait(v, "ac")

    def bcast_row_f32(row_f32, width, out_slot, bank=3):
        rb = sqa[0:1, 0, 0:width]
        inst = v.tensor_copy(rb, row_f32)
        pr.inc("ve", inst)
        pr.wait(t, "ve")
        banks.pe_acquire(bank)
        inst = t.matmul(pall(bank)[:, 0:width], ones1[:, :], rb, start=True, stop=True)
        pr.inc("pe", inst)
        banks.pe_filled(bank)
        banks.consumer_wait(v, bank)
        inst = v.tensor_copy(out_slot, pall(bank)[:, 0:width])
        pr.inc("ve", inst)
        banks.consumed(bank, "ve")

    def mm_tile(oc_src, rhs_getter, n_tk, evac_fn, bank_of):
        k = ws.load(oc_src)
        ws.use(k)
        for tk in range(n_tk):
            bk = bank_of(tk)
            banks.pe_acquire(bk)
            for kc in range(NCH):
                inst = t.matmul(pall(bk), wbuf[:, k, kc, :], rhs_getter(kc, tk),
                                start=(kc == 0), stop=(kc == NCH - 1))
            pr.inc("pe", inst)
            banks.pe_filled(bk)
            evac_fn(tk, bk)
        ws.done(k)

    mean_bc = scr[:, 0:2, :].rearrange("p a t -> p (a t)")
    rstd_bc = scr[:, 2:4, :].rearrange("p a t -> p (a t)")
    tf2 = scr[:, 4:6, :].rearrange("p a t -> p (a t)")
    rowA = scr[0:1, 12:14, :].rearrange("p a t -> p (a t)")
    rowB = scr[0:1, 10:12, :].rearrange("p a t -> p (a t)")

    def ln_apply_bcasts(mean_row, rstd_row):
        for hf in range(2):
            bcast_row_f32(mean_row[0:1, hf * TH:(hf + 1) * TH], TH, scr[:, 0 + hf, :])
            bcast_row_f32(rstd_row[0:1, hf * TH:(hf + 1) * TH], TH, scr[:, 2 + hf, :])

    if do("a"):
        ln_stats("x", rowA, rowB)
        ln_apply_bcasts(rowA, rowB)
        for c in range(NCH):
            v.tensor_tensor(tf2, x_sb[:, c, :], mean_bc, ALU.subtract)
            v.tensor_tensor(tf2, tf2, rstd_bc, ALU.mult)
            v.tensor_copy(xn3[:, c, :], tf2)
            inst = v.tensor_tensor(x_sb[:, c, :], x_sb[:, c, :], tf2, ALU.add)
        pr.inc("ve", inst)
        pr.barrier()

        ln_stats("xn", rowA, rowB)
        ln_apply_bcasts(rowA, rowB)
        for c in range(NCH):
            v.tensor_tensor(tf2, xn3[:, c, :], mean_bc, ALU.subtract)
            v.tensor_tensor(tf2, tf2, rstd_bc, ALU.mult)
            inst = v.tensor_scalar(bufA[:, c, :], tf2, bslot("gb0", c),
                                   bslot("gb0", 16 + c), ALU.mult, ALU.add)
        pr.inc("ve", inst)
        pr.barrier()

        for rc in range(4):
            pr.dma("sq", scr[:, 0 + rc, :], io["cos0"][:, rc, :])
            pr.dma("sq", scr[:, 4 + rc, :], io["sin0"][:, rc, :])
        pr.wait(v, "sq")

        for hf in range(2):
            cols = slice(hf * TH, (hf + 1) * TH)
            for oc in range(32):
                dst = bufB[:, oc % 16, 0:TH] if oc < 16 else bufB[:, oc % 16, TH:T]

                def evac(tk, bk, oc=oc, dst=dst):
                    banks.consumer_wait(s, bk)
                    inst = s.activation(dst, pall(bk), AF.Identity, bias=bslot("b_qkv0", oc))
                    pr.inc("ac", inst)
                    banks.consumed(bk, "ac")
                mm_tile(io["w_qkv0"][oc], lambda kc, tk: bufA[:, kc, cols], 1,
                        evac, lambda tk, oc=oc: oc % 4)
            pr.wait(v, "ac")
            sE, sO, sT = scr[:, 8, :], scr[:, 9, :], scr[:, 10, :]
            for part in (slice(0, TH), slice(TH, T)):
                for rc in range(4):
                    Ke = bufB[:, rc, part]
                    Ko = bufB[:, rc + 4, part]
                    cosr = scr[:, 0 + rc, :]
                    sinr = scr[:, 4 + rc, :]
                    v.tensor_tensor(sE, Ke, cosr, ALU.mult)
                    v.tensor_tensor(sT, Ko, sinr, ALU.mult)
                    v.tensor_tensor(sE, sE, sT, ALU.subtract)
                    v.tensor_tensor(sO, Ko, cosr, ALU.mult)
                    v.tensor_tensor(sT, Ke, sinr, ALU.mult)
                    v.tensor_tensor(sO, sO, sT, ALU.add)
                    v.tensor_copy(Ke, sE)
                    inst = v.tensor_copy(Ko, sO)
            pr.inc("ve", inst)
            for ofs, bank in ((0, 4), (TH, 5)):
                banks.pe_acquire(bank)
                mmv = [0, 0]
                for c in range(NCH):
                    pr.wait(s, "ve")
                    pr.wait(s, "pe", mmv[c % 2])
                    inst = s.activation(sqb[:, c % 2, :], bufB[:, c, ofs:ofs + TH], AF.Square)
                    pr.inc("ac", inst)
                    pr.wait(t, "ac")
                    inst = t.matmul(pall(bank)[0:1, :], onescol[:, :], sqb[:, c % 2, :],
                                    start=(c == 0), stop=(c == NCH - 1))
                    pr.inc("pe", inst)
                    mmv[c % 2] = pr.n["pe"]
                banks.pe_filled(bank)
            nk2 = scr[0:1, 12, :]
            nv2 = scr[0:1, 13, :]
            banks.consumer_wait(v, 4)
            banks.consumer_wait(v, 5)
            v.tensor_copy(nk2, pall(4)[0:1, :])
            inst = v.tensor_copy(nv2, pall(5)[0:1, :])
            pr.inc("ve", inst)
            banks.consumed(4, "ve")
            banks.consumed(5, "ve")
            inst = v.tensor_tensor(nk2, nk2, nv2, ALU.mult)
            pr.inc("ve", inst)
            pr.wait(s, "ve")
            s.activation(nk2, nk2, AF.Ln)
            inst = s.activation(nk2, nk2, AF.Exp, scale=-0.5)
            pr.inc("ac", inst)
            pr.wait(v, "ac")
            bcast_row_f32(nk2, TH, scr[:, 11, :])
            pP = scr[:, 8, :]
            for c in range(NCH):
                v.tensor_tensor(pP, bufB[:, c, 0:TH], bufB[:, c, TH:T], ALU.mult)
                v.tensor_tensor(pP, pP, scr[:, 11, :], ALU.mult)
                for b2 in range(2):
                    b = hf * 2 + b2
                    inst = v.reduce_sum(persist[:, c * 4 + b:c * 4 + b + 1],
                                        pP[:, b2 * SLOC:(b2 + 1) * SLOC], AX.X)
            pr.inc("ve", inst)
            pr.barrier()

        if mode == "a":
            pr.dma("sq", io["o_a"], persist[:, 0:64])

        for oc in range(32, 48):
            def evacq(tk, bk, oc=oc):
                banks.consumer_wait(s, bk)
                inst = s.activation(bufB[:, oc - 32, tk * TH:(tk + 1) * TH], pall(bk),
                                    AF.Identity, bias=bslot("b_qkv0", oc))
                pr.inc("ac", inst)
                banks.consumed(bk, "ac")
            mm_tile(io["w_qkv0"][oc],
                    lambda kc, tk: bufA[:, kc, tk * TH:(tk + 1) * TH], 2,
                    evacq, lambda tk, oc=oc: (oc * 2 + tk) % 4)
        if fused:
            pr.dma("gw", io["a_cc_in"], persist[:, 0:64])
            pr.wait(g, "gw")
            inst = g.collective_compute(
                "AllReduce", ALU.add, replica_groups=[list(range(NCORES))],
                ins=[io["a_cc_in"].tensor.ap().opt()],
                outs=[io["a_cc_out"].tensor.ap().opt()])
            pr.inc("cc", inst)
            pr.wait(g, "cc")
        pr.barrier()
        if mode == "a":
            for c in range(NCH):
                pr.dma("sq", io["o_x"][c], x_sb[:, c, :])
                pr.dma("sq", io["o_xn"][c], xn3[:, c, :])
                pr.dma("sq", io["o_q"][c], bufB[:, c, :])
            pr.wait(sy, "sq")

    if do("b"):
        if mode == "b":
            for c in range(NCH):
                pr.dma("sq", xn3[:, c, :], io["xn_in"][c])
                pr.dma("sq", bufB[:, c, :], io["q_in"][c])
            pr.dma("sq", scr[:, 0, :].rearrange("p (r c) -> p r c", r=NCORES),
                   io["a_all"].rearrange("r p c -> p r c"))
            pr.wait(v, "sq")
            av = scr[:, 0, :].rearrange("p (r c) -> p r c", r=NCORES)
            v.tensor_copy(persist[:, 0:64], av[:, 0, :])
            for r in range(1, NCORES):
                inst = v.tensor_tensor(persist[:, 0:64], persist[:, 0:64],
                                       av[:, r, :], ALU.add)
            pr.inc("ve", inst)
        if fused:
            pr.wait(sy, "cc")
            pr.dma("sq", persist[:, 0:64], io["a_cc_out"])
            pr.wait(v, "sq")
            if DEBUG_FUSED:
                pr.dma("sq", io["dbg"][:, 0:64], persist[:, 0:64])
        inst = v.tensor_scalar(persist[:, 0:64], persist[:, 0:64], 0.5, 0.5,
                               ALU.mult, ALU.add)
        pr.inc("ve", inst)
        for c in range(NCH):
            for b in range(NB):
                sl = bufB[:, c, b * SLOC:(b + 1) * SLOC]
                inst = v.tensor_scalar(sl, sl, persist[:, c * 4 + b:c * 4 + b + 1],
                                       None, ALU.mult)
        pr.inc("ve", inst)
        pr.barrier()

        for oc in range(NCH):
            def evaco(tk, bk, oc=oc):
                banks.consumer_wait(v, bk)
                xs = x_sb[:, oc, tk * TH:(tk + 1) * TH]
                inst = v.scalar_tensor_tensor(xs, pall(bk), bslot("b_out0", oc), xs,
                                              ALU.add, ALU.add)
                pr.inc("ve", inst)
                banks.consumed(bk, "ve")
            mm_tile(io["w_out0"][oc],
                    lambda kc, tk: bufB[:, kc, tk * TH:(tk + 1) * TH], 2,
                    evaco, lambda tk, oc=oc: (oc * 2 + tk) % 4)
        pr.barrier()

        stage_free = [0, 0]
        for c0 in range(NCH):
            for j in range(4):
                slot_i = c0 * 4 + j
                for gg in range(2):
                    k = ws.load(io["w_gu0"][slot_i, gg])
                    ws.use(k)
                    for tk in range(2):
                        bk = gg * 2 + tk
                        banks.pe_acquire(bk)
                        for kc in range(NCH):
                            inst = t.matmul(pall(bk), wbuf[:, k, kc, :],
                                            xn3[:, kc, tk * TH:(tk + 1) * TH],
                                            start=(kc == 0), stop=(kc == NCH - 1))
                        pr.inc("pe", inst)
                        banks.pe_filled(bk)
                    ws.done(k)
                for tk in range(2):
                    gbk, ubk = tk, 2 + tk
                    banks.consumer_wait(s, gbk)
                    pr.wait(s, "ve")
                    inst = s.activation(scr[:, 8 + tk, :], pall(gbk), AF.Silu)
                    pr.inc("ac", inst)
                    banks.consumed(gbk, "ac")
                    banks.consumer_wait(v, ubk)
                    pr.wait(v, "ac")
                    inst = v.tensor_tensor(scr[:, 2 * j + tk, :], scr[:, 8 + tk, :],
                                           pall(ubk), ALU.mult)
                    pr.inc("ve", inst)
                    banks.consumed(ubk, "ve")
            for tk in range(2):
                hs = [scr[:, 2 * j + tk, :] for j in range(4)]
                s0, s1, s2 = scr[:, 10, :], scr[:, 11, :], scr[:, 12, :]
                ab = [scr[:, 8, :], scr[:, 9, :], scr[:, 13, :], scr[:, 14, :]]
                pr.wait(s, "ve")
                for j in range(4):
                    inst = s.activation(ab[j], hs[j], AF.Abs)
                pr.inc("ac", inst)
                pr.wait(v, "ac")
                v.tensor_tensor(s0, ab[0], ab[1], ALU.max)
                v.tensor_tensor(s1, ab[2], ab[3], ALU.max)
                v.tensor_tensor(s2, s0, s1, ALU.min)
                v.tensor_tensor(s0, ab[0], ab[1], ALU.min)
                v.tensor_tensor(s1, ab[2], ab[3], ALU.min)
                v.tensor_tensor(s0, s0, s1, ALU.max)
                v.tensor_tensor(s2, s2, s0, ALU.max)
                for j in range(4):
                    st = (tk * 4 + j) % 2
                    v.tensor_tensor(ab[j], ab[j], s2, ALU.is_ge)
                    v.tensor_tensor(hs[j], hs[j], ab[j], ALU.mult)
                    pr.wait(v, "sq", stage_free[st])
                    inst = v.tensor_copy(stage[:, st, :], hs[j])
                    pr.inc("ve", inst)
                    pr.wait(sy, "ve")
                    pr.dma("sq", io["h_dram"][j * 16 + c0, :, tk * TH:(tk + 1) * TH],
                           stage[:, st, :])
                    stage_free[st] = pr.n["sq"]
        pr.barrier()

        for th in range(2):
            for grp in range(8):
                pr.dma("sq", hh_view[:, grp * 8:(grp + 1) * 8, :],
                       io["h_dram"].rearrange("k p t -> p k t")[
                           :, grp * 8:(grp + 1) * 8, th * TH:(th + 1) * TH])
            pr.wait(t, "sq")
            for oc in range(NCH):
                bk = oc % 4
                banks.pe_acquire(bk)
                first = True
                for qtr in range(4):
                    k = ws.load(io["w_down0"][oc, :, qtr * 16:(qtr + 1) * 16, :])
                    ws.use(k)
                    for kk in range(NCH):
                        inst = t.matmul(pall(bk), wbuf[:, k, kk, :],
                                        hh_view[:, qtr * 16 + kk, :],
                                        start=first, stop=(qtr == 3 and kk == NCH - 1))
                        first = False
                    pr.inc("pe", inst)
                    ws.done(k)
                banks.pe_filled(bk)
                banks.consumer_wait(v, bk)
                xs = x_sb[:, oc, th * TH:(th + 1) * TH]
                inst = v.tensor_tensor(xs, xs, pall(bk), ALU.add)
                pr.inc("ve", inst)
                banks.consumed(bk, "ve")
            pr.barrier()
        if fused and DEBUG_FUSED:
            for c in range(NCH):
                pr.dma("sq", io["o_x2"][c], x_sb[:, c, :])
            pr.wait(sy, "sq")
        if mode == "b":
            for c in range(NCH):
                pr.dma("sq", io["o_x"][c], x_sb[:, c, :])
            pr.wait(sy, "sq")

    if do("c"):
        ln_stats("x", rowA, rowB)
        ln_apply_bcasts(rowA, rowB)
        for c in range(NCH):
            v.tensor_tensor(tf2, x_sb[:, c, :], mean_bc, ALU.subtract)
            v.tensor_tensor(tf2, tf2, rstd_bc, ALU.mult)
            inst = v.tensor_copy(xn3[:, c, :], tf2)
        pr.inc("ve", inst)
        pr.barrier()
        if fused and DEBUG_FUSED:
            for c in range(NCH):
                g.dma_start(out=io["o_xn1"][c], in_=xn3[:, c, :]).then_inc(pr.h["gw"], 16)
                pr.n["gw"] += 16
            pr.wait(g, "gw")

        inst = v.tensor_scalar(bias_sb[:, BOFF["b_qk1"] + 16:BOFF["b_qk1"] + 32],
                               bias_sb[:, BOFF["b_qk1"] + 16:BOFF["b_qk1"] + 32],
                               RSCALE, None, ALU.mult)
        pr.inc("ve", inst)
        pr.wait(s, "ve")

        stage_free = [0, 0]
        sq_free = [0, 0]
        for oc in range(NCH):
            k = ws.load(io["w_qk1"][oc])
            ws.use(k)
            for tk in range(2):
                bk = tk
                banks.pe_acquire(bk)
                for kc in range(NCH):
                    inst = t.matmul(pall(bk), wbuf[:, k, kc, :],
                                    xn3[:, kc, tk * TH:(tk + 1) * TH],
                                    start=(kc == 0), stop=(kc == NCH - 1))
                pr.inc("pe", inst)
                banks.pe_filled(bk)
                banks.consumer_wait(s, bk)
                pr.wait(s, "sq", stage_free[tk])
                inst = s.activation(stage[:, tk, :], pall(bk), AF.Identity,
                                    bias=bslot("b_qk1", oc))
                pr.inc("ac", inst)
                banks.consumed(bk, "ac")
                pr.wait(sy, "ac")
                pr.dma("sq", io["k_contrib"][oc * P:(oc + 1) * P, tk * TH:(tk + 1) * TH],
                       stage[:, tk, :])
                stage_free[tk] = pr.n["sq"]
                pr.wait(s, "pe", sq_free[tk])
                inst = s.activation(sqb[:, tk, :], stage[:, tk, :], AF.Square)
                pr.inc("ac", inst)
                pr.wait(t, "ac")
                banks.pe_acquire(6 + tk)
                inst = t.matmul(pall(6 + tk)[0:1, :], onescol[:, :], sqb[:, tk, :],
                                start=True, stop=True)
                pr.inc("pe", inst)
                sq_free[tk] = pr.n["pe"]
                banks.pe_filled(6 + tk)
                banks.consumer_wait(v, 6 + tk)
                for b2 in range(2):
                    b = tk * 2 + b2
                    col = 64 + oc * 4 + b
                    inst = v.reduce_max(persist[0:1, col:col + 1],
                                        pall(6 + tk)[0:1, b2 * SLOC:(b2 + 1) * SLOC], AX.X)
                pr.inc("ve", inst)
                banks.consumed(6 + tk, "ve")
            ws.done(k)
        for oc in range(NCH, 32):
            k = ws.load(io["w_qk1"][oc])
            ws.use(k)
            for tk in range(2):
                bk = 2 + tk
                banks.pe_acquire(bk)
                for kc in range(NCH):
                    inst = t.matmul(pall(bk), wbuf[:, k, kc, :],
                                    xn3[:, kc, tk * TH:(tk + 1) * TH],
                                    start=(kc == 0), stop=(kc == NCH - 1))
                pr.inc("pe", inst)
                banks.pe_filled(bk)
                banks.consumer_wait(s, bk)
                inst = s.activation(bufB[:, oc - NCH, tk * TH:(tk + 1) * TH], pall(bk),
                                    AF.Identity, bias=bslot("b_qk1", oc), scale=RSCALE)
                pr.inc("ac", inst)
                banks.consumed(bk, "ac")
                pr.wait(s, "pe", sq_free[tk])
                inst = s.activation(sqb[:, tk, :], bufB[:, oc - NCH, tk * TH:(tk + 1) * TH],
                                    AF.Square)
                pr.inc("ac", inst)
                pr.wait(t, "ac")
                banks.pe_acquire(6 + tk)
                inst = t.matmul(pall(6 + tk)[0:1, :], onescol[:, :], sqb[:, tk, :],
                                start=True, stop=True)
                pr.inc("pe", inst)
                sq_free[tk] = pr.n["pe"]
                banks.pe_filled(6 + tk)
                banks.consumer_wait(v, 6 + tk)
                for b2 in range(2):
                    b = tk * 2 + b2
                    col = 128 + (oc - NCH) * 4 + b
                    inst = v.reduce_max(persist[0:1, col:col + 1],
                                        pall(6 + tk)[0:1, b2 * SLOC:(b2 + 1) * SLOC], AX.X)
                pr.inc("ve", inst)
                banks.consumed(6 + tk, "ve")
            ws.done(k)
        pr.barrier()

        vstage_free = [0, 0]
        for vs in range(4):
            k = wsv.load(io["w_v1"][:, :, vs * TH:(vs + 1) * TH])
            pr.wait(t, "gw", wsv.ready[k])
            for tm in range(8):
                bk = tm % 4
                banks.pe_acquire(bk)
                for kc in range(NCH):
                    inst = t.matmul(pall(bk), xn3[:, kc, tm * P:(tm + 1) * P],
                                    wsv.aps[k][:, kc, :], start=(kc == 0), stop=(kc == NCH - 1))
                pr.inc("pe", inst)
                banks.pe_filled(bk)
                banks.consumer_wait(v, bk)
                pr.wait(v, "sq", vstage_free[tm % 2])
                inst = v.tensor_copy(stage[:, tm % 2, :], pall(bk))
                pr.inc("ve", inst)
                banks.consumed(bk, "ve")
                pr.wait(sy, "ve")
                pr.dma("sq", io["v_contrib"][tm * P:(tm + 1) * P, vs * TH:(vs + 1) * TH],
                       stage[:, tm % 2, :])
                vstage_free[tm % 2] = pr.n["sq"]
            wsv.done(k)
        pr.barrier()
        if mode == "c":
            pr.dma("sq", io["o_mk2"], persist[0:1, 64:128])
            pr.dma("sq", io["o_mq2"], persist[0:1, 128:192])
            for c in range(NCH):
                pr.dma("sq", io["o_q"][c], bufB[:, c, :])
        h1_free = [0, 0]
        for ic in range(ICH):
            for gg in range(2):
                k = ws.load(io["w_gu1"][ic, gg])
                ws.use(k)
                for tk in range(2):
                    bk = gg * 2 + tk
                    banks.pe_acquire(bk)
                    for kc in range(NCH):
                        inst = t.matmul(pall(bk), wbuf[:, k, kc, :],
                                        xn3[:, kc, tk * TH:(tk + 1) * TH],
                                        start=(kc == 0), stop=(kc == NCH - 1))
                    pr.inc("pe", inst)
                    banks.pe_filled(bk)
                ws.done(k)
            for tk in range(2):
                gbk, ubk = tk, 2 + tk
                banks.consumer_wait(s, gbk)
                inst = s.activation(scr[:, 8 + tk, :], pall(gbk), AF.Silu)
                pr.inc("ac", inst)
                banks.consumed(gbk, "ac")
                banks.consumer_wait(v, ubk)
                pr.wait(v, "ac")
                pr.wait(v, "sq", h1_free[tk])
                inst = v.tensor_tensor(stage[:, tk, :], scr[:, 8 + tk, :],
                                       pall(ubk), ALU.mult)
                pr.inc("ve", inst)
                banks.consumed(ubk, "ve")
                pr.wait(sy, "ve")
                pr.dma("sq", io["h_dram"][ic, :, tk * TH:(tk + 1) * TH], stage[:, tk, :])
                h1_free[tk] = pr.n["sq"]
        pr.barrier()
        if fused:
            pr.wait(g, "sq")
            pr.dma("gw", io["mk2_cc_in"], persist[0:1, 64:128])
            pr.wait(g, "gw")
            for kind, op, iname, oname in (
                    ("AllGather", ALU.bypass, "k_contrib", "k_all"),
                    ("AllGather", ALU.bypass, "v_contrib", "v_all"),
                    ("AllReduce", ALU.max, "mk2_cc_in", "mk2_cc_out")):
                inst = g.collective_compute(
                    kind, op, replica_groups=[list(range(NCORES))],
                    ins=[io[iname].tensor.ap().opt()], outs=[io[oname].tensor.ap().opt()])
                pr.inc("cc", inst)
        if mode == "c":
            pr.wait(sy, "sq")

    if do("d"):
        flat = pool[:, 0:NCH * T]
        maskT = scr[:, 0:4, :].rearrange("p a t -> p (a t)").bitcast(BF16).rearrange(
            "p (c t) -> p c t", c=NCH)
        if mode == "d":
            for c in range(NCH):
                pr.dma("sq", bufB[:, c, :], io["q_in"][c])
            pr.dma("sq", scr[0:1, 4, :], io["mk2_allT"])
            pr.dma("sq", persist[0:1, 128:192], io["mq2_in"])
            pr.wait(v, "sq")
            inst = v.reduce_max(persist[0:1, 64:128],
                                scr[0:1, 4, :].rearrange("p (c r) -> p c r", r=NCORES),
                                AX.X)
            pr.inc("ve", inst)
        if fused:
            pr.wait(sy, "cc")
            if DEBUG_FUSED:
                pr.dma("sq", io["o_kall"], io["k_all"])
                pr.dma("sq", io["o_vall"], io["v_all"])
                pr.dma("sq", io["o_h1"], io["h_dram"])
            pr.dma("sq", persist[0:1, 64:128], io["mk2_cc_out"])
            pr.wait(v, "sq")
        if fused:
            pr.wait(g, "cc")
        pr.dma("gw", maskT.rearrange("p c t -> p (c t)"),
               io["masks"].rearrange("p c t -> p (c t)"))
        mg = persist[0:1, 192:256]
        v.tensor_tensor(mg, persist[0:1, 64:128], persist[0:1, 128:192], ALU.mult)
        inst = v.tensor_scalar(mg, mg, 1e-30, None, ALU.max)
        pr.inc("ve", inst)
        pr.wait(s, "ve")
        s.activation(mg, mg, AF.Ln)
        inst = s.activation(mg, mg, AF.Exp, scale=0.5)
        pr.inc("ac", inst)
        pr.wait(v, "ac")
        v.tensor_scalar(mg, mg, -1.0, None, ALU.mult)
        inst = v.tensor_copy(sqa[0:1, 0, 0:64], mg)
        pr.inc("ve", inst)
        pr.wait(t, "ve")
        banks.pe_acquire(6)
        inst = t.matmul(pall(6)[:, 0:64], ones1[:, :], sqa[0:1, 0, 0:64],
                        start=True, stop=True)
        pr.inc("pe", inst)
        banks.pe_filled(6)
        banks.consumer_wait(v, 6)
        negbc = persist[:, 0:64]
        inst = v.tensor_copy(negbc, pall(6)[:, 0:64])
        pr.inc("ve", inst)
        banks.consumed(6, "ve")
        if DEBUG_FUSED and fused:
            pr.wait(sy, "ve")
            pr.dma("sq", io["dbg"][:, 64:128], persist[:, 64:128])
            pr.dma("sq", io["dbg"][:, 128:192], persist[:, 128:192])
            pr.dma("sq", io["dbg"][:, 192:256], negbc)
        pr.wait(v, "gw")
        pr.barrier()

        kt = [flat[:, 0:4096].rearrange("p (c t) -> p c t", c=NCH),
              flat[:, 4096:8192].rearrange("p (c t) -> p c t", c=NCH)]
        vt = [flat[:, 8192:12288].rearrange("p (c t) -> p c t", c=NCH),
              flat[:, 12288:16384].rearrange("p (c t) -> p c t", c=NCH)]
        attT2 = [scr[:, 7:11, :].rearrange("p a t -> p (a t)").bitcast(BF16),
                 scr[:, 11:15, :].rearrange("p a t -> p (a t)").bitcast(BF16)]
        attTv = [a.rearrange("p (c t) -> p c t", c=NCH) for a in attT2]
        units = [(b, hg) for b in range(NB) for hg in range(8)]
        loaded = [(0, 0), (0, 0)]
        attdone = [0, 0]

        def prefetch(i):
            bb, hg = units[i]
            B = i % 2
            pr.wait(sy, "pe")
            pr.wait(s, "pe")
            for slot2 in range(8):
                for hh in range(2):
                    pr.dma("sq", kt[B][:, 2 * slot2:2 * slot2 + 2, hh * P:(hh + 1) * P],
                           io["k_all"][slot2 * D + hg * SLOC + hh * P:
                                       slot2 * D + hg * SLOC + (hh + 1) * P,
                                       bb * SLOC:(bb + 1) * SLOC].rearrange(
                               "p (kcc kk) -> p kcc kk", kcc=2))
                    pr.dma("sc", vt[B][:, 2 * slot2:2 * slot2 + 2, hh * P:(hh + 1) * P],
                           io["v_all"][slot2 * T + bb * SLOC:slot2 * T + (bb + 1) * SLOC,
                                       hg * SLOC + hh * P:hg * SLOC + (hh + 1) * P
                                       ].rearrange("(kcc p) dd -> p kcc dd", kcc=2))
            loaded[B] = (pr.n["sq"], pr.n["sc"])

        pend = None

        def finalize(pend):
            n, h, bb, avbank, ab, maskve = pend
            att = attTv[ab]
            banks.pe_acquire(5)
            pr.wait(t, "ve", maskve)
            for kc in range(NCH):
                inst = t.matmul(pall(5)[0:1, 0:SLOC], onescol[:, :], att[:, kc, :],
                                start=(kc == 0), stop=(kc == NCH - 1))
            pr.inc("pe", inst)
            banks.pe_filled(5)
            B = (n // 2) % 2
            hh = n % 2
            banks.pe_acquire(avbank)
            for kc in range(NCH):
                inst = t.matmul(pall(avbank)[:, 0:SLOC],
                                vt[B][:, kc, hh * P:(hh + 1) * P], att[:, kc, :],
                                start=(kc == 0), stop=(kc == NCH - 1))
            pr.inc("pe", inst)
            banks.pe_filled(avbank)
            attdone[ab] = pr.n["pe"]
            banks.consumer_wait(v, 5)
            sinv = scr[0:1, 6, 0:SLOC]
            inst = v.reciprocal(sinv, pall(5)[0:1, 0:SLOC])
            pr.inc("ve", inst)
            banks.consumed(5, "ve")
            inst = v.tensor_copy(sqa[0:1, 1, 0:SLOC], sinv)
            pr.inc("ve", inst)
            pr.wait(t, "ve")
            banks.pe_acquire(6)
            inst = t.matmul(pall(6)[:, 0:SLOC], ones1[:, :], sqa[0:1, 1, 0:SLOC],
                            start=True, stop=True)
            pr.inc("pe", inst)
            banks.pe_filled(6)
            banks.consumer_wait(v, 6)
            sbc = scr[:, 5, 0:SLOC]
            inst = v.tensor_copy(sbc, pall(6)[:, 0:SLOC])
            pr.inc("ve", inst)
            banks.consumed(6, "ve")
            banks.consumer_wait(v, avbank)
            tfy = scr[:, 4, 0:SLOC]
            v.tensor_tensor(tfy, pall(avbank)[:, 0:SLOC], sbc, ALU.mult)
            inst = v.tensor_scalar(bufA[:, h, bb * SLOC:(bb + 1) * SLOC], tfy,
                                   bslot("b_v1", h), None, ALU.add)
            pr.inc("ve", inst)
            banks.consumed(avbank, "ve")

        def emit_scores(i, hh):
            bb, hg = units[i]
            B = i % 2
            n = i * 2 + hh
            ab = n % 2
            h = hg * 2 + hh
            idx = h * 4 + bb
            q = bufB[:, h, bb * SLOC:(bb + 1) * SLOC]
            pr.wait(t, "sq", loaded[B][0])
            pr.wait(t, "sc", loaded[B][1])
            for grp in range(4):
                b0 = (grp % 2) * 2
                banks.pe_acquire(b0)
                banks.pe_acquire(b0 + 1)
                for ii in range(4):
                    kc = grp * 4 + ii
                    inst = t.matmul(
                        pall_t[:, b0 + ii // 2, (ii % 2) * SLOC:(ii % 2 + 1) * SLOC],
                        kt[B][:, kc, hh * P:(hh + 1) * P], q, start=True, stop=True)
                pr.inc("pe", inst)
                banks.pe_filled(b0)
                banks.pe_filled(b0 + 1)
                banks.consumer_wait(s, b0)
                pr.wait(s, "pe", attdone[ab])
                inst = s.activation(
                    attTv[ab][:, grp * 4:(grp + 1) * 4, :].rearrange("p c t -> p (c t)"),
                    pall_t[:, b0:b0 + 2, :].rearrange("p c t -> p (c t)"),
                    AF.Exp, bias=negbc[:, idx:idx + 1])
                pr.inc("ac", inst)
                banks.consumed(b0, "ac")
                banks.consumed(b0 + 1, "ac")
                pr.wait(v, "ac")
                av_ = attTv[ab][:, grp * 4:(grp + 1) * 4, :].rearrange("p c t -> p (c t)")
                mv_ = maskT[:, grp * 4:(grp + 1) * 4, :].rearrange("p c t -> p (c t)")
                inst = v.tensor_tensor(av_, av_, mv_, ALU.mult)
                pr.inc("ve", inst)
            return (n, h, bb, 4 if n % 2 == 0 else 7, ab, pr.n["ve"])

        prefetch(0)
        for i in range(len(units)):
            pend_new = emit_scores(i, 0)
            if pend is not None:
                finalize(pend)
            pend = pend_new
            if i + 1 < len(units):
                prefetch(i + 1)
            pend_new = emit_scores(i, 1)
            finalize(pend)
            pend = pend_new
        finalize(pend)
        pr.barrier()
        if DEBUG_FUSED and fused:
            g.dma_start(out=io["dbg"][:, 256:512], in_=bufA[:, 0, 0:256]).then_inc(pr.h["gw"], 16)
            pr.n["gw"] += 16
            for c in range(NCH):
                g.dma_start(out=io["o_y1"][c], in_=bufA[:, c, :]).then_inc(pr.h["gw"], 16)
                pr.n["gw"] += 16
            pr.wait(g, "gw")

        for oc in range(NCH):
            def evacp(tk, bk, oc=oc):
                banks.consumer_wait(v, bk)
                xs = x_sb[:, oc, tk * TH:(tk + 1) * TH]
                inst = v.scalar_tensor_tensor(xs, pall(bk), bslot("b_proj1", oc), xs,
                                              ALU.add, ALU.add)
                pr.inc("ve", inst)
                banks.consumed(bk, "ve")
            mm_tile(io["w_proj1"][oc],
                    lambda kc, tk: bufA[:, kc, tk * TH:(tk + 1) * TH], 2,
                    evacp, lambda tk, oc=oc: (oc * 2 + tk) % 4)
        pr.barrier()

        wstage = [scr[:, 0:4, :].rearrange("p a t -> p (a t)"),
                  scr[:, 4:8, :].rearrange("p a t -> p (a t)")]
        wflat = [wbuf[:, 0, :, :].rearrange("p c m -> p (c m)")[:, 0:2048],
                 wbuf[:, 1, :, :].rearrange("p c m -> p (c m)")[:, 0:2048]]
        st_cast = [0, 0]
        st_ready = [0, 0]
        bf_ready = [0, 0]
        bf_freed = [0, 0]

        def vc_load(i, src_ap):
            k = i % 2
            if st_cast[k]:
                pr.wait(sy, "ve", st_cast[k])
            pr.dma("sq", wstage[k].rearrange("p (c m) -> p c m", c=NCH), src_ap)
            st_ready[k] = pr.n["sq"]
            return k

        def vc_cast(k):
            pr.wait(v, "sq", st_ready[k])
            if bf_freed[k]:
                pr.wait(v, "pe", bf_freed[k])
            inst = v.tensor_copy(wflat[k], wstage[k])
            pr.inc("ve", inst)
            st_cast[k] = pr.n["ve"]
            bf_ready[k] = pr.n["ve"]

        vci = 0
        for th in range(2):
            for grp in range(8):
                pr.dma("sq", hh_view[:, grp * 8:(grp + 1) * 8, :],
                       io["h_dram"].rearrange("k p t -> p k t")[
                           :, grp * 8:(grp + 1) * 8, th * TH:(th + 1) * TH])
            pr.wait(t, "sq")
            for oc in range(NCH):
                bk = oc % 4
                banks.pe_acquire(bk)
                first = True
                for qtr in range(4):
                    k = vc_load(vci, io["w_down1"][oc, :, qtr * 16:(qtr + 1) * 16, :])
                    vci += 1
                    vc_cast(k)
                    pr.wait(t, "ve", bf_ready[k])
                    for kk in range(NCH):
                        inst = t.matmul(pall(bk), wbuf[:, k, kk, :],
                                        hh_view[:, qtr * 16 + kk, :],
                                        start=first, stop=(qtr == 3 and kk == NCH - 1))
                        first = False
                    pr.inc("pe", inst)
                    bf_freed[k] = pr.n["pe"]
                banks.pe_filled(bk)
                banks.consumer_wait(v, bk)
                xs = x_sb[:, oc, th * TH:(th + 1) * TH]
                inst = v.tensor_tensor(xs, xs, pall(bk), ALU.add)
                pr.inc("ve", inst)
                banks.consumed(bk, "ve")
            pr.barrier()
        for c in range(NCH):
            pr.dma("sq", io["o_x"][c], x_sb[:, c, :])
        pr.wait(sy, "sq")

    nc.all_engine_barrier()
    stack.close()
    return nc



def kernel(**inputs):
    X = np.asarray(inputs["L0_attn_X"])
    W = prep_weights(inputs)
    cores = [prep_core(c, X) for c in range(NCORES)]
    core_ids = list(range(NCORES))

    if MODE == "fused":
        nc = build("fused")
        in_maps = []
        for c in range(NCORES):
            m = {"x_in": cores[c]["x_in"], "cos0": cores[c]["cos0"],
                 "sin0": cores[c]["sin0"], "masks": cores[c]["masks"]}
            for k in ("w_qkv0", "b_qkv0", "gb0", "w_out0", "b_out0", "w_gu0", "w_down0",
                      "w_qk1", "b_qk1", "w_v1", "b_v1", "w_proj1", "b_proj1",
                      "w_gu1", "w_down1"):
                m[k] = W[k]
            in_maps.append(m)
        res = run_bass_kernel_spmd(nc, in_maps, core_ids=core_ids)
        outs = [r["o_x"] for r in res.results]
    else:
        nc = build("a")
        in_maps = [{"x_in": cores[c]["x_in"], "w_qkv0": W["w_qkv0"],
                    "b_qkv0": W["b_qkv0"], "gb0": W["gb0"],
                    "cos0": cores[c]["cos0"], "sin0": cores[c]["sin0"]}
                   for c in range(NCORES)]
        ra = run_bass_kernel_spmd(nc, in_maps, core_ids=core_ids).results
        a_all = np.ascontiguousarray(np.stack([ra[c]["o_a"] for c in range(NCORES)]))

        nc = build("b")
        in_maps = [{"x_in": ra[c]["o_x"], "xn_in": np.asarray(ra[c]["o_xn"], BF),
                    "q_in": np.asarray(ra[c]["o_q"], BF), "a_all": a_all,
                    "w_out0": W["w_out0"], "b_out0": W["b_out0"],
                    "w_gu0": W["w_gu0"], "w_down0": W["w_down0"]}
                   for c in range(NCORES)]
        rb = run_bass_kernel_spmd(nc, in_maps, core_ids=core_ids).results

        nc = build("c")
        in_maps = [{"x_in": rb[c]["o_x"], "w_qk1": W["w_qk1"], "b_qk1": W["b_qk1"],
                    "w_v1": W["w_v1"], "w_gu1": W["w_gu1"]}
                   for c in range(NCORES)]
        rc = run_bass_kernel_spmd(nc, in_maps, core_ids=core_ids).results
        k_all = np.ascontiguousarray(
            np.concatenate([np.asarray(rc[c]["k_contrib"], BF) for c in range(NCORES)], 0))
        v_all = np.ascontiguousarray(
            np.concatenate([np.asarray(rc[c]["v_contrib"], BF) for c in range(NCORES)], 0))
        mk2_allT = np.ascontiguousarray(
            np.stack([rc[c]["o_mk2"][0] for c in range(NCORES)]).T)

        nc = build("d")
        in_maps = [{"x_in": rb[c]["o_x"], "q_in": np.asarray(rc[c]["o_q"], BF),
                    "k_all": k_all, "v_all": v_all, "mk2_allT": mk2_allT,
                    "mq2_in": rc[c]["o_mq2"], "h_dram": np.asarray(rc[c]["h_dram"], BF),
                    "masks": cores[c]["masks"], "b_v1": W["b_v1"],
                    "w_proj1": W["w_proj1"], "b_proj1": W["b_proj1"],
                    "w_down1": W["w_down1"]}
                   for c in range(NCORES)]
        rd = run_bass_kernel_spmd(nc, in_maps, core_ids=core_ids).results
        outs = [rd[c]["o_x"] for c in range(NCORES)]

    full = np.empty((NB, SEQ, D), np.float32)
    for c in range(NCORES):
        o = np.asarray(outs[c], np.float32).reshape(NCH, P, NB, SLOC)
        full[:, SLOC * c:SLOC * (c + 1), :] = o.transpose(2, 3, 0, 1).reshape(NB, SLOC, D)
    return full
